# version 15
# baseline (speedup 1.0000x reference)
import os
import numpy as np
import ml_dtypes
import concourse.bass as bass
import concourse.mybir as mybir
from concourse.bass_utils import run_bass_kernel_spmd

F32 = mybir.dt.float32
BF16 = mybir.dt.bfloat16
I32 = mybir.dt.int32
AF = mybir.ActivationFunctionType
ALU = mybir.AluOpType
AX = mybir.AxisListType

T = 2048
NSEQ = 2
D = 1024
II = 2816
NIC = II // 128
NTT = T // 128
R = NSEQ * T
NRT = R // 128
QB = 512
NQB = T // QB
INV127 = float(np.float32(1.0) / np.float32(127.0))
INV1024 = float(np.float32(1.0) / np.float32(1024.0))


class Sem:
    def __init__(self, nc, name):
        self.h = nc.alloc_semaphore(name)
        self.n = 0

    def d(self, instr):
        instr.then_inc(self.h, 16)
        self.n += 16
        return self.n

    def m(self, instr):
        instr.then_inc(self.h, 1)
        self.n += 1
        return self.n


class StopBuild(Exception):
    pass


def build_program(stage="full"):
    nc = bass.Bass("TRN2", target_bir_lowering=False,
                   detect_race_conditions=False)
    SY, TEE, VE, SC, GP = nc.sync, nc.tensor, nc.vector, nc.scalar, nc.gpsimd

    x_in = nc.dram_tensor("x", [NSEQ, T, D], F32, kind="ExternalInput").ap()
    nw_in = nc.dram_tensor("norm_w", [1, D], F32, kind="ExternalInput").ap()
    wT_in = {w: nc.dram_tensor(f"w{w}T", [D, D], F32, kind="ExternalInput").ap()
             for w in "qkvo"}
    b_in = {w: nc.dram_tensor(f"b{w}", [1, D], F32, kind="ExternalInput").ap()
            for w in "qkvo"}
    clip_in = nc.dram_tensor("clips", [1, 2], F32, kind="ExternalInput").ap()
    wg_in = nc.dram_tensor("w_gate", [II, D], F32, kind="ExternalInput").ap()
    wu_in = nc.dram_tensor("w_up", [II, D], F32, kind="ExternalInput").ap()
    wd_in = nc.dram_tensor("w_down", [D, II], F32, kind="ExternalInput").ap()
    out = nc.dram_tensor("out", [NSEQ, T, D], F32, kind="ExternalOutput").ap()
    if stage in ("ln1", "ffnv", "full2"):
        dbgv = nc.dram_tensor("dbgv", [II, R], F32, kind="ExternalOutput").ap()
    if stage == "full2":
        dbgX2 = nc.dram_tensor("dbgX2", [R, D], F32, kind="ExternalOutput").ap()
    dbg_n = int(stage[4:]) if isinstance(stage, str) and stage.startswith("dbg:") else None
    if dbg_n is not None:
        dbgA = nc.dram_tensor("dbgA", [128, 16384], F32, kind="ExternalOutput").ap()
        dbgB = nc.dram_tensor("dbgB", [128, 16384], F32, kind="ExternalOutput").ap()
        dbgS = nc.dram_tensor("dbgS", [128, 14336], F32, kind="ExternalOutput").ap()
        dbgX = nc.dram_tensor("dbgX", [R, D], F32, kind="ExternalOutput").ap()
        dbgSd = nc.dram_tensor("dbgSd", [16, T], F32, kind="ExternalOutput").ap()
        dbgM = nc.dram_tensor("dbgM", [128, 320], F32, kind="ExternalOutput").ap()
        dbgW = nc.dram_tensor("dbgW", [128, 48], F32, kind="ExternalOutput").ap()

    x1dram = nc.dram_tensor("x1dram", [R, D], F32).ap()
    vdram = nc.dram_tensor("vdram", [II, R], F32).ap()
    sdram = nc.dram_tensor("sdram", [16, T], F32).ap()
    m128d = nc.dram_tensor("m128d", [1, 128], F32).ap()
    rsdd = nc.dram_tensor("rsdd", [1, D], F32).ap()
    cc_in = [nc.dram_tensor(f"cc_in{i}", [1, 64], F32) for i in range(2)]
    cc_out = [nc.dram_tensor(f"cc_out{i}", [1, 64], F32) for i in range(2)]

    A = nc.alloc_sbuf_tensor
    ARA = A("ARA", [128, 16384], F32)
    ARB = A("ARB", [128, 16384], F32)
    ARS = A("ARS", [128, 14336], F32)
    ARA_bf = ARA.bitcast(BF16)
    ARA_i = ARA.bitcast(I32)
    ARB_bf = ARB.bitcast(BF16)
    ARS_bf = ARS.bitcast(BF16)
    ARS_i = ARS.bitcast(I32)

    xnT = ARA
    attT = ARB
    qT = ARS[:, 0:2048]
    kT = ARS[:, 2048:4096]
    qs2 = [ARS_bf[:, 0:2048], ARS_bf[:, 2048:4096]]
    ks2 = [ARS_bf[:, 4096:6144], ARS_bf[:, 6144:8192]]
    dh_s = ARS_bf[:, 19520:20032]
    dl_s = ARS_bf[:, 20032:20544]
    wstage = ARS[:, 11296:12320]
    a1scr = ARS[:, 2048:3072]
    xnT_bf = ARA_bf
    vhp = ARS[:, 4096:6176]
    wsq = ARS[:, 6176:7200]
    wsk = ARS[:, 7200:8224]
    wsv = ARS[:, 8224:9248]
    srowb = ARS[:, 9248:9760]
    ph3 = [ARS_bf[:, 19520:20032], ARS_bf[:, 20544:21056],
           ARS_bf[:, 21568:22080]]
    pl3 = [ARS_bf[:, 20032:20544], ARS_bf[:, 21056:21568],
           ARS_bf[:, 22080:22592]]
    vhpb = ARS_bf[:, 8192:12352]
    sall = ARS[0:16, 10784:12832]
    srec = ARS[0:16, 0:2048]
    wo_full = ARA[:, 0:8192]

    xqT_bf = ARA_bf
    wdT_bf = ARB_bf
    wstg = ARB[:, 11264:14080]
    vtl = [ARB[:, 14080 + b * 512:14592 + b * 512] for b in range(2)]
    wq_u = ARS[:, 0:2816]
    wq_t1 = ARS[:, 2816:5632]
    wq_t1i = ARS_i[:, 2816:5632]
    wq_t2 = ARS[:, 5632:8448]
    wq_rnd = ARS[:, 8448:11264]
    wgT = [ARS_bf[:, 22528 + b * 1024:23552 + b * 1024] for b in range(2)]
    wuT = [ARS_bf[:, 24576 + b * 1024:25600 + b * 1024] for b in range(2)]
    atl = [ARS[:, 13312 + b * 512:13824 + b * 512] for b in range(2)]

    yt = ARS[:, 2048:3072]
    yb = ARS[:, 3072:4096]
    xnst = ARS[:, 4096:6144]
    maskb = ARS_bf[:, 25664:27712]
    mask_i = ARS[:, 0:512]
    iden_i = ARS[:, 512:640]
    nwrow = ARS[0:1, 6176:7200]
    rowt = ARS[0:1, 13856:13984]
    oh = ARS[0:16, 13984:14112]
    ohJ = ARS[0:16, 14112:14240]

    vstg = ARA[:, 0:2816]
    hq_u = ARA[:, 2816:5632]
    hq_t1 = ARA[:, 5632:8448]
    hq_t1i = ARA_i[:, 5632:8448]
    hq_t2 = ARA[:, 8448:11264]
    htile = ARA_bf[:, 22528:25344]
    sdb = ARA[:, 12672:13696]

    xst = A("xst", [128, 2 * D], F32)
    nwb = A("nwb", [128, D], F32)
    brows = A("brows", [128, D], F32)
    identf = A("identf", [128, 128], F32)
    identb = A("identb", [128, 128], BF16)
    onesP = A("onesP", [128, 512], F32)
    r_all = A("r_all", [128, NRT], F32)
    m1_all = A("m1_all", [128, NRT], F32)
    vm_all = A("vm_all", [128, 320], F32)
    stat = A("stat", [128, 20], F32)
    consts = A("consts", [128, 4], F32)
    clipt = A("clipt", [1, 2], F32)
    sig2 = A("sig2", [1, 2], F32)
    gsc = A("gsc", [1, 16], F32)
    ccst = A("ccst", [1, 64], F32)
    scbc = A("scbc", [128, 8], F32)
    swg = A("swg", [128, 4], F32)
    swu = A("swu", [128, 4], F32)
    sgu = A("sgu", [128, 4], F32)
    sdall = A("sdall", [128, 8], F32)
    bqkT = A("bqkT", [128, 24], F32)
    bob = ARS[:, 10272:11296]

    P5 = [nc.alloc_psum_tensor(f"ps{i}", [128, 512], F32) for i in range(4)]
    PAB = [nc.alloc_psum_tensor("psA", [128, 1024], F32),
           nc.alloc_psum_tensor("psB", [128, 1024], F32)]

    sLd = Sem(nc, "sLd")
    sSp = Sem(nc, "sSp")
    sTr = Sem(nc, "sTr")
    sEv = Sem(nc, "sEv")
    sMM = Sem(nc, "sMM")
    sPE = Sem(nc, "sPE")
    sXn = Sem(nc, "sXn")
    sSq = Sem(nc, "sSq")
    sR = Sem(nc, "sR")
    sP = Sem(nc, "sP")
    sMs = Sem(nc, "sMs")
    sGp = Sem(nc, "sGp")
    sOut = Sem(nc, "sOut")
    sLdV = Sem(nc, "sLdV")
    sLdV2 = Sem(nc, "sLdV2")
    sLdW = Sem(nc, "sLdW")
    ALLSEMS = [sLd, sSp, sTr, sEv, sMM, sPE, sXn, sSq, sR, sP, sMs, sOut,
               sLdV, sLdV2, sLdW]

    gates = {}

    def gwait(eng, key):
        if key in gates:
            h, v = gates[key]
            eng.wait_ge(h, v)

    def gset(key, sem, val=None):
        gates[key] = (sem.h, sem.n if val is None else val)

    ck = [0]

    def ckpt():
        ck[0] += 1
        if dbg_n is not None and ck[0] == dbg_n:
            raise StopBuild

    def barrier():
        for s_ in ALLSEMS:
            GP.wait_ge(s_.h, s_.n)
        sGp.m(GP.tensor_copy(stat[0:1, 14:15], stat[0:1, 15:16]))
        for eng in (SY, TEE, VE, SC):
            eng.wait_ge(sGp.h, sGp.n)

    with nc.Block() as block:
      try:
        sLd.d(SY.dma_start(out=nwrow[:, :], in_=nw_in))
        for j, w in enumerate("qkv"):
            sLd.d(SY.dma_start(
                out=bqkT[:, 8 * j:8 * j + 8],
                in_=b_in[w].rearrange("a (c p) -> p (a c)", p=128),
                allow_slow_non_contiguous=True))
        sLd.d(SY.dma_start(out=clipt[:, :], in_=clip_in))
        ld0 = sLd.n

        GP.iota(iden_i[:, :], [[1, 128]], base=0, channel_multiplier=-1,
                allow_small_or_imprecise_dtypes=True)
        GP.iota(mask_i[:, :], [[1, QB]], base=0, channel_multiplier=-1,
                allow_small_or_imprecise_dtypes=True)
        GP.iota(ohJ[:, :], [[-1, 2], [0, 64]], base=0, channel_multiplier=1,
                allow_small_or_imprecise_dtypes=True)
        GP.memset(onesP[:, :], 1.0)
        GP.memset(consts[:, 0:1], -30.0)
        GP.memset(consts[:, 1:2], 0.5)
        GP.memset(srowb[:, :], 0.0)
        vv = vhpb.rearrange("p (t c) -> p t c", c=260)
        GP.memset(vv[:, :, 64:65], 1.0)
        GP.memset(vv[:, :, 129:130], 1.0)
        GP.memset(vv[:, :, 194:195], 0.0)
        GP.memset(vv[:, :, 259:260], 0.0)
        sGp.m(GP.memset(ccst[:, :], 0.0))
        gp0 = sGp.n

        VE.wait_ge(sGp.h, gp0)
        VE.wait_ge(sLd.h, ld0)
        VE.tensor_scalar(out=identf[:, :], in0=iden_i[:, :], scalar1=0.0,
                         scalar2=None, op0=ALU.is_equal)
        VE.tensor_scalar(out=identb[:, :], in0=iden_i[:, :], scalar1=0.0,
                         scalar2=None, op0=ALU.is_equal)
        for j, dd in enumerate((0.0, 128.0, 256.0, 384.0)):
            VE.tensor_scalar(out=maskb[:, j * QB:(j + 1) * QB], in0=mask_i[:, :],
                             scalar1=dd, scalar2=-1e9, op0=ALU.is_lt, op1=ALU.mult)
        init_ve = sMs.m(VE.tensor_copy(stat[:, 15:16], consts[:, 0:1]))

        SC.wait_ge(sLd.h, ld0)
        a_e = sSq.m(SC.activation(out=gsc[0:1, 6:8], in_=clipt[:, :], func=AF.Exp,
                                  scale=-1.0))
        VE.wait_ge(sSq.h, a_e)
        v_e = sMs.m(VE.tensor_scalar(out=gsc[0:1, 9:11], in0=gsc[0:1, 6:8],
                                     scalar1=1.0, scalar2=None, op0=ALU.add))
        SC.wait_ge(sMs.h, v_e)
        a_e2 = sSq.m(SC.activation(out=gsc[0:1, 1:3], in_=gsc[0:1, 9:11],
                                   func=AF.Copy))
        VE.wait_ge(sSq.h, a_e2)
        sSq.m(VE.reciprocal(sig2[:, :], gsc[0:1, 1:3])) if False else None
        sMs.m(VE.reciprocal(sig2[0:1, :], gsc[0:1, 1:3]))

        TEE.wait_ge(sMs.h, init_ve)
        TEE.matmul(PAB[0][:, 0:512], onesP[0:1, 0:128], nwrow[0:1, 0:512],
                   start=True, stop=True)
        sMM.m(TEE.matmul(PAB[0][:, 512:1024], onesP[0:1, 0:128], nwrow[0:1, 512:1024],
                         start=True, stop=True))
        VE.wait_ge(sMM.h, sMM.n)
        sPE.m(VE.tensor_copy(nwb[:, :], PAB[0][:, :]))
        nwb_ready = sPE.n
        gset("PAB0", sPE)

        sall_ld_prev = [None]

        for s in range(NSEQ):
            VE.wait_ge(sMM.h, sMM.n)
            SC.wait_ge(sMs.h, sMs.n)

            bXn, bTr, bEv, bLd, bSq, bMs, bR = (
                sXn.n, sTr.n, sEv.n, sLd.n, sSq.n, sMs.n, sR.n)
            for tt in range(NTT):
                if tt >= 2:
                    SY.wait_ge(sXn.h, bXn + tt - 1)
                gwait(SY, f"xst:{tt % 2}")
                sLd.d(SY.dma_start(
                    out=xst[:, (tt % 2) * D:(tt % 2 + 1) * D],
                    in_=x_in[s, tt * 128:(tt + 1) * 128, :]))
                SC.wait_ge(sLd.h, bLd + 16 * (tt + 1))
                if tt >= 2:
                    SC.wait_ge(sTr.h, bTr + 8 * (tt - 1))
                SC.activation(
                    out=xnst[:, (tt % 2) * D:(tt % 2 + 1) * D],
                    in_=xst[:, (tt % 2) * D:(tt % 2 + 1) * D],
                    func=AF.Square, accum_out=stat[:, 0:1])
                sSq.m(SC.activation(out=stat[:, 16:17], in_=stat[:, 0:1],
                                    func=AF.Copy))
                VE.wait_ge(sSq.h, bSq + tt + 1)
                sMs.m(VE.tensor_scalar(out=stat[:, 1:2], in0=stat[:, 0:1],
                                       scalar1=INV1024, scalar2=1e-6,
                                       op0=ALU.mult, op1=ALU.add))
                SC.wait_ge(sMs.h, bMs + tt + 1)
                SC.activation(out=stat[:, 2:3], in_=stat[:, 1:2], func=AF.Ln)
                sR.m(SC.activation(out=r_all[:, s * NTT + tt: s * NTT + tt + 1],
                                   in_=stat[:, 2:3], func=AF.Exp, scale=-0.5))
                idx = s * NTT + tt
                VE.wait_ge(sPE.h, nwb_ready)
                VE.wait_ge(sR.h, bR + tt + 1)
                if tt >= 2:
                    VE.wait_ge(sTr.h, bTr + 8 * (tt - 1))
                sXn.m(VE.scalar_tensor_tensor(
                    out=xnst[:, (tt % 2) * D:(tt % 2 + 1) * D],
                    in0=xst[:, (tt % 2) * D:(tt % 2 + 1) * D],
                    scalar=r_all[:, idx:idx + 1], in1=nwb[:, :],
                    op0=ALU.mult, op1=ALU.mult))
                gset(f"xst:{tt % 2}", sXn)
                for dc in range(8):
                    j = tt * 8 + dc
                    TEE.wait_ge(sXn.h, bXn + tt + 1)
                    gwait(TEE, f"P5:{j % 2}")
                    mtr = sTr.m(TEE.transpose(
                        P5[j % 2][:, 0:128],
                        xnst[:, (tt % 2) * D + dc * 128:(tt % 2) * D + (dc + 1) * 128],
                        identf[:, :]))
                    VE.wait_ge(sTr.h, mtr)
                    sEv.m(VE.tensor_copy(a1scr[:, dc * 128:(dc + 1) * 128],
                                         P5[j % 2][:, 0:128]))
                    gset(f"P5:{j % 2}", sEv)
                hi3 = xnT_bf.rearrange("p (c z t) -> p c z t", c=8, z=2)[
                    :, :, 0, tt * 128:(tt + 1) * 128]
                lo3 = xnT_bf.rearrange("p (c z t) -> p c z t", c=8, z=2)[
                    :, :, 1, tt * 128:(tt + 1) * 128]
                VE.tensor_copy(hi3, a1scr.rearrange("p (c t) -> p c t", c=8))
                sEv.m(VE.tensor_tensor(
                    out=lo3, in0=a1scr.rearrange("p (c t) -> p c t", c=8),
                    in1=hi3, op=ALU.subtract))
            xnT_ready = sEv.n
            ckpt()
            GP.wait_ge(sTr.h, sTr.n)
            vv2 = vhpb.rearrange("p (t c) -> p t c", c=260)
            GP.memset(vv2[:, :, 64:65], 1.0)
            GP.memset(vv2[:, :, 129:130], 1.0)
            GP.memset(vv2[:, :, 194:195], 0.0)
            sGp.m(GP.memset(vv2[:, :, 259:260], 0.0))
            ones_ready = sGp.n

            for hp in range(8):
                wsplit_done = []
                for wname, dst in (("q", wsq), ("k", wsk), ("v", wsv)):
                    gwait(SY, "wstage")
                    gwait(SY, f"wsl:{wname}")
                    srcw = wT_in[wname][:, hp * 128:(hp + 1) * 128]
                    wld = sLd.d(SY.dma_start(
                        out=wstage.rearrange("p (c o) -> p c o", c=8),
                        in_=srcw.rearrange("(c p) o -> p c o", p=128)))
                    dbf = dst.bitcast(BF16)
                    VE.wait_ge(sLd.h, wld)
                    VE.tensor_copy(dbf[:, 0:1024], wstage[:, :])
                    wm = sMs.m(VE.tensor_tensor(out=dbf[:, 1024:2048],
                                                in0=wstage[:, :],
                                                in1=dbf[:, 0:1024],
                                                op=ALU.subtract))
                    gset("wstage", sMs)
                    wsplit_done.append(wm)
                w_ld = sLd.n

                TEE.wait_ge(sLd.h, w_ld)
                TEE.wait_ge(sMs.h, wsplit_done[2])
                TEE.wait_ge(sEv.h, xnT_ready)

                def xnslice(part, dc, c0, w):
                    off = dc * 4096 + (0 if part == 0 else 2048)
                    return xnT_bf[:, off + c0: off + c0 + w]

                for wi, wap in ((0, wsq), (1, wsk)):
                    wbf = wap.bitcast(BF16)
                    w3h = wbf[:, 0:1024].rearrange("p (c o) -> p c o", c=8)
                    w3l = wbf[:, 1024:2048].rearrange("p (c o) -> p c o", c=8)
                    bcol = bqkT[:, 8 * wi + hp: 8 * wi + hp + 1]
                    for tb in range(4):
                        gwait(TEE, f"P5:{tb % 2}")
                        for wpart, xpart in ((0, 0), (0, 1), (1, 0)):
                            w3 = w3h if wpart == 0 else w3l
                            for dc in range(8):
                                mi = TEE.matmul(
                                    P5[tb % 2][:, :], w3[:, dc, :],
                                    xnslice(xpart, dc, tb * QB, QB),
                                    start=(wpart == 0 and xpart == 0 and dc == 0),
                                    stop=(wpart == 1 and dc == 7))
                        qkm = sMM.m(mi)
                        VE.wait_ge(sMM.h, qkm)
                        VE.tensor_scalar(out=dh_s, in0=P5[tb % 2][:, :],
                                         scalar1=bcol, scalar2=None, op0=ALU.add)
                        pm_ = sPE.m(VE.scalar_tensor_tensor(
                            out=dl_s, in0=P5[tb % 2][:, :], scalar=bcol,
                            in1=dh_s, op0=ALU.add, op1=ALU.subtract))
                        gset(f"P5:{tb % 2}", sPE)
                        qsl2 = slice(tb * QB, (tb + 1) * QB)
                        ds2 = ks2 if wi == 1 else qs2
                        pieces = [(ds2[0], 0, dh_s, 0), (ds2[0], 64, dl_s, 0),
                                  (ds2[1], 0, dh_s, 64), (ds2[1], 64, dl_s, 64)]
                        for dst_, dp, src_, sp in pieces:
                            mcp = VE.tensor_copy(dst_[dp:dp + 64, qsl2],
                                                 src_[sp:sp + 64, :])
                        sPE.m(mcp)
                gset("wsl:q", sMM)
                gset("wsl:k", sMM)
                wvbf = wsv.bitcast(BF16)
                wv3h = wvbf[:, 0:1024].rearrange("p (c o) -> p c o", c=8)
                wv3l = wvbf[:, 1024:2048].rearrange("p (c o) -> p c o", c=8)
                vscr = ARS[:, 12320:12832]
                for tb in range(4):
                    pb = 2 + tb % 2
                    gwait(TEE, f"P5:{pb}")
                    for wpart, xpart in ((0, 0), (0, 1), (1, 0)):
                        wv3 = wv3h if wpart == 0 else wv3l
                        for dc in range(8):
                            mi = TEE.matmul(
                                P5[pb][:, :], wv3[:, dc, :],
                                xnslice(xpart, dc, tb * QB, QB),
                                start=(wpart == 0 and xpart == 0 and dc == 0),
                                stop=(wpart == 1 and dc == 7))
                    vtm = sMM.m(mi)
                    VE.wait_ge(sMM.h, vtm)
                    scm_ = sPE.m(VE.tensor_copy(vscr, P5[pb][:, :]))
                    gset(f"P5:{pb}", sPE)
                    for k4 in range(4):
                        tt = tb * 4 + k4
                        pb2 = k4 % 2
                        gwait(TEE, f"P5:{pb2}")
                        TEE.wait_ge(sPE.h, scm_)
                        ttm = sTr.m(TEE.transpose(
                            P5[pb2][:, 0:128], vscr[:, k4 * 128:(k4 + 1) * 128],
                            identf[:, :]))
                        vb_ = vhpb[:, tt * 260:(tt + 1) * 260]
                        SC.wait_ge(sTr.h, ttm)
                        SC.activation(out=vb_[:, 0:64], in_=P5[pb2][:, 0:64],
                                      func=AF.Copy)
                        hm_ = sSq.m(SC.activation(out=vb_[:, 65:129],
                                                  in_=P5[pb2][:, 64:128],
                                                  func=AF.Copy))
                        VE.wait_ge(sSq.h, hm_)
                        VE.tensor_tensor(out=vb_[:, 130:194],
                                         in0=P5[pb2][:, 0:64],
                                         in1=vb_[:, 0:64], op=ALU.subtract)
                        sPE.m(VE.tensor_tensor(out=vb_[:, 195:259],
                                               in0=P5[pb2][:, 64:128],
                                               in1=vb_[:, 65:129],
                                               op=ALU.subtract))
                        gset(f"P5:{pb2}", sPE)
                gset("wsl:v", sMM)
                qkv_done = sPE.n
                if hp == 0:
                    ckpt()

                TEE.wait_ge(sPE.h, qkv_done)
                if hp == 0:
                    TEE.wait_ge(sGp.h, ones_ready)
                    gwait(SC, "PAB1")
                pairs = [(hl, q) for hl in range(2) for q in range(NQB)]

                def build_sw(i2):
                    hl, q = pairs[i2]
                    ab = (hl * NQB + q) % 2
                    sw = ARS_bf[:, 24640 + ab * 512: 25152 + ab * 512]
                    gwait(VE, f"sw7:{ab}")
                    VE.tensor_copy(sw[0:64, :],
                                   qs2[hl][64:128, q * QB:(q + 1) * QB])
                    return sPE.m(VE.tensor_copy(
                        sw[64:128, :], qs2[hl][0:64, q * QB:(q + 1) * QB])), sw

                VE.wait_ge(sTr.h, sTr.n)
                m7n, swn = build_sw(0)
                for i2, (hloc, qb) in enumerate(pairs):
                    if True:
                        m7, sw7 = m7n, swn
                        nkb = 4 * qb + 4
                        SB3 = [0, 1, 3]
                        avb = (hloc * NQB + qb) % 2
                        AVP = PAB[0][:, avb * 512:(avb + 1) * 512]
                        qsl = slice(qb * QB, (qb + 1) * QB)

                        def emit_scores(kb):
                            sb = SB3[kb % 3]
                            gwait(TEE, f"P5:{sb}")
                            dofs = kb - 4 * qb
                            ksl = slice(kb * 128, (kb + 1) * 128)
                            TEE.matmul(P5[sb][:, :], ks2[hloc][:, ksl],
                                       qs2[hloc][:, qsl], start=True, stop=False)
                            mi = TEE.matmul(P5[sb][:, :], ks2[hloc][:, ksl],
                                            sw7[:, :], start=False,
                                            stop=(dofs < 0))
                            if dofs >= 0:
                                mi = TEE.matmul(P5[sb][:, :], identb[:, :],
                                                maskb[:, dofs * QB:(dofs + 1) * QB],
                                                start=False, stop=True)
                            return sMM.m(mi)

                        gwait(TEE, f"AVP:{avb}")
                        TEE.wait_ge(sPE.h, m7)
                        smarks = [emit_scores(0)]
                        if nkb > 1:
                            smarks.append(emit_scores(1))
                        for kb in range(nkb):
                            sb = SB3[kb % 3]
                            pp3 = kb % 3
                            ppf = kb % 2
                            SC.wait_ge(sMM.h, smarks[kb])
                            gwait(SC, f"ptl:{pp3}")
                            gwait(SC, f"pfp:{ppf}")
                            pm1 = sP.m(SC.activation(
                                out=ph3[pp3], in_=P5[sb][:, :], func=AF.Exp,
                                scale=0.125, bias=consts[:, 0:1]))
                            pm2 = sP.m(SC.activation(
                                out=PAB[1][:, ppf * 512:(ppf + 1) * 512],
                                in_=P5[sb][:, :], func=AF.Exp, scale=0.125,
                                bias=consts[:, 0:1]))
                            gset(f"P5:{sb}", sP)
                            if kb + 2 < nkb:
                                smarks.append(emit_scores(kb + 2))
                            VE.wait_ge(sP.h, pm2)
                            plm = sXn.m(VE.tensor_tensor(
                                out=pl3[pp3],
                                in0=PAB[1][:, ppf * 512:(ppf + 1) * 512],
                                in1=ph3[pp3], op=ALU.subtract))
                            gset(f"pfp:{ppf}", sXn)
                            kbase = kb * 260 + hloc * 65
                            vh_ = vhpb[:, kbase: kbase + 65]
                            vl_ = vhpb[:, kbase + 130: kbase + 195]
                            TEE.wait_ge(sP.h, pm1)
                            TEE.matmul(AVP[0:65, :], vh_, ph3[pp3],
                                       start=(kb == 0), stop=False)
                            TEE.matmul(AVP[0:65, :], vl_, ph3[pp3],
                                       start=False, stop=False)
                            TEE.wait_ge(sXn.h, plm)
                            sMM.m(TEE.matmul(AVP[0:65, :], vh_, pl3[pp3],
                                             start=False, stop=(kb == nkb - 1)))
                            gset(f"ptl:{pp3}", sMM)
                        av_done = sMM.n
                        gset(f"sw7:{avb}", sMM)
                        if i2 + 1 < len(pairs):
                            m7n, swn = build_sw(i2 + 1)
                        VE.wait_ge(sMM.h, av_done)
                        VE.tensor_copy(
                            attT[hloc * 64:hloc * 64 + 64,
                                 hp * T + qb * QB: hp * T + (qb + 1) * QB],
                            AVP[0:64, :])
                        gwait(VE, "srowb")
                        srm = sPE.m(VE.tensor_copy(srowb[64:65, 0:512],
                                                   AVP[64:65, :]))
                        gset(f"AVP:{avb}", sPE)
                        GP.wait_ge(sPE.h, srm)
                        if s > 0 and sall_ld_prev[0] is not None:
                            GP.wait_ge(sLd.h, sall_ld_prev[0])
                        sSp.d(GP.dma_start(
                            out=sdram[2 * hp + hloc: 2 * hp + hloc + 1,
                                      qb * QB:(qb + 1) * QB],
                            in_=srowb[64:65, 0:512]))
                        gset("srowb", sSp)
                if hp == 0:
                    ckpt()

            ckpt()
            SY.wait_ge(sSp.h, sSp.n)
            sall_ld = sLd.d(SY.dma_start(out=sall, in_=sdram[:, :]))
            sall_ld_prev[0] = sall_ld
            VE.wait_ge(sLd.h, sall_ld)
            VE.wait_ge(sMM.h, sMM.n)
            sMs.m(VE.reciprocal(srec, sall))
            gset("wstage", sMs)
            for hp in range(8):
                gwait(VE, "oh")
                ohm = sMs.m(VE.tensor_scalar(out=oh[:, :], in0=ohJ[:, :],
                                             scalar1=float(2 * hp), scalar2=None,
                                             op0=ALU.is_equal))
                for qb in range(NQB):
                    TEE.wait_ge(sMs.h, ohm)
                    gwait(TEE, "P5:3")
                    bm = sMM.m(TEE.matmul(P5[3][:, :], oh[:, :],
                                          srec[:, qb * QB:(qb + 1) * QB],
                                          start=True, stop=True))
                    VE.wait_ge(sMM.h, bm)
                    tdiv = ARS[:, 9760:10272]
                    abase = 2 * hp * T + qb * 1024
                    VE.tensor_tensor(
                        out=tdiv,
                        in0=attT[:, hp * T + qb * QB: hp * T + (qb + 1) * QB],
                        in1=P5[3][:, :], op=ALU.mult)
                    VE.tensor_scalar(
                        out=tdiv, in0=tdiv,
                        scalar1=bqkT[:, 16 + hp:17 + hp], scalar2=None,
                        op0=ALU.add)
                    VE.tensor_copy(ARB_bf[:, abase:abase + 512], tdiv)
                    sPE.m(VE.tensor_tensor(
                        out=ARB_bf[:, abase + 512:abase + 1024], in0=tdiv,
                        in1=ARB_bf[:, abase:abase + 512], op=ALU.subtract))
                    gset("P5:3", sPE)
                gset("oh", sMM)
            attT_done = sPE.n
            ckpt()

            SY.wait_ge(sMM.h, sMM.n)
            wo_ld = sLdW.d(SY.dma_start(
                out=wo_full.rearrange("p (c d) -> p c d", c=8),
                in_=wT_in["o"].rearrange("(c p) d -> p c d", p=128)))
            if s == 0:
                bo_ld = sLdW.d(SY.dma_start(out=brows[64:65, :], in_=b_in["o"]))
                TEE.wait_ge(sLdW.h, bo_ld)
            for nb2 in range(2):
                gwait(TEE, f"P5:{nb2}")
                bobm = sMM.m(TEE.matmul(
                    P5[nb2][:, :], onesP[64:65, 0:128],
                    brows[64:65, nb2 * 512:(nb2 + 1) * 512],
                    start=True, stop=True))
            VE.wait_ge(sMM.h, bobm)
            VE.tensor_copy(bob[:, 0:512], P5[0][:, :])
            sPE.m(VE.tensor_copy(bob[:, 512:1024], P5[1][:, :]))
            gset("P5:0", sPE)
            gset("P5:1", sPE)
            woh = ARA_bf[:, 16384:24576]
            wol = ARA_bf[:, 24576:32768]
            VE.wait_ge(sLdW.h, wo_ld)
            VE.tensor_copy(woh, wo_full[:, :])
            wom = sPE.m(VE.tensor_tensor(out=wol, in0=wo_full[:, :], in1=woh,
                                         op=ALU.subtract))
            s1_prev = None
            for tt in range(NTT):
                pb = tt % 2
                gwait(SY, f"xst:{pb}")
                xm2 = sLd.d(SY.dma_start(
                    out=xst[:, pb * D:(pb + 1) * D],
                    in_=x_in[s, tt * 128:(tt + 1) * 128, :]))
                TEE.wait_ge(sPE.h, max(attT_done, wom))
                gwait(TEE, f"PAB{pb}")

                def atts(oc, part):
                    base = (2 * oc * T + (tt // 4) * 1024 + part * 512
                            + (tt % 4) * 128)
                    return ARB_bf[:, base: base + 128]

                for nb in range(2):
                    for wpart, apart in ((0, 0), (0, 1), (1, 0)):
                        wo_ = woh if wpart == 0 else wol
                        for oc in range(8):
                            mi = TEE.matmul(
                                PAB[pb][:, nb * 512:(nb + 1) * 512],
                                atts(oc, apart),
                                wo_[:, oc * 1024 + nb * 512: oc * 1024 + (nb + 1) * 512],
                                start=(wpart == 0 and apart == 0 and oc == 0),
                                stop=(wpart == 1 and oc == 7))
                opm = sMM.m(mi)

                idx = s * NTT + tt
                _md = os.environ.get("K_OPROJ_MODE", "full")
                VE.wait_ge(sMM.h, opm)
                VE.wait_ge(sLd.h, xm2)
                if s1_prev is not None:
                    VE.wait_ge(sSq.h, s1_prev)
                gwait(VE, f"xnst:{pb}")
                VE.scalar_tensor_tensor(
                    out=xnst[:, pb * D:(pb + 1) * D],
                    in0=xst[:, pb * D:(pb + 1) * D],
                    scalar=r_all[:, idx:idx + 1], in1=nwb[:, :],
                    op0=ALU.mult, op1=ALU.mult)
                if _md == "pab":
                    VE.tensor_scalar(out=yt[:, :], in0=PAB[pb][:, :], scalar1=1.0,
                                     scalar2=None, op0=ALU.mult)
                else:
                    VE.tensor_tensor(out=yt[:, :], in0=PAB[pb][:, :],
                                     in1=bob[:, :], op=ALU.add)
                    VE.tensor_tensor(out=yt[:, :], in0=yt[:, :],
                                     in1=xst[:, pb * D:(pb + 1) * D], op=ALU.add)
                pe_m = sPE.m(VE.tensor_tensor(out=yb[:, :], in0=yt[:, :],
                                              in1=xnst[:, pb * D:(pb + 1) * D],
                                              op=ALU.add) if _md != "pab" else
                             VE.tensor_copy(yb[:, :], yt[:, :]))
                gset(f"PAB{pb}", sPE)
                gset(f"xnst:{pb}", sPE)
                SC.wait_ge(sPE.h, pe_m)
                SC.activation(out=yt[:, :], in_=yb[:, :], func=AF.Copy,
                              accum_out=stat[:, 4:5])
                SC.activation(out=yt[:, :], in_=yb[:, :], func=AF.Square,
                              accum_out=stat[:, 5:6])
                s1 = sSq.m(SC.activation(out=stat[:, 16:17], in_=stat[:, 4:5],
                                         func=AF.Copy))
                s1_prev = s1
                VE.wait_ge(sSq.h, s1)
                VE.tensor_scalar(out=stat[:, 6:7], in0=stat[:, 4:5],
                                 scalar1=INV1024, scalar2=None, op0=ALU.mult)
                vm1 = sMs.m(VE.tensor_scalar(out=stat[:, 7:8], in0=stat[:, 5:6],
                                             scalar1=INV1024, scalar2=None,
                                             op0=ALU.mult))
                SC.wait_ge(sMs.h, vm1)
                am1 = sSq.m(SC.activation(out=stat[:, 8:9], in_=stat[:, 6:7],
                                          func=AF.Square))
                VE.wait_ge(sSq.h, am1)
                v1 = sMs.m(VE.scalar_tensor_tensor(
                    out=stat[:, 10:11], in0=stat[:, 7:8], scalar=1e-5,
                    in1=stat[:, 8:9], op0=ALU.add, op1=ALU.subtract))
                SC.wait_ge(sMs.h, v1)
                SC.activation(out=stat[:, 11:12], in_=stat[:, 10:11], func=AF.Ln)
                r1 = sR.m(SC.activation(out=stat[:, 12:13], in_=stat[:, 11:12],
                                        func=AF.Exp, scale=-0.5))
                VE.wait_ge(sR.h, r1)
                if dbg_n is not None:
                    for ci, sc_ in enumerate((6, 12, 4, 5, 7, 8, 9, 10, 11)):
                        VE.tensor_copy(vm_all[:, 32 * ci + idx:32 * ci + idx + 1],
                                       stat[:, sc_:sc_ + 1])
                if _md in ("pab", "noln"):
                    VE.tensor_copy(xst[:, pb * D:(pb + 1) * D], yb[:, :])
                else:
                    VE.tensor_scalar(out=xst[:, pb * D:(pb + 1) * D], in0=yb[:, :],
                                     scalar1=stat[:, 6:7], scalar2=stat[:, 12:13],
                                     op0=ALU.subtract, op1=ALU.mult)
                x1m = sMs.m(VE.tensor_reduce(
                    out=m1_all[:, idx:idx + 1], in_=xst[:, pb * D:(pb + 1) * D],
                    axis=AX.X, op=ALU.max, apply_absolute_value=True))
                GP.wait_ge(sMs.h, x1m)
                sSp.d(GP.dma_start(
                    out=x1dram[idx * 128:(idx + 1) * 128, :],
                    in_=xst[:, pb * D:(pb + 1) * D]))
                gset(f"xst:{pb}", sSp)

            ckpt()

        if stage == "ln1":
            GP.wait_ge(sSp.h, sSp.n)
            for i4 in range(4):
                sOut.d(GP.dma_start(
                    out=out[i4 // 2, (i4 % 2) * 1024:(i4 % 2 + 1) * 1024, :],
                    in_=x1dram[i4 * 1024:(i4 + 1) * 1024, :]))
            GP.wait_ge(sOut.h, sOut.n)
            return nc

        barrier()
        g1m = sMs.m(VE.tensor_reduce(out=stat[:, 3:4], in_=m1_all[:, 0:NRT],
                                     axis=AX.X, op=ALU.max))
        GP.wait_ge(sMs.h, g1m)
        sSp.d(GP.dma_start(out=m128d, in_=stat[:, 3:4]))
        SY.wait_ge(sSp.h, sSp.n)
        sLd.d(SY.dma_start(out=rowt[:, :], in_=m128d))
        VE.wait_ge(sLd.h, sLd.n)
        gm1 = sMs.m(VE.tensor_reduce(out=gsc[0:1, 0:1], in_=rowt[:, :],
                                     axis=AX.X, op=ALU.max))
        GP.wait_ge(sMs.h, gm1)
        GP.tensor_copy(ccst[0:1, 0:1], gsc[0:1, 0:1])
        sSp.d(GP.dma_start(out=cc_in[0].ap(), in_=ccst[:, :]))
        GP.wait_ge(sSp.h, sSp.n)
        GP.collective_compute(
            "AllReduce", ALU.max, ins=[cc_in[0].ap().opt()],
            outs=[cc_out[0].ap().opt()],
            replica_groups=[[0, 1, 2, 3, 4, 5, 6, 7]]).then_inc(sGp.h, 1)
        sGp.n += 1
        GP.wait_ge(sGp.h, sGp.n)
        sSp.d(GP.dma_start(out=rowt[0:1, 0:64], in_=cc_out[0].ap()))
        SC.wait_ge(sSp.h, sSp.n)
        SC.activation(out=gsc[0:1, 5:6], in_=rowt[0:1, 0:1], func=AF.Copy,
                      scale=sig2[0:1, 0:1])
        a1g = sSq.m(SC.activation(out=gsc[0:1, 3:4], in_=gsc[0:1, 5:6],
                                  func=AF.Copy, scale=INV127))
        VE.wait_ge(sSq.h, a1g)
        v1g = sMs.m(VE.reciprocal(gsc[0:1, 8:9], gsc[0:1, 3:4]))
        SC.wait_ge(sMs.h, v1g)
        scm = sSq.m(SC.activation(out=gsc[0:1, 4:5], in_=gsc[0:1, 8:9],
                                  func=AF.Copy))
        TEE.wait_ge(sSq.h, scm)
        bcm = sMM.m(TEE.matmul(P5[0][:, 0:2], onesP[0:1, 0:128], gsc[0:1, 3:5],
                               start=True, stop=True))
        VE.wait_ge(sMM.h, bcm)
        sPE.m(VE.tensor_copy(scbc[:, 0:2], P5[0][:, 0:2]))
        scx_ready = sPE.n
        ckpt()

        barrier()
        gates.pop("P5:0", None); gates.pop("P5:1", None)
        gates.pop("xst:0", None); gates.pop("xst:1", None)
        for t32 in range(NRT):
            pb = t32 % 2
            gwait(SY, f"xst:{pb}")
            xm = sLd.d(SY.dma_start(
                out=xst[:, pb * D:(pb + 1) * D],
                in_=x1dram[t32 * 128:(t32 + 1) * 128, :]))
            VE.wait_ge(sLd.h, xm)
            VE.wait_ge(sPE.h, scx_ready)
            gwait(VE, "wq")
            VE.tensor_scalar(out=wq_u[:, 0:1024], in0=xst[:, pb * D:(pb + 1) * D],
                             scalar1=scbc[:, 1:2], scalar2=0.5,
                             op0=ALU.mult, op1=ALU.add)
            gset(f"xst:{pb}", sMs, sMs.n + 1)
            VE.tensor_copy(wq_t1i[:, 0:1024], wq_u[:, 0:1024])
            VE.tensor_copy(wq_t2[:, 0:1024], wq_t1i[:, 0:1024])
            VE.tensor_tensor(out=wq_t1[:, 0:1024], in0=wq_t2[:, 0:1024],
                             in1=wq_u[:, 0:1024], op=ALU.is_gt)
            VE.tensor_tensor(out=wq_rnd[:, 0:1024], in0=wq_t2[:, 0:1024],
                             in1=wq_t1[:, 0:1024], op=ALU.subtract)
            qm = sMs.m(VE.tensor_scalar(out=wq_rnd[:, 0:1024], in0=wq_rnd[:, 0:1024],
                                        scalar1=-128.0, scalar2=127.0,
                                        op0=ALU.max, op1=ALU.min))
            for dc in range(8):
                j = t32 * 8 + dc
                TEE.wait_ge(sMs.h, qm)
                gwait(TEE, f"P5:{j % 2}")
                mtr = sTr.m(TEE.transpose(P5[j % 2][:, 0:128],
                                          wq_rnd[:, dc * 128:(dc + 1) * 128],
                                          identf[:, :]))
                VE.wait_ge(sTr.h, mtr)
                em = sEv.m(VE.tensor_copy(
                    xqT_bf[:, dc * R + t32 * 128: dc * R + (t32 + 1) * 128],
                    P5[j % 2][:, 0:128]))
                gset(f"P5:{j % 2}", sEv)
                if dc == 7:
                    gset("wq", sEv, em)
        xqT_ready = sEv.n
        ckpt()

        def prepB(j):
            bb = j % 2
            gwait(SY, "wstg")
            sLd.d(SY.dma_start(out=wstg[:, 0:1024],
                               in_=wg_in[j * 128:(j + 1) * 128, :]))
            l2 = sLd.d(SY.dma_start(out=wstg[:, 1024:2048],
                                    in_=wu_in[j * 128:(j + 1) * 128, :]))
            VE.wait_ge(sLd.h, l2)
            VE.tensor_reduce(out=stat[:, 17:18], in_=wstg[:, 0:1024],
                             axis=AX.X, op=ALU.max, apply_absolute_value=True)
            vr = sMs.m(VE.tensor_reduce(
                out=stat[:, 18:19], in_=wstg[:, 1024:2048],
                axis=AX.X, op=ALU.max, apply_absolute_value=True))
            SC.wait_ge(sMs.h, vr)
            ar = sSq.m(SC.activation(out=swg[:, 2 * bb:2 * bb + 2],
                                     in_=stat[:, 17:19], func=AF.Copy,
                                     scale=INV127))
            VE.wait_ge(sSq.h, ar)
            vr2 = sMs.m(VE.reciprocal(stat[:, 17:19], swg[:, 2 * bb:2 * bb + 2]))
            SC.wait_ge(sMs.h, vr2)
            ar2 = sSq.m(SC.activation(out=swu[:, 2 * bb:2 * bb + 2],
                                      in_=stat[:, 17:19], func=AF.Copy))
            VE.wait_ge(sSq.h, ar2)
            gwait(VE, "wq")
            qms = []
            for wi in range(2):
                sl = slice(wi * 1024, (wi + 1) * 1024)
                VE.tensor_scalar(out=wq_u[:, 0:1024], in0=wstg[:, sl],
                                 scalar1=swu[:, 2 * bb + wi:2 * bb + wi + 1],
                                 scalar2=0.5, op0=ALU.mult, op1=ALU.add)
                VE.tensor_copy(wq_t1i[:, 0:1024], wq_u[:, 0:1024])
                VE.tensor_copy(wq_t2[:, 0:1024], wq_t1i[:, 0:1024])
                VE.tensor_tensor(out=wq_t1[:, 0:1024], in0=wq_t2[:, 0:1024],
                                 in1=wq_u[:, 0:1024], op=ALU.is_gt)
                qms.append(sMs.m(VE.tensor_tensor(
                    out=wq_rnd[:, wi * 1024:(wi + 1) * 1024],
                    in0=wq_t2[:, 0:1024], in1=wq_t1[:, 0:1024], op=ALU.subtract)))
            gset("wstg", sMs)
            return ar, qms

        def prepB_T(j, qms):
            bb = j % 2
            for wi, dstT in ((0, wgT), (1, wuT)):
                for dc in range(8):
                    TEE.wait_ge(sMs.h, qms[wi])
                    gwait(TEE, f"P5:{dc % 2}")
                    mtr = sTr.m(TEE.transpose(
                        P5[dc % 2][:, 0:128],
                        wq_rnd[:, wi * 1024 + dc * 128: wi * 1024 + (dc + 1) * 128],
                        identf[:, :]))
                    VE.wait_ge(sTr.h, mtr)
                    gwait(VE, f"wgT:{bb}")
                    sEv.m(VE.tensor_copy(
                        dstT[bb][:, dc * 128:(dc + 1) * 128], P5[dc % 2][:, 0:128]))
                    gset(f"P5:{dc % 2}", sEv)
            gset("wq", sEv)
            return sEv.n

        ar_p, qms_p = prepB(0)
        wready_p = prepB_T(0, qms_p)
        for ib in range(NIC):
            bb = ib % 2
            armark, wready = ar_p, wready_p
            VE.wait_ge(sPE.h, scx_ready)
            VE.wait_ge(sSq.h, armark)
            VE.tensor_tensor(out=sgu[:, 2 * bb:2 * bb + 1],
                             in0=swg[:, 2 * bb:2 * bb + 1], in1=scbc[:, 0:1],
                             op=ALU.mult)
            sgm = sMs.m(VE.tensor_tensor(out=sgu[:, 2 * bb + 1:2 * bb + 2],
                                         in0=swg[:, 2 * bb + 1:2 * bb + 2],
                                         in1=scbc[:, 0:1], op=ALU.mult))
            if ib + 1 < NIC:
                ar_p, qms_p = prepB(ib + 1)
            for tb in range(8):
                pp = tb % 2
                TEE.wait_ge(sEv.h, max(wready, xqT_ready))
                gwait(TEE, f"PAB{pp}")
                for half, wT_ in ((0, wgT), (1, wuT)):
                    for dc in range(8):
                        mi = TEE.matmul(
                            PAB[pp][:, half * 512:(half + 1) * 512],
                            wT_[bb][:, dc * 128:(dc + 1) * 128],
                            xqT_bf[:, dc * R + tb * 512: dc * R + (tb + 1) * 512],
                            start=(dc == 0), stop=(dc == 7))
                gm_ = sMM.m(mi)
                gset(f"wgT:{bb}", sMM)
                SC.wait_ge(sMM.h, gm_)
                SC.wait_ge(sMs.h, sgm)
                gwait(SC, f"atl:{pp}")
                am = sSq.m(SC.activation(out=atl[pp], in_=PAB[pp][:, 0:512],
                                         func=AF.Silu,
                                         scale=sgu[:, 2 * bb:2 * bb + 1]))
                VE.wait_ge(sSq.h, am)
                gwait(VE, f"vtl:{pp}")
                VE.scalar_tensor_tensor(out=vtl[pp], in0=PAB[pp][:, 512:1024],
                                        scalar=sgu[:, 2 * bb + 1:2 * bb + 2],
                                        in1=atl[pp], op0=ALU.mult, op1=ALU.mult)
                gset(f"atl:{pp}", sMs, sMs.n + 1)
                vm = sMs.m(VE.tensor_reduce(
                    out=vm_all[:, ib * 8 + tb: ib * 8 + tb + 1], in_=vtl[pp],
                    axis=AX.X, op=ALU.max, apply_absolute_value=True))
                gates[f"PAB{pp}"] = (sMs.h, vm)
                GP.wait_ge(sMs.h, vm)
                sSp.d(GP.dma_start(
                    out=vdram[ib * 128:(ib + 1) * 128, tb * 512:(tb + 1) * 512],
                    in_=vtl[pp]))
                gset(f"vtl:{pp}", sSp)
            if ib + 1 < NIC:
                wready_p = prepB_T(ib + 1, qms_p)

        ckpt()
        if stage == "ffnv":
            GP.wait_ge(sSp.h, sSp.n)
            for i4 in range(4):
                sOut.d(GP.dma_start(
                    out=out[i4 // 2, (i4 % 2) * 1024:(i4 % 2 + 1) * 1024, :],
                    in_=x1dram[i4 * 1024:(i4 + 1) * 1024, :]))
            for ic in range(NIC):
                sOut.d(GP.dma_start(out=dbgv[ic * 128:(ic + 1) * 128, :],
                                    in_=vdram[ic * 128:(ic + 1) * 128, :]))
            GP.wait_ge(sOut.h, sOut.n)
            return nc

        barrier()
        g2m = sMs.m(VE.tensor_reduce(out=stat[:, 3:4], in_=vm_all[:, 0:NIC * 8],
                                     axis=AX.X, op=ALU.max))
        GP.wait_ge(sMs.h, g2m)
        sSp.d(GP.dma_start(out=m128d, in_=stat[:, 3:4]))
        SY.wait_ge(sSp.h, sSp.n)
        sLd.d(SY.dma_start(out=rowt[:, :], in_=m128d))
        VE.wait_ge(sLd.h, sLd.n)
        gm2 = sMs.m(VE.tensor_reduce(out=gsc[0:1, 8:9], in_=rowt[:, :],
                                     axis=AX.X, op=ALU.max))
        GP.wait_ge(sMs.h, gm2)
        GP.tensor_copy(ccst[0:1, 0:1], gsc[0:1, 8:9])
        sSp.d(GP.dma_start(out=cc_in[1].ap(), in_=ccst[:, :]))
        GP.wait_ge(sSp.h, sSp.n)
        GP.collective_compute(
            "AllReduce", ALU.max, ins=[cc_in[1].ap().opt()],
            outs=[cc_out[1].ap().opt()],
            replica_groups=[[0, 1, 2, 3, 4, 5, 6, 7]]).then_inc(sGp.h, 1)
        sGp.n += 1
        GP.wait_ge(sGp.h, sGp.n)
        sSp.d(GP.dma_start(out=rowt[0:1, 0:64], in_=cc_out[1].ap()))
        SC.wait_ge(sSp.h, sSp.n)
        SC.activation(out=gsc[0:1, 13:14], in_=rowt[0:1, 0:1], func=AF.Copy,
                      scale=sig2[0:1, 1:2])
        a2g = sSq.m(SC.activation(out=gsc[0:1, 11:12], in_=gsc[0:1, 13:14],
                                  func=AF.Copy, scale=INV127))
        VE.wait_ge(sSq.h, a2g)
        v2g = sMs.m(VE.reciprocal(gsc[0:1, 14:15], gsc[0:1, 11:12]))
        SC.wait_ge(sMs.h, v2g)
        scm2 = sSq.m(SC.activation(out=gsc[0:1, 12:13], in_=gsc[0:1, 14:15],
                                   func=AF.Copy))
        TEE.wait_ge(sSq.h, scm2)
        bcm2 = sMM.m(TEE.matmul(P5[0][:, 0:2], onesP[0:1, 0:128], gsc[0:1, 11:13],
                                start=True, stop=True))
        VE.wait_ge(sMM.h, bcm2)
        sPE.m(VE.tensor_copy(scbc[:, 2:4], P5[0][:, 0:2]))
        sch_ready = sPE.n
        ckpt()

        barrier()
        gates.pop("wstg", None); gates.pop("wq", None)
        gates.pop("P5:0", None); gates.pop("P5:1", None)
        gates.pop("xst:0", None); gates.pop("xst:1", None)
        for ds in range(8):
            gwait(SY, "wstg")
            l1 = sLd.d(SY.dma_start(out=wstg[:, :],
                                    in_=wd_in[ds * 128:(ds + 1) * 128, :]))
            VE.wait_ge(sLd.h, l1)
            gwait(VE, "wq")
            vr = sMs.m(VE.tensor_reduce(out=stat[:, 17:18], in_=wstg[:, :],
                                        axis=AX.X, op=ALU.max,
                                        apply_absolute_value=True))
            SC.wait_ge(sMs.h, vr)
            ar = sSq.m(SC.activation(out=sdall[:, ds:ds + 1], in_=stat[:, 17:18],
                                     func=AF.Copy, scale=INV127))
            VE.wait_ge(sSq.h, ar)
            vr2 = sMs.m(VE.reciprocal(stat[:, 18:19], sdall[:, ds:ds + 1]))
            SC.wait_ge(sMs.h, vr2)
            ar2 = sSq.m(SC.activation(out=stat[:, 13:14], in_=stat[:, 18:19],
                                      func=AF.Copy))
            VE.wait_ge(sSq.h, ar2)
            VE.tensor_scalar(out=wq_u[:, :], in0=wstg[:, :],
                             scalar1=stat[:, 13:14], scalar2=0.5,
                             op0=ALU.mult, op1=ALU.add)
            gset("wstg", sMs, sMs.n + 1)
            VE.tensor_copy(wq_t1i[:, :], wq_u[:, :])
            VE.tensor_copy(wq_t2[:, :], wq_t1i[:, :])
            VE.tensor_tensor(out=wq_t1[:, :], in0=wq_t2[:, :], in1=wq_u[:, :],
                             op=ALU.is_gt)
            qm = sMs.m(VE.tensor_tensor(out=wq_rnd[:, :], in0=wq_t2[:, :],
                                        in1=wq_t1[:, :], op=ALU.subtract))
            for ic in range(NIC):
                TEE.wait_ge(sMs.h, qm)
                gwait(TEE, f"P5:{ic % 2}")
                mtr = sTr.m(TEE.transpose(P5[ic % 2][:, 0:128],
                                          wq_rnd[:, ic * 128:(ic + 1) * 128],
                                          identf[:, :]))
                VE.wait_ge(sTr.h, mtr)
                sEv.m(VE.tensor_copy(
                    wdT_bf[:, ic * 1024 + ds * 128: ic * 1024 + (ds + 1) * 128],
                    P5[ic % 2][:, 0:128]))
                gset(f"P5:{ic % 2}", sEv)
            gset("wq", sEv)
        wdT_ready = sEv.n
        GP.wait_ge(sEv.h, wdT_ready)
        sSp.d(GP.dma_start(out=rsdd.rearrange("a (c p) -> (a p) c", p=128),
                           in_=sdall[:, 0:8], allow_slow_non_contiguous=True))
        SY.wait_ge(sSp.h, sSp.n)
        rl = sLd.d(SY.dma_start(out=nwrow[:, :], in_=rsdd))
        TEE.wait_ge(sLd.h, rl)
        gwait(TEE, "P5:0")
        TEE.matmul(P5[0][:, :], onesP[0:1, 0:128], nwrow[0:1, 0:512],
                   start=True, stop=True)
        gwait(TEE, "P5:1")
        sdm = sMM.m(TEE.matmul(P5[1][:, :], onesP[0:1, 0:128], nwrow[0:1, 512:1024],
                               start=True, stop=True))
        VE.wait_ge(sMM.h, sdm)
        VE.tensor_copy(sdb[:, 0:512], P5[0][:, :])
        sdbm = sPE.m(VE.tensor_copy(sdb[:, 512:1024], P5[1][:, :]))
        gset("P5:0", sPE)
        gset("P5:1", sPE)
        ckpt()

        vstgB = [vstg, ARS[:, 8448:11264]]
        htileB = [htile, ARA_bf[:, 27392:30208]]
        sLdVB = [sLdV, sLdV2]

        def ldV(i):
            pv = i % 2
            gwait(SY, f"vstg:{pv}")
            if i < 2:
                gwait(SY, "wq")
                gwait(SY, "wstg")
            return sLdVB[pv].d(SY.dma_start(
                out=vstgB[pv].rearrange("p (c t) -> p c t", c=NIC),
                in_=vdram[:, i * 128:(i + 1) * 128]
                .rearrange("(c p) t -> p c t", p=128)))

        def ldX(i):
            pv = i % 2
            gwait(SY, f"xst:{pv}")
            return sLd.d(SY.dma_start(
                out=xst[:, pv * D:(pv + 1) * D],
                in_=x1dram[i * 128:(i + 1) * 128, :]))

        def quantC(i, vm_):
            pv = i % 2
            VE.wait_ge(sLdVB[pv].h, vm_)
            VE.wait_ge(sPE.h, sch_ready)
            gwait(VE, f"htile:{pv}")
            VE.tensor_scalar(out=hq_u[:, :], in0=vstgB[pv], scalar1=scbc[:, 3:4],
                             scalar2=0.5, op0=ALU.mult, op1=ALU.add)
            gset(f"vstg:{pv}", sMs, sMs.n + 1)
            VE.tensor_copy(hq_t1i[:, :], hq_u[:, :])
            VE.tensor_copy(hq_t2[:, :], hq_t1i[:, :])
            VE.tensor_tensor(out=hq_t1[:, :], in0=hq_t2[:, :], in1=hq_u[:, :],
                             op=ALU.is_gt)
            VE.tensor_tensor(out=hq_u[:, :], in0=hq_t2[:, :], in1=hq_t1[:, :],
                             op=ALU.subtract)
            return sMs.m(VE.tensor_scalar(out=htileB[pv], in0=hq_u[:, :],
                                          scalar1=-128.0, scalar2=127.0,
                                          op0=ALU.max, op1=ALU.min))

        s2_prev = None
        vlds = {0: ldV(0), 1: ldV(1)}
        xms = {0: ldX(0), 1: ldX(1)}
        hm_p = quantC(0, vlds[0])
        for t32 in range(NRT):
            pb = t32 % 2
            s_, tt = t32 // NTT, t32 % NTT
            hm, xm_ = hm_p, xms[t32]
            if t32 + 1 < NRT:
                hm_p = quantC(t32 + 1, vlds[t32 + 1])
            TEE.wait_ge(sMs.h, hm)
            TEE.wait_ge(sEv.h, wdT_ready)
            TEE.wait_ge(sPE.h, sdbm)
            gwait(TEE, f"PAB{pb}")
            for nb in range(2):
                for ic in range(NIC):
                    mi = TEE.matmul(
                        PAB[pb][:, nb * 512:(nb + 1) * 512],
                        htileB[pb][:, ic * 128:(ic + 1) * 128],
                        wdT_bf[:, ic * 1024 + nb * 512: ic * 1024 + (nb + 1) * 512],
                        start=(ic == 0), stop=(ic == NIC - 1))
            dm = sMM.m(mi)
            gset(f"htile:{pb}", sMM)
            _mc = os.environ.get("K_C_MODE", "full")
            VE.wait_ge(sMM.h, dm)
            VE.wait_ge(sLd.h, xm_)
            if s2_prev is not None:
                VE.wait_ge(sSq.h, s2_prev)
            VE.scalar_tensor_tensor(out=yb[:, :], in0=PAB[pb][:, :],
                                    scalar=scbc[:, 2:3], in1=sdb[:, :],
                                    op0=ALU.mult, op1=ALU.mult)
            gset(f"PAB{pb}", sMs, sMs.n + 1)
            pe2 = sMs.m(VE.tensor_tensor(out=yt[:, :], in0=yb[:, :],
                                          in1=xst[:, pb * D:(pb + 1) * D],
                                          op=ALU.add) if _mc == "full" else
                        VE.tensor_copy(yt[:, :], yb[:, :]))
            SC.wait_ge(sMs.h, pe2)
            SC.activation(out=yb[:, :], in_=yt[:, :], func=AF.Copy,
                          accum_out=stat[:, 4:5])
            SC.activation(out=yb[:, :], in_=yt[:, :], func=AF.Square,
                          accum_out=stat[:, 5:6])
            s2 = sSq.m(SC.activation(out=stat[:, 16:17], in_=stat[:, 4:5],
                                     func=AF.Copy))
            s2_prev = s2
            VE.wait_ge(sSq.h, s2)
            VE.tensor_scalar(out=stat[:, 6:7], in0=stat[:, 4:5],
                             scalar1=INV1024, scalar2=None, op0=ALU.mult)
            vm1c = sMs.m(VE.tensor_scalar(out=stat[:, 7:8], in0=stat[:, 5:6],
                                          scalar1=INV1024, scalar2=None,
                                          op0=ALU.mult))
            SC.wait_ge(sMs.h, vm1c)
            am1c = sSq.m(SC.activation(out=stat[:, 8:9], in_=stat[:, 6:7],
                                       func=AF.Square))
            VE.wait_ge(sSq.h, am1c)
            v2 = sMs.m(VE.scalar_tensor_tensor(
                out=stat[:, 10:11], in0=stat[:, 7:8], scalar=1e-5,
                in1=stat[:, 8:9], op0=ALU.add, op1=ALU.subtract))
            SC.wait_ge(sMs.h, v2)
            SC.activation(out=stat[:, 11:12], in_=stat[:, 10:11], func=AF.Ln)
            r2 = sR.m(SC.activation(out=stat[:, 12:13], in_=stat[:, 11:12],
                                    func=AF.Exp, scale=-0.5))
            VE.wait_ge(sR.h, r2)
            om = sMs.m(VE.tensor_scalar(out=xst[:, pb * D:(pb + 1) * D],
                                        in0=yt[:, :],
                                        scalar1=stat[:, 6:7], scalar2=stat[:, 12:13],
                                        op0=ALU.subtract, op1=ALU.mult)
                       if _mc == "full" else
                       VE.tensor_copy(xst[:, pb * D:(pb + 1) * D], yt[:, :]))
            GP.wait_ge(sMs.h, om)
            sOut.d(GP.dma_start(out=out[s_, tt * 128:(tt + 1) * 128, :],
                                in_=xst[:, pb * D:(pb + 1) * D]))
            gset(f"xst:{pb}", sOut)
            if t32 + 2 < NRT:
                vlds[t32 + 2] = ldV(t32 + 2)
                xms[t32 + 2] = ldX(t32 + 2)
            if t32 == int(os.environ.get("K_T32", "0")):
                ckpt()

        if stage == "full2":
            for ic in range(NIC):
                sOut.d(GP.dma_start(out=dbgv[ic * 128:(ic + 1) * 128, :],
                                    in_=vdram[ic * 128:(ic + 1) * 128, :]))
            for i4 in range(4):
                sOut.d(GP.dma_start(out=dbgX2[i4 * 1024:(i4 + 1) * 1024, :],
                                    in_=x1dram[i4 * 1024:(i4 + 1) * 1024, :]))
        GP.wait_ge(sOut.h, sOut.n)
      except StopBuild:
        for s_ in ALLSEMS:
            GP.wait_ge(s_.h, s_.n)
        sOut.d(GP.dma_start(out=dbgA, in_=ARA[:, :]))
        sOut.d(GP.dma_start(out=dbgB, in_=ARB[:, :]))
        sOut.d(GP.dma_start(out=dbgS, in_=ARS[:, :]))
        sOut.d(GP.dma_start(out=dbgX, in_=x1dram))
        sOut.d(GP.dma_start(out=dbgSd, in_=sdram))
        sOut.d(GP.dma_start(out=dbgM, in_=vm_all[:, 0:320]))
        sOut.d(GP.dma_start(out=dbgW[:, 0:8], in_=scbc[:, :]))
        sOut.d(GP.dma_start(out=dbgW[:, 8:16], in_=sdall[:, :]))
        sOut.d(GP.dma_start(out=dbgW[:, 16:36], in_=stat[:, :]))
        sOut.d(GP.dma_start(out=dbgW[0:1, 36:48], in_=gsc[0:1, 0:12]))
        GP.wait_ge(sOut.h, sOut.n)

    return nc


def kernel(**inputs):
    x = np.ascontiguousarray(np.asarray(inputs["L0_attn_x"], dtype=np.float32))
    nw = np.asarray(inputs["L0_attn_norm_w"], dtype=np.float32).reshape(1, D)
    Ws = {w: np.ascontiguousarray(
        np.asarray(inputs[f"L0_attn_W{w}"], dtype=np.float32).T) for w in "qkvo"}
    bs = {w: np.asarray(inputs[f"L0_attn_b{w}"], dtype=np.float32).reshape(1, D)
          for w in "qkvo"}
    clips = np.array([[np.asarray(inputs["L0_ffn_clip_x"], np.float32).ravel()[0],
                       np.asarray(inputs["L0_ffn_clip_down"], np.float32).ravel()[0]]],
                     dtype=np.float32)
    wg = np.ascontiguousarray(np.asarray(inputs["L0_ffn_w_gate"], np.float32))
    wu = np.ascontiguousarray(np.asarray(inputs["L0_ffn_w_up"], np.float32))
    wd = np.ascontiguousarray(np.asarray(inputs["L0_ffn_w_down"], np.float32))

    nc = build_program("full")
    in_maps = []
    for c in range(8):
        m = {"x": np.ascontiguousarray(x[2 * c:2 * c + 2]),
             "norm_w": nw, "clips": clips,
             "w_gate": wg, "w_up": wu, "w_down": wd}
        for w in "qkvo":
            m[f"w{w}T"] = Ws[w]
            m[f"b{w}"] = bs[w]
        in_maps.append(m)
    res = run_bass_kernel_spmd(nc, in_maps, core_ids=list(range(8)))
    return np.concatenate([np.asarray(r["out"]) for r in res.results], axis=0)


# version 16
# speedup vs baseline: 1.0493x; 1.0493x over previous
import os
import numpy as np
import ml_dtypes
import concourse.bass as bass
import concourse.mybir as mybir
from concourse.bass_utils import run_bass_kernel_spmd

F32 = mybir.dt.float32
BF16 = mybir.dt.bfloat16
I32 = mybir.dt.int32
AF = mybir.ActivationFunctionType
ALU = mybir.AluOpType
AX = mybir.AxisListType

T = 2048
NSEQ = 2
D = 1024
II = 2816
NIC = II // 128
NTT = T // 128
R = NSEQ * T
NRT = R // 128
QB = 512
NQB = T // QB
INV127 = float(np.float32(1.0) / np.float32(127.0))
INV1024 = float(np.float32(1.0) / np.float32(1024.0))


class Sem:
    def __init__(self, nc, name):
        self.h = nc.alloc_semaphore(name)
        self.n = 0

    def d(self, instr):
        instr.then_inc(self.h, 16)
        self.n += 16
        return self.n

    def m(self, instr):
        instr.then_inc(self.h, 1)
        self.n += 1
        return self.n


class StopBuild(Exception):
    pass


def build_program(stage="full"):
    nc = bass.Bass("TRN2", target_bir_lowering=False,
                   detect_race_conditions=False)
    SY, TEE, VE, SC, GP = nc.sync, nc.tensor, nc.vector, nc.scalar, nc.gpsimd

    x_in = nc.dram_tensor("x", [NSEQ, T, D], F32, kind="ExternalInput").ap()
    nw_in = nc.dram_tensor("norm_w", [1, D], F32, kind="ExternalInput").ap()
    wT_in = {w: nc.dram_tensor(f"w{w}T", [D, D], F32, kind="ExternalInput").ap()
             for w in "qkvo"}
    b_in = {w: nc.dram_tensor(f"b{w}", [1, D], F32, kind="ExternalInput").ap()
            for w in "qkvo"}
    clip_in = nc.dram_tensor("clips", [1, 2], F32, kind="ExternalInput").ap()
    wg_in = nc.dram_tensor("w_gate", [II, D], F32, kind="ExternalInput").ap()
    wu_in = nc.dram_tensor("w_up", [II, D], F32, kind="ExternalInput").ap()
    wd_in = nc.dram_tensor("w_down", [D, II], F32, kind="ExternalInput").ap()
    out = nc.dram_tensor("out", [NSEQ, T, D], F32, kind="ExternalOutput").ap()
    if stage in ("ln1", "ffnv", "full2"):
        dbgv = nc.dram_tensor("dbgv", [II, R], F32, kind="ExternalOutput").ap()
    if stage == "full2":
        dbgX2 = nc.dram_tensor("dbgX2", [R, D], F32, kind="ExternalOutput").ap()
    dbg_n = int(stage[4:]) if isinstance(stage, str) and stage.startswith("dbg:") else None
    if dbg_n is not None:
        dbgA = nc.dram_tensor("dbgA", [128, 16384], F32, kind="ExternalOutput").ap()
        dbgB = nc.dram_tensor("dbgB", [128, 16384], F32, kind="ExternalOutput").ap()
        dbgS = nc.dram_tensor("dbgS", [128, 14336], F32, kind="ExternalOutput").ap()
        dbgX = nc.dram_tensor("dbgX", [R, D], F32, kind="ExternalOutput").ap()
        dbgSd = nc.dram_tensor("dbgSd", [16, T], F32, kind="ExternalOutput").ap()
        dbgM = nc.dram_tensor("dbgM", [128, 320], F32, kind="ExternalOutput").ap()
        dbgW = nc.dram_tensor("dbgW", [128, 48], F32, kind="ExternalOutput").ap()

    x1dram = nc.dram_tensor("x1dram", [R, D], F32).ap()
    vdram = nc.dram_tensor("vdram", [II, R], F32).ap()
    sdram = nc.dram_tensor("sdram", [16, T], F32).ap()
    m128d = nc.dram_tensor("m128d", [1, 128], F32).ap()
    rsdd = nc.dram_tensor("rsdd", [1, D], F32).ap()
    cc_in = [nc.dram_tensor(f"cc_in{i}", [1, 64], F32) for i in range(2)]
    cc_out = [nc.dram_tensor(f"cc_out{i}", [1, 64], F32) for i in range(2)]

    A = nc.alloc_sbuf_tensor
    ARA = A("ARA", [128, 16384], F32)
    ARB = A("ARB", [128, 16384], F32)
    ARS = A("ARS", [128, 14336], F32)
    ARA_bf = ARA.bitcast(BF16)
    ARA_i = ARA.bitcast(I32)
    ARB_bf = ARB.bitcast(BF16)
    ARS_bf = ARS.bitcast(BF16)
    ARS_i = ARS.bitcast(I32)

    xnT = ARA
    attT = ARB
    qT = ARS[:, 0:2048]
    kT = ARS[:, 2048:4096]
    qs2 = [ARS_bf[:, 0:2048], ARS_bf[:, 2048:4096]]
    ks2 = [ARS_bf[:, 4096:6144], ARS_bf[:, 6144:8192]]
    dh_s = ARS_bf[:, 19520:20032]
    dl_s = ARS_bf[:, 20032:20544]
    wstage = ARS[:, 11296:12320]
    a1scr = ARS[:, 2048:3072]
    xnT_bf = ARA_bf
    vhp = ARS[:, 4096:6176]
    wsq = ARS[:, 6176:7200]
    wsk = ARS[:, 7200:8224]
    wsv = ARS[:, 8224:9248]
    srowb = ARS[:, 9248:9760]
    ph3 = [ARS_bf[:, 19520:20032], ARS_bf[:, 20544:21056],
           ARS_bf[:, 21568:22080]]
    pl3 = [ARS_bf[:, 20032:20544], ARS_bf[:, 21056:21568],
           ARS_bf[:, 22080:22592]]
    vhpb = ARS_bf[:, 8192:12352]
    sall = ARS[0:16, 10784:12832]
    srec = ARS[0:16, 0:2048]
    wo_full = ARA[:, 0:8192]

    xqT_bf = ARA_bf
    wdT_bf = ARB_bf
    wstg = ARB[:, 11264:14080]
    vtl = [ARB[:, 14080 + b * 512:14592 + b * 512] for b in range(2)]
    wq_u = ARS[:, 0:2816]
    wq_t1 = ARS[:, 2816:5632]
    wq_t1i = ARS_i[:, 2816:5632]
    wq_t2 = ARS[:, 5632:8448]
    wq_rnd = ARS[:, 8448:11264]
    wgT = [ARS_bf[:, 22528 + b * 1024:23552 + b * 1024] for b in range(2)]
    wuT = [ARS_bf[:, 24576 + b * 1024:25600 + b * 1024] for b in range(2)]
    atl = [ARS[:, 13312 + b * 512:13824 + b * 512] for b in range(2)]

    yt = ARS[:, 2048:3072]
    yb = ARS[:, 3072:4096]
    xnst = ARS[:, 4096:6144]
    maskb = ARS_bf[:, 25664:27712]
    mask_i = ARS[:, 0:512]
    iden_i = ARS[:, 512:640]
    nwrow = ARS[0:1, 6176:7200]
    rowt = ARS[0:1, 13856:13984]
    oh = ARS[0:16, 13984:14112]
    ohJ = ARS[0:16, 14112:14240]

    vstg = ARA[:, 0:2816]
    hq_u = ARA[:, 2816:5632]
    hq_t1 = ARA[:, 5632:8448]
    hq_t1i = ARA_i[:, 5632:8448]
    hq_t2 = ARA[:, 8448:11264]
    htile = ARA_bf[:, 22528:25344]
    sdb = ARA[:, 12672:13696]

    xst = A("xst", [128, 2 * D], F32)
    nwb = A("nwb", [128, D], F32)
    brows = A("brows", [128, D], F32)
    identf = A("identf", [128, 128], F32)
    identb = A("identb", [128, 128], BF16)
    onesP = A("onesP", [128, 512], F32)
    r_all = A("r_all", [128, NRT], F32)
    m1_all = A("m1_all", [128, NRT], F32)
    vm_all = A("vm_all", [128, 320], F32)
    stat = A("stat", [128, 20], F32)
    consts = A("consts", [128, 4], F32)
    clipt = A("clipt", [1, 2], F32)
    sig2 = A("sig2", [1, 2], F32)
    gsc = A("gsc", [1, 16], F32)
    ccst = A("ccst", [1, 64], F32)
    scbc = A("scbc", [128, 8], F32)
    swg = A("swg", [128, 4], F32)
    swu = A("swu", [128, 4], F32)
    sgu = A("sgu", [128, 4], F32)
    sdall = A("sdall", [128, 8], F32)
    bqkT = A("bqkT", [128, 24], F32)
    bob = ARS[:, 10272:11296]

    P5 = [nc.alloc_psum_tensor(f"ps{i}", [128, 512], F32) for i in range(4)]
    PAB = [nc.alloc_psum_tensor("psA", [128, 1024], F32),
           nc.alloc_psum_tensor("psB", [128, 1024], F32)]

    sLd = Sem(nc, "sLd")
    sSp = Sem(nc, "sSp")
    sTr = Sem(nc, "sTr")
    sEv = Sem(nc, "sEv")
    sMM = Sem(nc, "sMM")
    sPE = Sem(nc, "sPE")
    sXn = Sem(nc, "sXn")
    sSq = Sem(nc, "sSq")
    sR = Sem(nc, "sR")
    sP = Sem(nc, "sP")
    sMs = Sem(nc, "sMs")
    sGp = Sem(nc, "sGp")
    sOut = Sem(nc, "sOut")
    sLdV = Sem(nc, "sLdV")
    sLdV2 = Sem(nc, "sLdV2")
    sLdW = Sem(nc, "sLdW")
    ALLSEMS = [sLd, sSp, sTr, sEv, sMM, sPE, sXn, sSq, sR, sP, sMs, sOut,
               sLdV, sLdV2, sLdW]

    gates = {}

    def gwait(eng, key):
        if key in gates:
            h, v = gates[key]
            eng.wait_ge(h, v)

    def gset(key, sem, val=None):
        gates[key] = (sem.h, sem.n if val is None else val)

    ck = [0]

    def ckpt():
        ck[0] += 1
        if dbg_n is not None and ck[0] == dbg_n:
            raise StopBuild

    def barrier():
        for s_ in ALLSEMS:
            GP.wait_ge(s_.h, s_.n)
        sGp.m(GP.tensor_copy(stat[0:1, 14:15], stat[0:1, 15:16]))
        for eng in (SY, TEE, VE, SC):
            eng.wait_ge(sGp.h, sGp.n)

    with nc.Block() as block:
      try:
        sLd.d(SY.dma_start(out=nwrow[:, :], in_=nw_in))
        for j, w in enumerate("qkv"):
            sLd.d(SY.dma_start(
                out=bqkT[:, 8 * j:8 * j + 8],
                in_=b_in[w].rearrange("a (c p) -> p (a c)", p=128),
                allow_slow_non_contiguous=True))
        sLd.d(SY.dma_start(out=clipt[:, :], in_=clip_in))
        ld0 = sLd.n

        GP.iota(iden_i[:, :], [[1, 128]], base=0, channel_multiplier=-1,
                allow_small_or_imprecise_dtypes=True)
        GP.iota(mask_i[:, :], [[1, QB]], base=0, channel_multiplier=-1,
                allow_small_or_imprecise_dtypes=True)
        GP.iota(ohJ[:, :], [[-1, 2], [0, 64]], base=0, channel_multiplier=1,
                allow_small_or_imprecise_dtypes=True)
        GP.memset(onesP[:, :], 1.0)
        GP.memset(consts[:, 0:1], -30.0)
        GP.memset(consts[:, 1:2], 0.5)
        GP.memset(srowb[:, :], 0.0)
        vv = vhpb.rearrange("p (t c) -> p t c", c=260)
        GP.memset(vv[:, :, 64:65], 1.0)
        GP.memset(vv[:, :, 129:130], 1.0)
        GP.memset(vv[:, :, 194:195], 0.0)
        GP.memset(vv[:, :, 259:260], 0.0)
        sGp.m(GP.memset(ccst[:, :], 0.0))
        gp0 = sGp.n

        VE.wait_ge(sGp.h, gp0)
        VE.wait_ge(sLd.h, ld0)
        VE.tensor_scalar(out=identf[:, :], in0=iden_i[:, :], scalar1=0.0,
                         scalar2=None, op0=ALU.is_equal)
        VE.tensor_scalar(out=identb[:, :], in0=iden_i[:, :], scalar1=0.0,
                         scalar2=None, op0=ALU.is_equal)
        for j, dd in enumerate((0.0, 128.0, 256.0, 384.0)):
            VE.tensor_scalar(out=maskb[:, j * QB:(j + 1) * QB], in0=mask_i[:, :],
                             scalar1=dd, scalar2=-1e9, op0=ALU.is_lt, op1=ALU.mult)
        init_ve = sMs.m(VE.tensor_copy(stat[:, 15:16], consts[:, 0:1]))

        SC.wait_ge(sLd.h, ld0)
        a_e = sSq.m(SC.activation(out=gsc[0:1, 6:8], in_=clipt[:, :], func=AF.Exp,
                                  scale=-1.0))
        VE.wait_ge(sSq.h, a_e)
        v_e = sMs.m(VE.tensor_scalar(out=gsc[0:1, 9:11], in0=gsc[0:1, 6:8],
                                     scalar1=1.0, scalar2=None, op0=ALU.add))
        SC.wait_ge(sMs.h, v_e)
        a_e2 = sSq.m(SC.activation(out=gsc[0:1, 1:3], in_=gsc[0:1, 9:11],
                                   func=AF.Copy))
        VE.wait_ge(sSq.h, a_e2)
        sSq.m(VE.reciprocal(sig2[:, :], gsc[0:1, 1:3])) if False else None
        sMs.m(VE.reciprocal(sig2[0:1, :], gsc[0:1, 1:3]))

        TEE.wait_ge(sMs.h, init_ve)
        TEE.matmul(PAB[0][:, 0:512], onesP[0:1, 0:128], nwrow[0:1, 0:512],
                   start=True, stop=True)
        sMM.m(TEE.matmul(PAB[0][:, 512:1024], onesP[0:1, 0:128], nwrow[0:1, 512:1024],
                         start=True, stop=True))
        VE.wait_ge(sMM.h, sMM.n)
        sPE.m(VE.tensor_copy(nwb[:, :], PAB[0][:, :]))
        nwb_ready = sPE.n
        gset("PAB0", sPE)

        sall_ld_prev = [None]

        for s in range(NSEQ):
            VE.wait_ge(sMM.h, sMM.n)
            SC.wait_ge(sMs.h, sMs.n)

            bXn, bTr, bEv, bLd, bSq, bMs, bR = (
                sXn.n, sTr.n, sEv.n, sLd.n, sSq.n, sMs.n, sR.n)
            for tt in range(NTT):
                if tt >= 2:
                    SY.wait_ge(sXn.h, bXn + tt - 1)
                gwait(SY, f"xst:{tt % 2}")
                sLd.d(SY.dma_start(
                    out=xst[:, (tt % 2) * D:(tt % 2 + 1) * D],
                    in_=x_in[s, tt * 128:(tt + 1) * 128, :]))
                SC.wait_ge(sLd.h, bLd + 16 * (tt + 1))
                if tt >= 2:
                    SC.wait_ge(sTr.h, bTr + 8 * (tt - 1))
                SC.activation(
                    out=xnst[:, (tt % 2) * D:(tt % 2 + 1) * D],
                    in_=xst[:, (tt % 2) * D:(tt % 2 + 1) * D],
                    func=AF.Square, accum_out=stat[:, 0:1])
                sSq.m(SC.activation(out=stat[:, 16:17], in_=stat[:, 0:1],
                                    func=AF.Copy))
                VE.wait_ge(sSq.h, bSq + tt + 1)
                sMs.m(VE.tensor_scalar(out=stat[:, 1:2], in0=stat[:, 0:1],
                                       scalar1=INV1024, scalar2=1e-6,
                                       op0=ALU.mult, op1=ALU.add))
                SC.wait_ge(sMs.h, bMs + tt + 1)
                SC.activation(out=stat[:, 2:3], in_=stat[:, 1:2], func=AF.Ln)
                sR.m(SC.activation(out=r_all[:, s * NTT + tt: s * NTT + tt + 1],
                                   in_=stat[:, 2:3], func=AF.Exp, scale=-0.5))
                idx = s * NTT + tt
                VE.wait_ge(sPE.h, nwb_ready)
                VE.wait_ge(sR.h, bR + tt + 1)
                if tt >= 2:
                    VE.wait_ge(sTr.h, bTr + 8 * (tt - 1))
                sXn.m(VE.scalar_tensor_tensor(
                    out=xnst[:, (tt % 2) * D:(tt % 2 + 1) * D],
                    in0=xst[:, (tt % 2) * D:(tt % 2 + 1) * D],
                    scalar=r_all[:, idx:idx + 1], in1=nwb[:, :],
                    op0=ALU.mult, op1=ALU.mult))
                gset(f"xst:{tt % 2}", sXn)
                for dc in range(8):
                    j = tt * 8 + dc
                    TEE.wait_ge(sXn.h, bXn + tt + 1)
                    gwait(TEE, f"P5:{j % 2}")
                    mtr = sTr.m(TEE.transpose(
                        P5[j % 2][:, 0:128],
                        xnst[:, (tt % 2) * D + dc * 128:(tt % 2) * D + (dc + 1) * 128],
                        identf[:, :]))
                    VE.wait_ge(sTr.h, mtr)
                    sEv.m(VE.tensor_copy(a1scr[:, dc * 128:(dc + 1) * 128],
                                         P5[j % 2][:, 0:128]))
                    gset(f"P5:{j % 2}", sEv)
                hi3 = xnT_bf.rearrange("p (c z t) -> p c z t", c=8, z=2)[
                    :, :, 0, tt * 128:(tt + 1) * 128]
                lo3 = xnT_bf.rearrange("p (c z t) -> p c z t", c=8, z=2)[
                    :, :, 1, tt * 128:(tt + 1) * 128]
                VE.tensor_copy(hi3, a1scr.rearrange("p (c t) -> p c t", c=8))
                sEv.m(VE.tensor_tensor(
                    out=lo3, in0=a1scr.rearrange("p (c t) -> p c t", c=8),
                    in1=hi3, op=ALU.subtract))
            xnT_ready = sEv.n
            ckpt()
            GP.wait_ge(sTr.h, sTr.n)
            vv2 = vhpb.rearrange("p (t c) -> p t c", c=260)
            GP.memset(vv2[:, :, 64:65], 1.0)
            GP.memset(vv2[:, :, 129:130], 1.0)
            GP.memset(vv2[:, :, 194:195], 0.0)
            sGp.m(GP.memset(vv2[:, :, 259:260], 0.0))
            ones_ready = sGp.n

            for hp in range(8):
                wsplit_done = []
                for wname, dst in (("q", wsq), ("k", wsk), ("v", wsv)):
                    gwait(SY, "wstage")
                    gwait(SY, f"wsl:{wname}")
                    srcw = wT_in[wname][:, hp * 128:(hp + 1) * 128]
                    wld = sLd.d(SY.dma_start(
                        out=wstage.rearrange("p (c o) -> p c o", c=8),
                        in_=srcw.rearrange("(c p) o -> p c o", p=128)))
                    dbf = dst.bitcast(BF16)
                    VE.wait_ge(sLd.h, wld)
                    VE.tensor_copy(dbf[:, 0:1024], wstage[:, :])
                    wm = sMs.m(VE.tensor_tensor(out=dbf[:, 1024:2048],
                                                in0=wstage[:, :],
                                                in1=dbf[:, 0:1024],
                                                op=ALU.subtract))
                    gset("wstage", sMs)
                    wsplit_done.append(wm)
                w_ld = sLd.n

                TEE.wait_ge(sLd.h, w_ld)
                TEE.wait_ge(sMs.h, wsplit_done[2])
                TEE.wait_ge(sEv.h, xnT_ready)

                def xnslice(part, dc, c0, w):
                    off = dc * 4096 + (0 if part == 0 else 2048)
                    return xnT_bf[:, off + c0: off + c0 + w]

                for wi, wap in ((0, wsq), (1, wsk)):
                    wbf = wap.bitcast(BF16)
                    w3h = wbf[:, 0:1024].rearrange("p (c o) -> p c o", c=8)
                    w3l = wbf[:, 1024:2048].rearrange("p (c o) -> p c o", c=8)
                    bcol = bqkT[:, 8 * wi + hp: 8 * wi + hp + 1]
                    for tb in range(4):
                        gwait(TEE, f"P5:{tb % 2}")
                        for wpart, xpart in ((0, 0), (0, 1), (1, 0)):
                            w3 = w3h if wpart == 0 else w3l
                            for dc in range(8):
                                mi = TEE.matmul(
                                    P5[tb % 2][:, :], w3[:, dc, :],
                                    xnslice(xpart, dc, tb * QB, QB),
                                    start=(wpart == 0 and xpart == 0 and dc == 0),
                                    stop=(wpart == 1 and dc == 7))
                        qkm = sMM.m(mi)
                        VE.wait_ge(sMM.h, qkm)
                        VE.tensor_scalar(out=dh_s, in0=P5[tb % 2][:, :],
                                         scalar1=bcol, scalar2=None, op0=ALU.add)
                        pm_ = sPE.m(VE.scalar_tensor_tensor(
                            out=dl_s, in0=P5[tb % 2][:, :], scalar=bcol,
                            in1=dh_s, op0=ALU.add, op1=ALU.subtract))
                        gset(f"P5:{tb % 2}", sPE)
                        qsl2 = slice(tb * QB, (tb + 1) * QB)
                        ds2 = ks2 if wi == 1 else qs2
                        pieces = [(ds2[0], 0, dh_s, 0), (ds2[0], 64, dl_s, 0),
                                  (ds2[1], 0, dh_s, 64), (ds2[1], 64, dl_s, 64)]
                        for dst_, dp, src_, sp in pieces:
                            mcp = VE.tensor_copy(dst_[dp:dp + 64, qsl2],
                                                 src_[sp:sp + 64, :])
                        sPE.m(mcp)
                gset("wsl:q", sMM)
                gset("wsl:k", sMM)
                wvbf = wsv.bitcast(BF16)
                wv3h = wvbf[:, 0:1024].rearrange("p (c o) -> p c o", c=8)
                wv3l = wvbf[:, 1024:2048].rearrange("p (c o) -> p c o", c=8)
                for tt in range(NTT):
                    pb = 2 + tt % 2
                    sl = P5[pb][:, 0:128]
                    gwait(TEE, f"P5:{pb}")
                    for wpart, xpart in ((0, 0), (0, 1), (1, 0)):
                        wv3 = wv3h if wpart == 0 else wv3l
                        for dc in range(8):
                            mi = TEE.matmul(
                                sl, xnslice(xpart, dc, tt * 128, 128),
                                wv3[:, dc, :],
                                start=(wpart == 0 and xpart == 0 and dc == 0),
                                stop=(wpart == 1 and dc == 7))
                    vmk = sMM.m(mi)
                    vb_ = vhpb[:, tt * 260:(tt + 1) * 260]
                    SC.wait_ge(sMM.h, vmk)
                    SC.activation(out=vb_[:, 0:64], in_=P5[pb][:, 0:64],
                                  func=AF.Copy)
                    hm_ = sSq.m(SC.activation(out=vb_[:, 65:129],
                                              in_=P5[pb][:, 64:128],
                                              func=AF.Copy))
                    VE.wait_ge(sSq.h, hm_)
                    VE.tensor_tensor(out=vb_[:, 130:194], in0=P5[pb][:, 0:64],
                                     in1=vb_[:, 0:64], op=ALU.subtract)
                    sPE.m(VE.tensor_tensor(out=vb_[:, 195:259],
                                           in0=P5[pb][:, 64:128],
                                           in1=vb_[:, 65:129], op=ALU.subtract))
                    gset(f"P5:{pb}", sPE)
                gset("wsl:v", sMM)
                qkv_done = sPE.n
                if hp == 0:
                    ckpt()

                TEE.wait_ge(sPE.h, qkv_done)
                if hp == 0:
                    TEE.wait_ge(sGp.h, ones_ready)
                    gwait(SC, "PAB1")
                pairs = [(hl, q) for hl in range(2) for q in range(NQB)]

                def build_sw(i2):
                    hl, q = pairs[i2]
                    ab = (hl * NQB + q) % 2
                    sw = ARS_bf[:, 24640 + ab * 512: 25152 + ab * 512]
                    gwait(VE, f"sw7:{ab}")
                    VE.tensor_copy(sw[0:64, :],
                                   qs2[hl][64:128, q * QB:(q + 1) * QB])
                    return sPE.m(VE.tensor_copy(
                        sw[64:128, :], qs2[hl][0:64, q * QB:(q + 1) * QB])), sw

                m7n, swn = build_sw(0)
                for i2, (hloc, qb) in enumerate(pairs):
                    if True:
                        m7, sw7 = m7n, swn
                        nkb = 4 * qb + 4
                        SB3 = [0, 1, 3]
                        avb = (hloc * NQB + qb) % 2
                        AVP = PAB[0][:, avb * 512:(avb + 1) * 512]
                        qsl = slice(qb * QB, (qb + 1) * QB)

                        def emit_scores(kb):
                            sb = SB3[kb % 3]
                            gwait(TEE, f"P5:{sb}")
                            dofs = kb - 4 * qb
                            ksl = slice(kb * 128, (kb + 1) * 128)
                            TEE.matmul(P5[sb][:, :], ks2[hloc][:, ksl],
                                       qs2[hloc][:, qsl], start=True, stop=False)
                            mi = TEE.matmul(P5[sb][:, :], ks2[hloc][:, ksl],
                                            sw7[:, :], start=False,
                                            stop=(dofs < 0))
                            if dofs >= 0:
                                mi = TEE.matmul(P5[sb][:, :], identb[:, :],
                                                maskb[:, dofs * QB:(dofs + 1) * QB],
                                                start=False, stop=True)
                            return sMM.m(mi)

                        gwait(TEE, f"AVP:{avb}")
                        TEE.wait_ge(sPE.h, m7)
                        smarks = [emit_scores(0)]
                        if nkb > 1:
                            smarks.append(emit_scores(1))
                        for kb in range(nkb):
                            sb = SB3[kb % 3]
                            pp3 = kb % 3
                            ppf = kb % 2
                            SC.wait_ge(sMM.h, smarks[kb])
                            gwait(SC, f"ptl:{pp3}")
                            gwait(SC, f"pfp:{ppf}")
                            pm1 = sP.m(SC.activation(
                                out=ph3[pp3], in_=P5[sb][:, :], func=AF.Exp,
                                scale=0.125, bias=consts[:, 0:1]))
                            pm2 = sP.m(SC.activation(
                                out=PAB[1][:, ppf * 512:(ppf + 1) * 512],
                                in_=P5[sb][:, :], func=AF.Exp, scale=0.125,
                                bias=consts[:, 0:1]))
                            gset(f"P5:{sb}", sP)
                            if kb + 2 < nkb:
                                smarks.append(emit_scores(kb + 2))
                            VE.wait_ge(sP.h, pm2)
                            plm = sXn.m(VE.tensor_tensor(
                                out=pl3[pp3],
                                in0=PAB[1][:, ppf * 512:(ppf + 1) * 512],
                                in1=ph3[pp3], op=ALU.subtract))
                            gset(f"pfp:{ppf}", sXn)
                            kbase = kb * 260 + hloc * 65
                            vh_ = vhpb[:, kbase: kbase + 65]
                            vl_ = vhpb[:, kbase + 130: kbase + 195]
                            TEE.wait_ge(sP.h, pm1)
                            TEE.matmul(AVP[0:65, :], vh_, ph3[pp3],
                                       start=(kb == 0), stop=False)
                            TEE.matmul(AVP[0:65, :], vl_, ph3[pp3],
                                       start=False, stop=False)
                            TEE.wait_ge(sXn.h, plm)
                            sMM.m(TEE.matmul(AVP[0:65, :], vh_, pl3[pp3],
                                             start=False, stop=(kb == nkb - 1)))
                            gset(f"ptl:{pp3}", sMM)
                        av_done = sMM.n
                        gset(f"sw7:{avb}", sMM)
                        if i2 + 1 < len(pairs):
                            m7n, swn = build_sw(i2 + 1)
                        VE.wait_ge(sMM.h, av_done)
                        VE.tensor_copy(
                            attT[hloc * 64:hloc * 64 + 64,
                                 hp * T + qb * QB: hp * T + (qb + 1) * QB],
                            AVP[0:64, :])
                        gwait(VE, "srowb")
                        srm = sPE.m(VE.tensor_copy(srowb[64:65, 0:512],
                                                   AVP[64:65, :]))
                        gset(f"AVP:{avb}", sPE)
                        GP.wait_ge(sPE.h, srm)
                        if s > 0 and sall_ld_prev[0] is not None:
                            GP.wait_ge(sLd.h, sall_ld_prev[0])
                        sSp.d(GP.dma_start(
                            out=sdram[2 * hp + hloc: 2 * hp + hloc + 1,
                                      qb * QB:(qb + 1) * QB],
                            in_=srowb[64:65, 0:512]))
                        gset("srowb", sSp)
                if hp == 0:
                    ckpt()

            ckpt()
            SY.wait_ge(sSp.h, sSp.n)
            sall_ld = sLd.d(SY.dma_start(out=sall, in_=sdram[:, :]))
            sall_ld_prev[0] = sall_ld
            VE.wait_ge(sLd.h, sall_ld)
            VE.wait_ge(sMM.h, sMM.n)
            VE.reciprocal(srec, sall)
            srh = ARS_bf[0:16, 4096:6144]
            srl = ARS_bf[0:16, 6144:8192]
            VE.tensor_copy(srh, srec)
            sMs.m(VE.tensor_tensor(out=srl, in0=srec, in1=srh,
                                   op=ALU.subtract))
            gset("wstage", sMs)
            ohb = ARS_bf[0:16, 27968:28096]
            for hp in range(8):
                gwait(VE, "oh")
                ohm = sMs.m(VE.tensor_scalar(out=ohb[:, :], in0=ohJ[:, :],
                                             scalar1=float(2 * hp), scalar2=None,
                                             op0=ALU.is_equal))
                for qb in range(NQB):
                    TEE.wait_ge(sMs.h, ohm)
                    gwait(TEE, "P5:3")
                    TEE.matmul(P5[3][:, :], ohb[:, :],
                               srh[:, qb * QB:(qb + 1) * QB],
                               start=True, stop=False)
                    bm = sMM.m(TEE.matmul(P5[3][:, :], ohb[:, :],
                                          srl[:, qb * QB:(qb + 1) * QB],
                                          start=False, stop=True))
                    VE.wait_ge(sMM.h, bm)
                    tdiv = ARS[:, 9760:10272]
                    abase = 2 * hp * T + qb * 1024
                    VE.tensor_tensor(
                        out=tdiv,
                        in0=attT[:, hp * T + qb * QB: hp * T + (qb + 1) * QB],
                        in1=P5[3][:, :], op=ALU.mult)
                    VE.tensor_scalar(
                        out=tdiv, in0=tdiv,
                        scalar1=bqkT[:, 16 + hp:17 + hp], scalar2=None,
                        op0=ALU.add)
                    VE.tensor_copy(ARB_bf[:, abase:abase + 512], tdiv)
                    sPE.m(VE.tensor_tensor(
                        out=ARB_bf[:, abase + 512:abase + 1024], in0=tdiv,
                        in1=ARB_bf[:, abase:abase + 512], op=ALU.subtract))
                    gset("P5:3", sPE)
                gset("oh", sMM)
            attT_done = sPE.n
            ckpt()

            SY.wait_ge(sMM.h, sMM.n)
            wo_ld = sLdW.d(SY.dma_start(
                out=wo_full.rearrange("p (c d) -> p c d", c=8),
                in_=wT_in["o"].rearrange("(c p) d -> p c d", p=128)))
            if s == 0:
                bo_ld = sLdW.d(SY.dma_start(out=brows[64:65, :], in_=b_in["o"]))
                TEE.wait_ge(sLdW.h, bo_ld)
            for nb2 in range(2):
                gwait(TEE, f"P5:{nb2}")
                bobm = sMM.m(TEE.matmul(
                    P5[nb2][:, :], onesP[64:65, 0:128],
                    brows[64:65, nb2 * 512:(nb2 + 1) * 512],
                    start=True, stop=True))
            VE.wait_ge(sMM.h, bobm)
            VE.tensor_copy(bob[:, 0:512], P5[0][:, :])
            sPE.m(VE.tensor_copy(bob[:, 512:1024], P5[1][:, :]))
            gset("P5:0", sPE)
            gset("P5:1", sPE)
            woh = ARA_bf[:, 16384:24576]
            wol = ARA_bf[:, 24576:32768]
            VE.wait_ge(sLdW.h, wo_ld)
            VE.tensor_copy(woh, wo_full[:, :])
            wom = sPE.m(VE.tensor_tensor(out=wol, in0=wo_full[:, :], in1=woh,
                                         op=ALU.subtract))
            s1_prev = None
            for tt in range(NTT):
                pb = tt % 2
                gwait(SY, f"xst:{pb}")
                xm2 = sLd.d(SY.dma_start(
                    out=xst[:, pb * D:(pb + 1) * D],
                    in_=x_in[s, tt * 128:(tt + 1) * 128, :]))
                TEE.wait_ge(sPE.h, max(attT_done, wom))
                gwait(TEE, f"PAB{pb}")

                def atts(oc, part):
                    base = (2 * oc * T + (tt // 4) * 1024 + part * 512
                            + (tt % 4) * 128)
                    return ARB_bf[:, base: base + 128]

                for nb in range(2):
                    for wpart, apart in ((0, 0), (0, 1), (1, 0)):
                        wo_ = woh if wpart == 0 else wol
                        for oc in range(8):
                            mi = TEE.matmul(
                                PAB[pb][:, nb * 512:(nb + 1) * 512],
                                atts(oc, apart),
                                wo_[:, oc * 1024 + nb * 512: oc * 1024 + (nb + 1) * 512],
                                start=(wpart == 0 and apart == 0 and oc == 0),
                                stop=(wpart == 1 and oc == 7))
                opm = sMM.m(mi)

                idx = s * NTT + tt
                _md = os.environ.get("K_OPROJ_MODE", "full")
                VE.wait_ge(sMM.h, opm)
                VE.wait_ge(sLd.h, xm2)
                if s1_prev is not None:
                    VE.wait_ge(sSq.h, s1_prev)
                gwait(VE, f"xnst:{pb}")
                VE.scalar_tensor_tensor(
                    out=xnst[:, pb * D:(pb + 1) * D],
                    in0=xst[:, pb * D:(pb + 1) * D],
                    scalar=r_all[:, idx:idx + 1], in1=nwb[:, :],
                    op0=ALU.mult, op1=ALU.mult)
                if _md == "pab":
                    VE.tensor_scalar(out=yt[:, :], in0=PAB[pb][:, :], scalar1=1.0,
                                     scalar2=None, op0=ALU.mult)
                else:
                    VE.tensor_tensor(out=yt[:, :], in0=PAB[pb][:, :],
                                     in1=bob[:, :], op=ALU.add)
                    VE.tensor_tensor(out=yt[:, :], in0=yt[:, :],
                                     in1=xst[:, pb * D:(pb + 1) * D], op=ALU.add)
                pe_m = sPE.m(VE.tensor_tensor(out=yb[:, :], in0=yt[:, :],
                                              in1=xnst[:, pb * D:(pb + 1) * D],
                                              op=ALU.add) if _md != "pab" else
                             VE.tensor_copy(yb[:, :], yt[:, :]))
                gset(f"PAB{pb}", sPE)
                gset(f"xnst:{pb}", sPE)
                SC.wait_ge(sPE.h, pe_m)
                SC.activation(out=yt[:, :], in_=yb[:, :], func=AF.Copy,
                              accum_out=stat[:, 4:5])
                SC.activation(out=yt[:, :], in_=yb[:, :], func=AF.Square,
                              accum_out=stat[:, 5:6])
                s1 = sSq.m(SC.activation(out=stat[:, 16:17], in_=stat[:, 4:5],
                                         func=AF.Copy))
                s1_prev = s1
                VE.wait_ge(sSq.h, s1)
                VE.tensor_scalar(out=stat[:, 6:7], in0=stat[:, 4:5],
                                 scalar1=INV1024, scalar2=None, op0=ALU.mult)
                vm1 = sMs.m(VE.tensor_scalar(out=stat[:, 7:8], in0=stat[:, 5:6],
                                             scalar1=INV1024, scalar2=None,
                                             op0=ALU.mult))
                SC.wait_ge(sMs.h, vm1)
                am1 = sSq.m(SC.activation(out=stat[:, 8:9], in_=stat[:, 6:7],
                                          func=AF.Square))
                VE.wait_ge(sSq.h, am1)
                v1 = sMs.m(VE.scalar_tensor_tensor(
                    out=stat[:, 10:11], in0=stat[:, 7:8], scalar=1e-5,
                    in1=stat[:, 8:9], op0=ALU.add, op1=ALU.subtract))
                SC.wait_ge(sMs.h, v1)
                SC.activation(out=stat[:, 11:12], in_=stat[:, 10:11], func=AF.Ln)
                r1 = sR.m(SC.activation(out=stat[:, 12:13], in_=stat[:, 11:12],
                                        func=AF.Exp, scale=-0.5))
                VE.wait_ge(sR.h, r1)
                if dbg_n is not None:
                    for ci, sc_ in enumerate((6, 12, 4, 5, 7, 8, 9, 10, 11)):
                        VE.tensor_copy(vm_all[:, 32 * ci + idx:32 * ci + idx + 1],
                                       stat[:, sc_:sc_ + 1])
                if _md in ("pab", "noln"):
                    VE.tensor_copy(xst[:, pb * D:(pb + 1) * D], yb[:, :])
                else:
                    VE.tensor_scalar(out=xst[:, pb * D:(pb + 1) * D], in0=yb[:, :],
                                     scalar1=stat[:, 6:7], scalar2=stat[:, 12:13],
                                     op0=ALU.subtract, op1=ALU.mult)
                x1m = sMs.m(VE.tensor_reduce(
                    out=m1_all[:, idx:idx + 1], in_=xst[:, pb * D:(pb + 1) * D],
                    axis=AX.X, op=ALU.max, apply_absolute_value=True))
                GP.wait_ge(sMs.h, x1m)
                sSp.d(GP.dma_start(
                    out=x1dram[idx * 128:(idx + 1) * 128, :],
                    in_=xst[:, pb * D:(pb + 1) * D]))
                gset(f"xst:{pb}", sSp)

            ckpt()

        if stage == "ln1":
            GP.wait_ge(sSp.h, sSp.n)
            for i4 in range(4):
                sOut.d(GP.dma_start(
                    out=out[i4 // 2, (i4 % 2) * 1024:(i4 % 2 + 1) * 1024, :],
                    in_=x1dram[i4 * 1024:(i4 + 1) * 1024, :]))
            GP.wait_ge(sOut.h, sOut.n)
            return nc

        barrier()
        g1m = sMs.m(VE.tensor_reduce(out=stat[:, 3:4], in_=m1_all[:, 0:NRT],
                                     axis=AX.X, op=ALU.max))
        GP.wait_ge(sMs.h, g1m)
        sSp.d(GP.dma_start(out=m128d, in_=stat[:, 3:4]))
        SY.wait_ge(sSp.h, sSp.n)
        sLd.d(SY.dma_start(out=rowt[:, :], in_=m128d))
        VE.wait_ge(sLd.h, sLd.n)
        gm1 = sMs.m(VE.tensor_reduce(out=gsc[0:1, 0:1], in_=rowt[:, :],
                                     axis=AX.X, op=ALU.max))
        GP.wait_ge(sMs.h, gm1)
        GP.tensor_copy(ccst[0:1, 0:1], gsc[0:1, 0:1])
        sSp.d(GP.dma_start(out=cc_in[0].ap(), in_=ccst[:, :]))
        GP.wait_ge(sSp.h, sSp.n)
        GP.collective_compute(
            "AllReduce", ALU.max, ins=[cc_in[0].ap().opt()],
            outs=[cc_out[0].ap().opt()],
            replica_groups=[[0, 1, 2, 3, 4, 5, 6, 7]]).then_inc(sGp.h, 1)
        sGp.n += 1
        GP.wait_ge(sGp.h, sGp.n)
        sSp.d(GP.dma_start(out=rowt[0:1, 0:64], in_=cc_out[0].ap()))
        SC.wait_ge(sSp.h, sSp.n)
        SC.activation(out=gsc[0:1, 5:6], in_=rowt[0:1, 0:1], func=AF.Copy,
                      scale=sig2[0:1, 0:1])
        a1g = sSq.m(SC.activation(out=gsc[0:1, 3:4], in_=gsc[0:1, 5:6],
                                  func=AF.Copy, scale=INV127))
        VE.wait_ge(sSq.h, a1g)
        v1g = sMs.m(VE.reciprocal(gsc[0:1, 8:9], gsc[0:1, 3:4]))
        SC.wait_ge(sMs.h, v1g)
        scm = sSq.m(SC.activation(out=gsc[0:1, 4:5], in_=gsc[0:1, 8:9],
                                  func=AF.Copy))
        TEE.wait_ge(sSq.h, scm)
        bcm = sMM.m(TEE.matmul(P5[0][:, 0:2], onesP[0:1, 0:128], gsc[0:1, 3:5],
                               start=True, stop=True))
        VE.wait_ge(sMM.h, bcm)
        sPE.m(VE.tensor_copy(scbc[:, 0:2], P5[0][:, 0:2]))
        scx_ready = sPE.n
        ckpt()

        barrier()
        gates.pop("P5:0", None); gates.pop("P5:1", None)
        gates.pop("xst:0", None); gates.pop("xst:1", None)
        for t32 in range(NRT):
            pb = t32 % 2
            gwait(SY, f"xst:{pb}")
            xm = sLd.d(SY.dma_start(
                out=xst[:, pb * D:(pb + 1) * D],
                in_=x1dram[t32 * 128:(t32 + 1) * 128, :]))
            VE.wait_ge(sLd.h, xm)
            VE.wait_ge(sPE.h, scx_ready)
            gwait(VE, "wq")
            VE.tensor_scalar(out=wq_u[:, 0:1024], in0=xst[:, pb * D:(pb + 1) * D],
                             scalar1=scbc[:, 1:2], scalar2=0.5,
                             op0=ALU.mult, op1=ALU.add)
            gset(f"xst:{pb}", sMs, sMs.n + 1)
            VE.tensor_copy(wq_t1i[:, 0:1024], wq_u[:, 0:1024])
            VE.tensor_copy(wq_t2[:, 0:1024], wq_t1i[:, 0:1024])
            VE.tensor_tensor(out=wq_t1[:, 0:1024], in0=wq_t2[:, 0:1024],
                             in1=wq_u[:, 0:1024], op=ALU.is_gt)
            VE.tensor_tensor(out=wq_rnd[:, 0:1024], in0=wq_t2[:, 0:1024],
                             in1=wq_t1[:, 0:1024], op=ALU.subtract)
            qm = sMs.m(VE.tensor_scalar(out=wq_rnd[:, 0:1024], in0=wq_rnd[:, 0:1024],
                                        scalar1=-128.0, scalar2=127.0,
                                        op0=ALU.max, op1=ALU.min))
            for dc in range(8):
                j = t32 * 8 + dc
                TEE.wait_ge(sMs.h, qm)
                gwait(TEE, f"P5:{j % 2}")
                mtr = sTr.m(TEE.transpose(P5[j % 2][:, 0:128],
                                          wq_rnd[:, dc * 128:(dc + 1) * 128],
                                          identf[:, :]))
                VE.wait_ge(sTr.h, mtr)
                em = sEv.m(VE.tensor_copy(
                    xqT_bf[:, dc * R + t32 * 128: dc * R + (t32 + 1) * 128],
                    P5[j % 2][:, 0:128]))
                gset(f"P5:{j % 2}", sEv)
                if dc == 7:
                    gset("wq", sEv, em)
        xqT_ready = sEv.n
        ckpt()

        def prepB(j):
            bb = j % 2
            gwait(SY, "wstg")
            sLd.d(SY.dma_start(out=wstg[:, 0:1024],
                               in_=wg_in[j * 128:(j + 1) * 128, :]))
            l2 = sLd.d(SY.dma_start(out=wstg[:, 1024:2048],
                                    in_=wu_in[j * 128:(j + 1) * 128, :]))
            VE.wait_ge(sLd.h, l2)
            VE.tensor_reduce(out=stat[:, 17:18], in_=wstg[:, 0:1024],
                             axis=AX.X, op=ALU.max, apply_absolute_value=True)
            vr = sMs.m(VE.tensor_reduce(
                out=stat[:, 18:19], in_=wstg[:, 1024:2048],
                axis=AX.X, op=ALU.max, apply_absolute_value=True))
            SC.wait_ge(sMs.h, vr)
            ar = sSq.m(SC.activation(out=swg[:, 2 * bb:2 * bb + 2],
                                     in_=stat[:, 17:19], func=AF.Copy,
                                     scale=INV127))
            VE.wait_ge(sSq.h, ar)
            vr2 = sMs.m(VE.reciprocal(stat[:, 17:19], swg[:, 2 * bb:2 * bb + 2]))
            SC.wait_ge(sMs.h, vr2)
            ar2 = sSq.m(SC.activation(out=swu[:, 2 * bb:2 * bb + 2],
                                      in_=stat[:, 17:19], func=AF.Copy))
            VE.wait_ge(sSq.h, ar2)
            gwait(VE, "wq")
            qms = []
            for wi in range(2):
                sl = slice(wi * 1024, (wi + 1) * 1024)
                VE.tensor_scalar(out=wq_u[:, 0:1024], in0=wstg[:, sl],
                                 scalar1=swu[:, 2 * bb + wi:2 * bb + wi + 1],
                                 scalar2=0.5, op0=ALU.mult, op1=ALU.add)
                VE.tensor_copy(wq_t1i[:, 0:1024], wq_u[:, 0:1024])
                VE.tensor_copy(wq_t2[:, 0:1024], wq_t1i[:, 0:1024])
                VE.tensor_tensor(out=wq_t1[:, 0:1024], in0=wq_t2[:, 0:1024],
                                 in1=wq_u[:, 0:1024], op=ALU.is_gt)
                qms.append(sMs.m(VE.tensor_tensor(
                    out=wq_rnd[:, wi * 1024:(wi + 1) * 1024],
                    in0=wq_t2[:, 0:1024], in1=wq_t1[:, 0:1024], op=ALU.subtract)))
            gset("wstg", sMs)
            return ar, qms

        def prepB_T(j, qms):
            bb = j % 2
            for wi, dstT in ((0, wgT), (1, wuT)):
                for dc in range(8):
                    TEE.wait_ge(sMs.h, qms[wi])
                    gwait(TEE, f"P5:{dc % 2}")
                    mtr = sTr.m(TEE.transpose(
                        P5[dc % 2][:, 0:128],
                        wq_rnd[:, wi * 1024 + dc * 128: wi * 1024 + (dc + 1) * 128],
                        identf[:, :]))
                    VE.wait_ge(sTr.h, mtr)
                    gwait(VE, f"wgT:{bb}")
                    sEv.m(VE.tensor_copy(
                        dstT[bb][:, dc * 128:(dc + 1) * 128], P5[dc % 2][:, 0:128]))
                    gset(f"P5:{dc % 2}", sEv)
            gset("wq", sEv)
            return sEv.n

        ar_p, qms_p = prepB(0)
        wready_p = prepB_T(0, qms_p)
        for ib in range(NIC):
            bb = ib % 2
            armark, wready = ar_p, wready_p
            VE.wait_ge(sPE.h, scx_ready)
            VE.wait_ge(sSq.h, armark)
            VE.tensor_tensor(out=sgu[:, 2 * bb:2 * bb + 1],
                             in0=swg[:, 2 * bb:2 * bb + 1], in1=scbc[:, 0:1],
                             op=ALU.mult)
            sgm = sMs.m(VE.tensor_tensor(out=sgu[:, 2 * bb + 1:2 * bb + 2],
                                         in0=swg[:, 2 * bb + 1:2 * bb + 2],
                                         in1=scbc[:, 0:1], op=ALU.mult))
            if ib + 1 < NIC:
                ar_p, qms_p = prepB(ib + 1)
            for tb in range(8):
                pp = tb % 2
                TEE.wait_ge(sEv.h, max(wready, xqT_ready))
                gwait(TEE, f"PAB{pp}")
                for half, wT_ in ((0, wgT), (1, wuT)):
                    for dc in range(8):
                        mi = TEE.matmul(
                            PAB[pp][:, half * 512:(half + 1) * 512],
                            wT_[bb][:, dc * 128:(dc + 1) * 128],
                            xqT_bf[:, dc * R + tb * 512: dc * R + (tb + 1) * 512],
                            start=(dc == 0), stop=(dc == 7))
                gm_ = sMM.m(mi)
                gset(f"wgT:{bb}", sMM)
                SC.wait_ge(sMM.h, gm_)
                SC.wait_ge(sMs.h, sgm)
                gwait(SC, f"atl:{pp}")
                am = sSq.m(SC.activation(out=atl[pp], in_=PAB[pp][:, 0:512],
                                         func=AF.Silu,
                                         scale=sgu[:, 2 * bb:2 * bb + 1]))
                VE.wait_ge(sSq.h, am)
                gwait(VE, f"vtl:{pp}")
                VE.scalar_tensor_tensor(out=vtl[pp], in0=PAB[pp][:, 512:1024],
                                        scalar=sgu[:, 2 * bb + 1:2 * bb + 2],
                                        in1=atl[pp], op0=ALU.mult, op1=ALU.mult)
                gset(f"atl:{pp}", sMs, sMs.n + 1)
                vm = sMs.m(VE.tensor_reduce(
                    out=vm_all[:, ib * 8 + tb: ib * 8 + tb + 1], in_=vtl[pp],
                    axis=AX.X, op=ALU.max, apply_absolute_value=True))
                gates[f"PAB{pp}"] = (sMs.h, vm)
                GP.wait_ge(sMs.h, vm)
                sSp.d(GP.dma_start(
                    out=vdram[ib * 128:(ib + 1) * 128, tb * 512:(tb + 1) * 512],
                    in_=vtl[pp]))
                gset(f"vtl:{pp}", sSp)
            if ib + 1 < NIC:
                wready_p = prepB_T(ib + 1, qms_p)

        ckpt()
        if stage == "ffnv":
            GP.wait_ge(sSp.h, sSp.n)
            for i4 in range(4):
                sOut.d(GP.dma_start(
                    out=out[i4 // 2, (i4 % 2) * 1024:(i4 % 2 + 1) * 1024, :],
                    in_=x1dram[i4 * 1024:(i4 + 1) * 1024, :]))
            for ic in range(NIC):
                sOut.d(GP.dma_start(out=dbgv[ic * 128:(ic + 1) * 128, :],
                                    in_=vdram[ic * 128:(ic + 1) * 128, :]))
            GP.wait_ge(sOut.h, sOut.n)
            return nc

        barrier()
        g2m = sMs.m(VE.tensor_reduce(out=stat[:, 3:4], in_=vm_all[:, 0:NIC * 8],
                                     axis=AX.X, op=ALU.max))
        GP.wait_ge(sMs.h, g2m)
        sSp.d(GP.dma_start(out=m128d, in_=stat[:, 3:4]))
        SY.wait_ge(sSp.h, sSp.n)
        sLd.d(SY.dma_start(out=rowt[:, :], in_=m128d))
        VE.wait_ge(sLd.h, sLd.n)
        gm2 = sMs.m(VE.tensor_reduce(out=gsc[0:1, 8:9], in_=rowt[:, :],
                                     axis=AX.X, op=ALU.max))
        GP.wait_ge(sMs.h, gm2)
        GP.tensor_copy(ccst[0:1, 0:1], gsc[0:1, 8:9])
        sSp.d(GP.dma_start(out=cc_in[1].ap(), in_=ccst[:, :]))
        GP.wait_ge(sSp.h, sSp.n)
        GP.collective_compute(
            "AllReduce", ALU.max, ins=[cc_in[1].ap().opt()],
            outs=[cc_out[1].ap().opt()],
            replica_groups=[[0, 1, 2, 3, 4, 5, 6, 7]]).then_inc(sGp.h, 1)
        sGp.n += 1
        GP.wait_ge(sGp.h, sGp.n)
        sSp.d(GP.dma_start(out=rowt[0:1, 0:64], in_=cc_out[1].ap()))
        SC.wait_ge(sSp.h, sSp.n)
        SC.activation(out=gsc[0:1, 13:14], in_=rowt[0:1, 0:1], func=AF.Copy,
                      scale=sig2[0:1, 1:2])
        a2g = sSq.m(SC.activation(out=gsc[0:1, 11:12], in_=gsc[0:1, 13:14],
                                  func=AF.Copy, scale=INV127))
        VE.wait_ge(sSq.h, a2g)
        v2g = sMs.m(VE.reciprocal(gsc[0:1, 14:15], gsc[0:1, 11:12]))
        SC.wait_ge(sMs.h, v2g)
        scm2 = sSq.m(SC.activation(out=gsc[0:1, 12:13], in_=gsc[0:1, 14:15],
                                   func=AF.Copy))
        TEE.wait_ge(sSq.h, scm2)
        bcm2 = sMM.m(TEE.matmul(P5[0][:, 0:2], onesP[0:1, 0:128], gsc[0:1, 11:13],
                                start=True, stop=True))
        VE.wait_ge(sMM.h, bcm2)
        sPE.m(VE.tensor_copy(scbc[:, 2:4], P5[0][:, 0:2]))
        sch_ready = sPE.n
        ckpt()

        barrier()
        gates.pop("wstg", None); gates.pop("wq", None)
        gates.pop("P5:0", None); gates.pop("P5:1", None)
        gates.pop("xst:0", None); gates.pop("xst:1", None)
        for ds in range(8):
            gwait(SY, "wstg")
            l1 = sLd.d(SY.dma_start(out=wstg[:, :],
                                    in_=wd_in[ds * 128:(ds + 1) * 128, :]))
            VE.wait_ge(sLd.h, l1)
            gwait(VE, "wq")
            vr = sMs.m(VE.tensor_reduce(out=stat[:, 17:18], in_=wstg[:, :],
                                        axis=AX.X, op=ALU.max,
                                        apply_absolute_value=True))
            SC.wait_ge(sMs.h, vr)
            ar = sSq.m(SC.activation(out=sdall[:, ds:ds + 1], in_=stat[:, 17:18],
                                     func=AF.Copy, scale=INV127))
            VE.wait_ge(sSq.h, ar)
            vr2 = sMs.m(VE.reciprocal(stat[:, 18:19], sdall[:, ds:ds + 1]))
            SC.wait_ge(sMs.h, vr2)
            ar2 = sSq.m(SC.activation(out=stat[:, 13:14], in_=stat[:, 18:19],
                                      func=AF.Copy))
            VE.wait_ge(sSq.h, ar2)
            VE.tensor_scalar(out=wq_u[:, :], in0=wstg[:, :],
                             scalar1=stat[:, 13:14], scalar2=0.5,
                             op0=ALU.mult, op1=ALU.add)
            gset("wstg", sMs, sMs.n + 1)
            VE.tensor_copy(wq_t1i[:, :], wq_u[:, :])
            VE.tensor_copy(wq_t2[:, :], wq_t1i[:, :])
            VE.tensor_tensor(out=wq_t1[:, :], in0=wq_t2[:, :], in1=wq_u[:, :],
                             op=ALU.is_gt)
            qm = sMs.m(VE.tensor_tensor(out=wq_rnd[:, :], in0=wq_t2[:, :],
                                        in1=wq_t1[:, :], op=ALU.subtract))
            for ic in range(NIC):
                TEE.wait_ge(sMs.h, qm)
                gwait(TEE, f"P5:{ic % 2}")
                mtr = sTr.m(TEE.transpose(P5[ic % 2][:, 0:128],
                                          wq_rnd[:, ic * 128:(ic + 1) * 128],
                                          identf[:, :]))
                VE.wait_ge(sTr.h, mtr)
                sEv.m(VE.tensor_copy(
                    wdT_bf[:, ic * 1024 + ds * 128: ic * 1024 + (ds + 1) * 128],
                    P5[ic % 2][:, 0:128]))
                gset(f"P5:{ic % 2}", sEv)
            gset("wq", sEv)
        wdT_ready = sEv.n
        GP.wait_ge(sEv.h, wdT_ready)
        sSp.d(GP.dma_start(out=rsdd.rearrange("a (c p) -> (a p) c", p=128),
                           in_=sdall[:, 0:8], allow_slow_non_contiguous=True))
        SY.wait_ge(sSp.h, sSp.n)
        rl = sLd.d(SY.dma_start(out=nwrow[:, :], in_=rsdd))
        TEE.wait_ge(sLd.h, rl)
        gwait(TEE, "P5:0")
        TEE.matmul(P5[0][:, :], onesP[0:1, 0:128], nwrow[0:1, 0:512],
                   start=True, stop=True)
        gwait(TEE, "P5:1")
        sdm = sMM.m(TEE.matmul(P5[1][:, :], onesP[0:1, 0:128], nwrow[0:1, 512:1024],
                               start=True, stop=True))
        VE.wait_ge(sMM.h, sdm)
        VE.tensor_copy(sdb[:, 0:512], P5[0][:, :])
        sdbm = sPE.m(VE.tensor_copy(sdb[:, 512:1024], P5[1][:, :]))
        gset("P5:0", sPE)
        gset("P5:1", sPE)
        ckpt()

        vstgB = [vstg, ARS[:, 8448:11264]]
        htileB = [htile, ARA_bf[:, 27392:30208]]
        sLdVB = [sLdV, sLdV2]

        def ldV(i):
            pv = i % 2
            gwait(SY, f"vstg:{pv}")
            if i < 2:
                gwait(SY, "wq")
                gwait(SY, "wstg")
            return sLdVB[pv].d(SY.dma_start(
                out=vstgB[pv].rearrange("p (c t) -> p c t", c=NIC),
                in_=vdram[:, i * 128:(i + 1) * 128]
                .rearrange("(c p) t -> p c t", p=128)))

        def ldX(i):
            pv = i % 2
            gwait(SY, f"xst:{pv}")
            return sLd.d(SY.dma_start(
                out=xst[:, pv * D:(pv + 1) * D],
                in_=x1dram[i * 128:(i + 1) * 128, :]))

        def quantC(i, vm_):
            pv = i % 2
            VE.wait_ge(sLdVB[pv].h, vm_)
            VE.wait_ge(sPE.h, sch_ready)
            gwait(VE, f"htile:{pv}")
            VE.tensor_scalar(out=hq_u[:, :], in0=vstgB[pv], scalar1=scbc[:, 3:4],
                             scalar2=0.5, op0=ALU.mult, op1=ALU.add)
            gset(f"vstg:{pv}", sMs, sMs.n + 1)
            VE.tensor_copy(hq_t1i[:, :], hq_u[:, :])
            VE.tensor_copy(hq_t2[:, :], hq_t1i[:, :])
            VE.tensor_tensor(out=hq_t1[:, :], in0=hq_t2[:, :], in1=hq_u[:, :],
                             op=ALU.is_gt)
            VE.tensor_tensor(out=hq_u[:, :], in0=hq_t2[:, :], in1=hq_t1[:, :],
                             op=ALU.subtract)
            return sMs.m(VE.tensor_scalar(out=htileB[pv], in0=hq_u[:, :],
                                          scalar1=-128.0, scalar2=127.0,
                                          op0=ALU.max, op1=ALU.min))

        s2_prev = None
        vlds = {0: ldV(0), 1: ldV(1)}
        xms = {0: ldX(0), 1: ldX(1)}
        hm_p = quantC(0, vlds[0])
        for t32 in range(NRT):
            pb = t32 % 2
            s_, tt = t32 // NTT, t32 % NTT
            hm, xm_ = hm_p, xms[t32]
            if t32 + 1 < NRT:
                hm_p = quantC(t32 + 1, vlds[t32 + 1])
            TEE.wait_ge(sMs.h, hm)
            TEE.wait_ge(sEv.h, wdT_ready)
            TEE.wait_ge(sPE.h, sdbm)
            gwait(TEE, f"PAB{pb}")
            for nb in range(2):
                for ic in range(NIC):
                    mi = TEE.matmul(
                        PAB[pb][:, nb * 512:(nb + 1) * 512],
                        htileB[pb][:, ic * 128:(ic + 1) * 128],
                        wdT_bf[:, ic * 1024 + nb * 512: ic * 1024 + (nb + 1) * 512],
                        start=(ic == 0), stop=(ic == NIC - 1))
            dm = sMM.m(mi)
            gset(f"htile:{pb}", sMM)
            _mc = os.environ.get("K_C_MODE", "full")
            VE.wait_ge(sMM.h, dm)
            VE.wait_ge(sLd.h, xm_)
            if s2_prev is not None:
                VE.wait_ge(sSq.h, s2_prev)
            VE.scalar_tensor_tensor(out=yb[:, :], in0=PAB[pb][:, :],
                                    scalar=scbc[:, 2:3], in1=sdb[:, :],
                                    op0=ALU.mult, op1=ALU.mult)
            gset(f"PAB{pb}", sMs, sMs.n + 1)
            pe2 = sMs.m(VE.tensor_tensor(out=yt[:, :], in0=yb[:, :],
                                          in1=xst[:, pb * D:(pb + 1) * D],
                                          op=ALU.add) if _mc == "full" else
                        VE.tensor_copy(yt[:, :], yb[:, :]))
            SC.wait_ge(sMs.h, pe2)
            SC.activation(out=yb[:, :], in_=yt[:, :], func=AF.Copy,
                          accum_out=stat[:, 4:5])
            SC.activation(out=yb[:, :], in_=yt[:, :], func=AF.Square,
                          accum_out=stat[:, 5:6])
            s2 = sSq.m(SC.activation(out=stat[:, 16:17], in_=stat[:, 4:5],
                                     func=AF.Copy))
            s2_prev = s2
            VE.wait_ge(sSq.h, s2)
            VE.tensor_scalar(out=stat[:, 6:7], in0=stat[:, 4:5],
                             scalar1=INV1024, scalar2=None, op0=ALU.mult)
            vm1c = sMs.m(VE.tensor_scalar(out=stat[:, 7:8], in0=stat[:, 5:6],
                                          scalar1=INV1024, scalar2=None,
                                          op0=ALU.mult))
            SC.wait_ge(sMs.h, vm1c)
            am1c = sSq.m(SC.activation(out=stat[:, 8:9], in_=stat[:, 6:7],
                                       func=AF.Square))
            VE.wait_ge(sSq.h, am1c)
            v2 = sMs.m(VE.scalar_tensor_tensor(
                out=stat[:, 10:11], in0=stat[:, 7:8], scalar=1e-5,
                in1=stat[:, 8:9], op0=ALU.add, op1=ALU.subtract))
            SC.wait_ge(sMs.h, v2)
            SC.activation(out=stat[:, 11:12], in_=stat[:, 10:11], func=AF.Ln)
            r2 = sR.m(SC.activation(out=stat[:, 12:13], in_=stat[:, 11:12],
                                    func=AF.Exp, scale=-0.5))
            VE.wait_ge(sR.h, r2)
            om = sMs.m(VE.tensor_scalar(out=xst[:, pb * D:(pb + 1) * D],
                                        in0=yt[:, :],
                                        scalar1=stat[:, 6:7], scalar2=stat[:, 12:13],
                                        op0=ALU.subtract, op1=ALU.mult)
                       if _mc == "full" else
                       VE.tensor_copy(xst[:, pb * D:(pb + 1) * D], yt[:, :]))
            GP.wait_ge(sMs.h, om)
            sOut.d(GP.dma_start(out=out[s_, tt * 128:(tt + 1) * 128, :],
                                in_=xst[:, pb * D:(pb + 1) * D]))
            gset(f"xst:{pb}", sOut)
            if t32 + 2 < NRT:
                vlds[t32 + 2] = ldV(t32 + 2)
                xms[t32 + 2] = ldX(t32 + 2)
            if t32 == int(os.environ.get("K_T32", "0")):
                ckpt()

        if stage == "full2":
            for ic in range(NIC):
                sOut.d(GP.dma_start(out=dbgv[ic * 128:(ic + 1) * 128, :],
                                    in_=vdram[ic * 128:(ic + 1) * 128, :]))
            for i4 in range(4):
                sOut.d(GP.dma_start(out=dbgX2[i4 * 1024:(i4 + 1) * 1024, :],
                                    in_=x1dram[i4 * 1024:(i4 + 1) * 1024, :]))
        GP.wait_ge(sOut.h, sOut.n)
      except StopBuild:
        for s_ in ALLSEMS:
            GP.wait_ge(s_.h, s_.n)
        sOut.d(GP.dma_start(out=dbgA, in_=ARA[:, :]))
        sOut.d(GP.dma_start(out=dbgB, in_=ARB[:, :]))
        sOut.d(GP.dma_start(out=dbgS, in_=ARS[:, :]))
        sOut.d(GP.dma_start(out=dbgX, in_=x1dram))
        sOut.d(GP.dma_start(out=dbgSd, in_=sdram))
        sOut.d(GP.dma_start(out=dbgM, in_=vm_all[:, 0:320]))
        sOut.d(GP.dma_start(out=dbgW[:, 0:8], in_=scbc[:, :]))
        sOut.d(GP.dma_start(out=dbgW[:, 8:16], in_=sdall[:, :]))
        sOut.d(GP.dma_start(out=dbgW[:, 16:36], in_=stat[:, :]))
        sOut.d(GP.dma_start(out=dbgW[0:1, 36:48], in_=gsc[0:1, 0:12]))
        GP.wait_ge(sOut.h, sOut.n)

    return nc


def kernel(**inputs):
    x = np.ascontiguousarray(np.asarray(inputs["L0_attn_x"], dtype=np.float32))
    nw = np.asarray(inputs["L0_attn_norm_w"], dtype=np.float32).reshape(1, D)
    Ws = {w: np.ascontiguousarray(
        np.asarray(inputs[f"L0_attn_W{w}"], dtype=np.float32).T) for w in "qkvo"}
    bs = {w: np.asarray(inputs[f"L0_attn_b{w}"], dtype=np.float32).reshape(1, D)
          for w in "qkvo"}
    clips = np.array([[np.asarray(inputs["L0_ffn_clip_x"], np.float32).ravel()[0],
                       np.asarray(inputs["L0_ffn_clip_down"], np.float32).ravel()[0]]],
                     dtype=np.float32)
    wg = np.ascontiguousarray(np.asarray(inputs["L0_ffn_w_gate"], np.float32))
    wu = np.ascontiguousarray(np.asarray(inputs["L0_ffn_w_up"], np.float32))
    wd = np.ascontiguousarray(np.asarray(inputs["L0_ffn_w_down"], np.float32))

    nc = build_program("full")
    in_maps = []
    for c in range(8):
        m = {"x": np.ascontiguousarray(x[2 * c:2 * c + 2]),
             "norm_w": nw, "clips": clips,
             "w_gate": wg, "w_up": wu, "w_down": wd}
        for w in "qkvo":
            m[f"w{w}T"] = Ws[w]
            m[f"b{w}"] = bs[w]
        in_maps.append(m)
    res = run_bass_kernel_spmd(nc, in_maps, core_ids=list(range(8)))
    return np.concatenate([np.asarray(r["out"]) for r in res.results], axis=0)


# version 18
# speedup vs baseline: 1.0498x; 1.0005x over previous
import os
import numpy as np
import ml_dtypes
import concourse.bass as bass
import concourse.mybir as mybir
from concourse.bass_utils import run_bass_kernel_spmd

F32 = mybir.dt.float32
BF16 = mybir.dt.bfloat16
I32 = mybir.dt.int32
AF = mybir.ActivationFunctionType
ALU = mybir.AluOpType
AX = mybir.AxisListType

T = 2048
NSEQ = 2
D = 1024
II = 2816
NIC = II // 128
NTT = T // 128
R = NSEQ * T
NRT = R // 128
QB = 512
NQB = T // QB
INV127 = float(np.float32(1.0) / np.float32(127.0))
INV1024 = float(np.float32(1.0) / np.float32(1024.0))


class Sem:
    def __init__(self, nc, name):
        self.h = nc.alloc_semaphore(name)
        self.n = 0

    def d(self, instr):
        instr.then_inc(self.h, 16)
        self.n += 16
        return self.n

    def m(self, instr):
        instr.then_inc(self.h, 1)
        self.n += 1
        return self.n


class StopBuild(Exception):
    pass


def build_program(stage="full"):
    nc = bass.Bass("TRN2", target_bir_lowering=False,
                   detect_race_conditions=False)
    SY, TEE, VE, SC, GP = nc.sync, nc.tensor, nc.vector, nc.scalar, nc.gpsimd

    x_in = nc.dram_tensor("x", [NSEQ, T, D], F32, kind="ExternalInput").ap()
    nw_in = nc.dram_tensor("norm_w", [1, D], F32, kind="ExternalInput").ap()
    wT_in = {w: nc.dram_tensor(f"w{w}T", [D, D], F32, kind="ExternalInput").ap()
             for w in "qkvo"}
    b_in = {w: nc.dram_tensor(f"b{w}", [1, D], F32, kind="ExternalInput").ap()
            for w in "qkvo"}
    clip_in = nc.dram_tensor("clips", [1, 2], F32, kind="ExternalInput").ap()
    wg_in = nc.dram_tensor("w_gate", [II, D], F32, kind="ExternalInput").ap()
    wu_in = nc.dram_tensor("w_up", [II, D], F32, kind="ExternalInput").ap()
    wd_in = nc.dram_tensor("w_down", [D, II], F32, kind="ExternalInput").ap()
    out = nc.dram_tensor("out", [NSEQ, T, D], F32, kind="ExternalOutput").ap()
    if stage in ("ln1", "ffnv", "full2"):
        dbgv = nc.dram_tensor("dbgv", [II, R], F32, kind="ExternalOutput").ap()
    if stage == "full2":
        dbgX2 = nc.dram_tensor("dbgX2", [R, D], F32, kind="ExternalOutput").ap()
    dbg_n = int(stage[4:]) if isinstance(stage, str) and stage.startswith("dbg:") else None
    if dbg_n is not None:
        dbgA = nc.dram_tensor("dbgA", [128, 16384], F32, kind="ExternalOutput").ap()
        dbgB = nc.dram_tensor("dbgB", [128, 16384], F32, kind="ExternalOutput").ap()
        dbgS = nc.dram_tensor("dbgS", [128, 14336], F32, kind="ExternalOutput").ap()
        dbgX = nc.dram_tensor("dbgX", [R, D], F32, kind="ExternalOutput").ap()
        dbgSd = nc.dram_tensor("dbgSd", [16, T], F32, kind="ExternalOutput").ap()
        dbgM = nc.dram_tensor("dbgM", [128, 320], F32, kind="ExternalOutput").ap()
        dbgW = nc.dram_tensor("dbgW", [128, 48], F32, kind="ExternalOutput").ap()

    x1dram = nc.dram_tensor("x1dram", [R, D], F32).ap()
    vdram = nc.dram_tensor("vdram", [II, R], F32).ap()
    sdram = nc.dram_tensor("sdram", [16, T], F32).ap()
    m128d = nc.dram_tensor("m128d", [1, 128], F32).ap()
    rsdd = nc.dram_tensor("rsdd", [1, D], F32).ap()
    cc_in = [nc.dram_tensor(f"cc_in{i}", [1, 64], F32) for i in range(2)]
    cc_out = [nc.dram_tensor(f"cc_out{i}", [1, 64], F32) for i in range(2)]

    A = nc.alloc_sbuf_tensor
    ARA = A("ARA", [128, 16384], F32)
    ARB = A("ARB", [128, 16384], F32)
    ARS = A("ARS", [128, 14336], F32)
    ARA_bf = ARA.bitcast(BF16)
    ARA_i = ARA.bitcast(I32)
    ARB_bf = ARB.bitcast(BF16)
    ARS_bf = ARS.bitcast(BF16)
    ARS_i = ARS.bitcast(I32)

    xnT = ARA
    attT = ARB
    qT = ARS[:, 0:2048]
    kT = ARS[:, 2048:4096]
    qs2 = [ARS_bf[:, 0:2048], ARS_bf[:, 2048:4096]]
    ks2 = [ARS_bf[:, 4096:6144], ARS_bf[:, 6144:8192]]
    dh_s = ARS_bf[:, 19520:20032]
    dl_s = ARS_bf[:, 20032:20544]
    wstage = ARS[:, 11296:12320]
    a1scr = ARS[:, 2048:3072]
    xnT_bf = ARA_bf
    vhp = ARS[:, 4096:6176]
    wsq = ARS[:, 6176:7200]
    wsk = ARS[:, 7200:8224]
    wsv = ARS[:, 8224:9248]
    srowb = ARS[:, 9248:9760]
    ph3 = [ARS_bf[:, 19520:20032], ARS_bf[:, 20544:21056],
           ARS_bf[:, 21568:22080]]
    pl3 = [ARS_bf[:, 20032:20544], ARS_bf[:, 21056:21568],
           ARS_bf[:, 22080:22592]]
    vhpb = ARS_bf[:, 8192:12352]
    sall = ARS[0:16, 10784:12832]
    srec = ARS[0:16, 0:2048]
    wo_full = ARA[:, 0:8192]

    xqT_bf = ARA_bf
    wdT_bf = ARB_bf
    wstg = ARB[:, 11264:14080]
    vtl = [ARB[:, 14080 + b * 512:14592 + b * 512] for b in range(2)]
    wq_u = ARS[:, 0:2816]
    wq_t1 = ARS[:, 2816:5632]
    wq_t1i = ARS_i[:, 2816:5632]
    wq_t2 = ARS[:, 5632:8448]
    wq_rnd = ARS[:, 8448:11264]
    wgT = [ARS_bf[:, 22528 + b * 1024:23552 + b * 1024] for b in range(2)]
    wuT = [ARS_bf[:, 24576 + b * 1024:25600 + b * 1024] for b in range(2)]
    atl = [ARS[:, 13312 + b * 512:13824 + b * 512] for b in range(2)]

    yt = ARS[:, 2048:3072]
    yb = ARS[:, 3072:4096]
    xnst = ARS[:, 4096:6144]
    maskb = ARS_bf[:, 25664:27712]
    mask_i = ARS[:, 0:512]
    iden_i = ARS[:, 512:640]
    nwrow = ARS[0:1, 6176:7200]
    rowt = ARS[0:1, 13856:13984]
    oh = ARS[0:16, 13984:14112]
    ohJ = ARS[0:16, 14112:14240]

    vstg = ARA[:, 0:2816]
    hq_u = ARA[:, 2816:5632]
    hq_t1 = ARA[:, 5632:8448]
    hq_t1i = ARA_i[:, 5632:8448]
    hq_t2 = ARA[:, 8448:11264]
    htile = ARA_bf[:, 22528:25344]
    sdb = ARA[:, 12672:13696]

    xst = A("xst", [128, 2 * D], F32)
    nwb = A("nwb", [128, D], F32)
    brows = A("brows", [128, D], F32)
    identf = A("identf", [128, 128], F32)
    identb = A("identb", [128, 128], BF16)
    onesP = A("onesP", [128, 512], F32)
    r_all = A("r_all", [128, NRT], F32)
    m1_all = A("m1_all", [128, NRT], F32)
    vm_all = A("vm_all", [128, 320], F32)
    stat = A("stat", [128, 20], F32)
    consts = A("consts", [128, 4], F32)
    clipt = A("clipt", [1, 2], F32)
    sig2 = A("sig2", [1, 2], F32)
    gsc = A("gsc", [1, 16], F32)
    ccst = A("ccst", [1, 64], F32)
    scbc = A("scbc", [128, 8], F32)
    swg = A("swg", [128, 4], F32)
    swu = A("swu", [128, 4], F32)
    sgu = A("sgu", [128, 4], F32)
    sdall = A("sdall", [128, 8], F32)
    bqkT = A("bqkT", [128, 24], F32)
    bob = ARS[:, 9760:10784]

    P5 = [nc.alloc_psum_tensor(f"ps{i}", [128, 512], F32) for i in range(4)]
    PAB = [nc.alloc_psum_tensor("psA", [128, 1024], F32),
           nc.alloc_psum_tensor("psB", [128, 1024], F32)]

    sLd = Sem(nc, "sLd")
    sSp = Sem(nc, "sSp")
    sTr = Sem(nc, "sTr")
    sEv = Sem(nc, "sEv")
    sMM = Sem(nc, "sMM")
    sPE = Sem(nc, "sPE")
    sXn = Sem(nc, "sXn")
    sSq = Sem(nc, "sSq")
    sR = Sem(nc, "sR")
    sP = Sem(nc, "sP")
    sMs = Sem(nc, "sMs")
    sGp = Sem(nc, "sGp")
    sOut = Sem(nc, "sOut")
    sLdV = Sem(nc, "sLdV")
    sLdV2 = Sem(nc, "sLdV2")
    sLdW = Sem(nc, "sLdW")
    ALLSEMS = [sLd, sSp, sTr, sEv, sMM, sPE, sXn, sSq, sR, sP, sMs, sOut,
               sLdV, sLdV2, sLdW]

    gates = {}

    def gwait(eng, key):
        if key in gates:
            h, v = gates[key]
            eng.wait_ge(h, v)

    def gset(key, sem, val=None):
        gates[key] = (sem.h, sem.n if val is None else val)

    ck = [0]

    def ckpt():
        ck[0] += 1
        if dbg_n is not None and ck[0] == dbg_n:
            raise StopBuild

    def barrier():
        for s_ in ALLSEMS:
            GP.wait_ge(s_.h, s_.n)
        sGp.m(GP.tensor_copy(stat[0:1, 14:15], stat[0:1, 15:16]))
        for eng in (SY, TEE, VE, SC):
            eng.wait_ge(sGp.h, sGp.n)

    with nc.Block() as block:
      try:
        sLd.d(SY.dma_start(out=nwrow[:, :], in_=nw_in))
        for j, w in enumerate("qkv"):
            sLd.d(SY.dma_start(
                out=bqkT[:, 8 * j:8 * j + 8],
                in_=b_in[w].rearrange("a (c p) -> p (a c)", p=128),
                allow_slow_non_contiguous=True))
        sLd.d(SY.dma_start(out=clipt[:, :], in_=clip_in))
        ld0 = sLd.n

        GP.iota(iden_i[:, :], [[1, 128]], base=0, channel_multiplier=-1,
                allow_small_or_imprecise_dtypes=True)
        GP.iota(mask_i[:, :], [[1, QB]], base=0, channel_multiplier=-1,
                allow_small_or_imprecise_dtypes=True)
        GP.iota(ohJ[:, :], [[-1, 2], [0, 64]], base=0, channel_multiplier=1,
                allow_small_or_imprecise_dtypes=True)
        GP.memset(onesP[:, :], 1.0)
        GP.memset(consts[:, 0:1], -30.0)
        GP.memset(consts[:, 1:2], 0.5)
        GP.memset(srowb[:, :], 0.0)
        vv = vhpb.rearrange("p (t c) -> p t c", c=260)
        GP.memset(vv[:, :, 64:65], 1.0)
        GP.memset(vv[:, :, 129:130], 1.0)
        GP.memset(vv[:, :, 194:195], 0.0)
        GP.memset(vv[:, :, 259:260], 0.0)
        sGp.m(GP.memset(ccst[:, :], 0.0))
        gp0 = sGp.n

        VE.wait_ge(sGp.h, gp0)
        VE.wait_ge(sLd.h, ld0)
        VE.tensor_scalar(out=identf[:, :], in0=iden_i[:, :], scalar1=0.0,
                         scalar2=None, op0=ALU.is_equal)
        VE.tensor_scalar(out=identb[:, :], in0=iden_i[:, :], scalar1=0.0,
                         scalar2=None, op0=ALU.is_equal)
        for j, dd in enumerate((0.0, 128.0, 256.0, 384.0)):
            VE.tensor_scalar(out=maskb[:, j * QB:(j + 1) * QB], in0=mask_i[:, :],
                             scalar1=dd, scalar2=-1e9, op0=ALU.is_lt, op1=ALU.mult)
        init_ve = sMs.m(VE.tensor_copy(stat[:, 15:16], consts[:, 0:1]))

        SC.wait_ge(sLd.h, ld0)
        a_e = sSq.m(SC.activation(out=gsc[0:1, 6:8], in_=clipt[:, :], func=AF.Exp,
                                  scale=-1.0))
        VE.wait_ge(sSq.h, a_e)
        v_e = sMs.m(VE.tensor_scalar(out=gsc[0:1, 9:11], in0=gsc[0:1, 6:8],
                                     scalar1=1.0, scalar2=None, op0=ALU.add))
        SC.wait_ge(sMs.h, v_e)
        a_e2 = sSq.m(SC.activation(out=gsc[0:1, 1:3], in_=gsc[0:1, 9:11],
                                   func=AF.Copy))
        VE.wait_ge(sSq.h, a_e2)
        sSq.m(VE.reciprocal(sig2[:, :], gsc[0:1, 1:3])) if False else None
        sMs.m(VE.reciprocal(sig2[0:1, :], gsc[0:1, 1:3]))

        TEE.wait_ge(sMs.h, init_ve)
        TEE.matmul(PAB[0][:, 0:512], onesP[0:1, 0:128], nwrow[0:1, 0:512],
                   start=True, stop=True)
        sMM.m(TEE.matmul(PAB[0][:, 512:1024], onesP[0:1, 0:128], nwrow[0:1, 512:1024],
                         start=True, stop=True))
        VE.wait_ge(sMM.h, sMM.n)
        sPE.m(VE.tensor_copy(nwb[:, :], PAB[0][:, :]))
        nwb_ready = sPE.n
        gset("PAB0", sPE)

        sall_ld_prev = [None]

        for s in range(NSEQ):
            VE.wait_ge(sMM.h, sMM.n)
            SC.wait_ge(sMs.h, sMs.n)

            bXn, bTr, bEv, bLd, bSq, bMs, bR = (
                sXn.n, sTr.n, sEv.n, sLd.n, sSq.n, sMs.n, sR.n)
            for tt in range(NTT):
                if tt >= 2:
                    SY.wait_ge(sXn.h, bXn + tt - 1)
                gwait(SY, f"xst:{tt % 2}")
                sLd.d(SY.dma_start(
                    out=xst[:, (tt % 2) * D:(tt % 2 + 1) * D],
                    in_=x_in[s, tt * 128:(tt + 1) * 128, :]))
                SC.wait_ge(sLd.h, bLd + 16 * (tt + 1))
                if tt >= 2:
                    SC.wait_ge(sTr.h, bTr + 8 * (tt - 1))
                SC.activation(
                    out=xnst[:, (tt % 2) * D:(tt % 2 + 1) * D],
                    in_=xst[:, (tt % 2) * D:(tt % 2 + 1) * D],
                    func=AF.Square, accum_out=stat[:, 0:1])
                sSq.m(SC.activation(out=stat[:, 16:17], in_=stat[:, 0:1],
                                    func=AF.Copy))
                VE.wait_ge(sSq.h, bSq + tt + 1)
                sMs.m(VE.tensor_scalar(out=stat[:, 1:2], in0=stat[:, 0:1],
                                       scalar1=INV1024, scalar2=1e-6,
                                       op0=ALU.mult, op1=ALU.add))
                SC.wait_ge(sMs.h, bMs + tt + 1)
                SC.activation(out=stat[:, 2:3], in_=stat[:, 1:2], func=AF.Ln)
                sR.m(SC.activation(out=r_all[:, s * NTT + tt: s * NTT + tt + 1],
                                   in_=stat[:, 2:3], func=AF.Exp, scale=-0.5))
                idx = s * NTT + tt
                VE.wait_ge(sPE.h, nwb_ready)
                VE.wait_ge(sR.h, bR + tt + 1)
                if tt >= 2:
                    VE.wait_ge(sTr.h, bTr + 8 * (tt - 1))
                sXn.m(VE.scalar_tensor_tensor(
                    out=xnst[:, (tt % 2) * D:(tt % 2 + 1) * D],
                    in0=xst[:, (tt % 2) * D:(tt % 2 + 1) * D],
                    scalar=r_all[:, idx:idx + 1], in1=nwb[:, :],
                    op0=ALU.mult, op1=ALU.mult))
                gset(f"xst:{tt % 2}", sXn)
                for dc in range(8):
                    j = tt * 8 + dc
                    TEE.wait_ge(sXn.h, bXn + tt + 1)
                    gwait(TEE, f"P5:{j % 2}")
                    mtr = sTr.m(TEE.transpose(
                        P5[j % 2][:, 0:128],
                        xnst[:, (tt % 2) * D + dc * 128:(tt % 2) * D + (dc + 1) * 128],
                        identf[:, :]))
                    VE.wait_ge(sTr.h, mtr)
                    sEv.m(VE.tensor_copy(a1scr[:, dc * 128:(dc + 1) * 128],
                                         P5[j % 2][:, 0:128]))
                    gset(f"P5:{j % 2}", sEv)
                hi3 = xnT_bf.rearrange("p (c z t) -> p c z t", c=8, z=2)[
                    :, :, 0, tt * 128:(tt + 1) * 128]
                lo3 = xnT_bf.rearrange("p (c z t) -> p c z t", c=8, z=2)[
                    :, :, 1, tt * 128:(tt + 1) * 128]
                VE.tensor_copy(hi3, a1scr.rearrange("p (c t) -> p c t", c=8))
                sEv.m(VE.tensor_tensor(
                    out=lo3, in0=a1scr.rearrange("p (c t) -> p c t", c=8),
                    in1=hi3, op=ALU.subtract))
            xnT_ready = sEv.n
            ckpt()
            GP.wait_ge(sTr.h, sTr.n)
            vv2 = vhpb.rearrange("p (t c) -> p t c", c=260)
            GP.memset(vv2[:, :, 64:65], 1.0)
            GP.memset(vv2[:, :, 129:130], 1.0)
            GP.memset(vv2[:, :, 194:195], 0.0)
            sGp.m(GP.memset(vv2[:, :, 259:260], 0.0))
            ones_ready = sGp.n

            for hp in range(8):
                wsplit_done = []
                for wname, dst in (("q", wsq), ("k", wsk), ("v", wsv)):
                    gwait(SY, "wstage")
                    gwait(SY, f"wsl:{wname}")
                    srcw = wT_in[wname][:, hp * 128:(hp + 1) * 128]
                    wld = sLd.d(SY.dma_start(
                        out=wstage.rearrange("p (c o) -> p c o", c=8),
                        in_=srcw.rearrange("(c p) o -> p c o", p=128)))
                    dbf = dst.bitcast(BF16)
                    VE.wait_ge(sLd.h, wld)
                    VE.tensor_copy(dbf[:, 0:1024], wstage[:, :])
                    wm = sMs.m(VE.tensor_tensor(out=dbf[:, 1024:2048],
                                                in0=wstage[:, :],
                                                in1=dbf[:, 0:1024],
                                                op=ALU.subtract))
                    gset("wstage", sMs)
                    wsplit_done.append(wm)
                w_ld = sLd.n

                TEE.wait_ge(sLd.h, w_ld)
                TEE.wait_ge(sMs.h, wsplit_done[2])
                TEE.wait_ge(sEv.h, xnT_ready)

                def xnslice(part, dc, c0, w):
                    off = dc * 4096 + (0 if part == 0 else 2048)
                    return xnT_bf[:, off + c0: off + c0 + w]

                for wi, wap in ((0, wsq), (1, wsk)):
                    wbf = wap.bitcast(BF16)
                    w3h = wbf[:, 0:1024].rearrange("p (c o) -> p c o", c=8)
                    w3l = wbf[:, 1024:2048].rearrange("p (c o) -> p c o", c=8)
                    bcol = bqkT[:, 8 * wi + hp: 8 * wi + hp + 1]
                    for tb in range(4):
                        gwait(TEE, f"P5:{tb % 2}")
                        for wpart, xpart in ((0, 0), (0, 1), (1, 0)):
                            w3 = w3h if wpart == 0 else w3l
                            for dc in range(8):
                                mi = TEE.matmul(
                                    P5[tb % 2][:, :], w3[:, dc, :],
                                    xnslice(xpart, dc, tb * QB, QB),
                                    start=(wpart == 0 and xpart == 0 and dc == 0),
                                    stop=(wpart == 1 and dc == 7))
                        qkm = sMM.m(mi)
                        VE.wait_ge(sMM.h, qkm)
                        VE.tensor_scalar(out=dh_s, in0=P5[tb % 2][:, :],
                                         scalar1=bcol, scalar2=None, op0=ALU.add)
                        pm_ = sPE.m(VE.scalar_tensor_tensor(
                            out=dl_s, in0=P5[tb % 2][:, :], scalar=bcol,
                            in1=dh_s, op0=ALU.add, op1=ALU.subtract))
                        gset(f"P5:{tb % 2}", sPE)
                        qsl2 = slice(tb * QB, (tb + 1) * QB)
                        ds2 = ks2 if wi == 1 else qs2
                        pieces = [(ds2[0], 0, dh_s, 0), (ds2[0], 64, dl_s, 0),
                                  (ds2[1], 0, dh_s, 64), (ds2[1], 64, dl_s, 64)]
                        for dst_, dp, src_, sp in pieces:
                            mcp = VE.tensor_copy(dst_[dp:dp + 64, qsl2],
                                                 src_[sp:sp + 64, :])
                        sPE.m(mcp)
                gset("wsl:q", sMM)
                gset("wsl:k", sMM)
                wvbf = wsv.bitcast(BF16)
                wv3h = wvbf[:, 0:1024].rearrange("p (c o) -> p c o", c=8)
                wv3l = wvbf[:, 1024:2048].rearrange("p (c o) -> p c o", c=8)
                for tt in range(NTT):
                    pb = 2 + tt % 2
                    sl = P5[pb][:, 0:128]
                    gwait(TEE, f"P5:{pb}")
                    for wpart, xpart in ((0, 0), (0, 1), (1, 0)):
                        wv3 = wv3h if wpart == 0 else wv3l
                        for dc in range(8):
                            mi = TEE.matmul(
                                sl, xnslice(xpart, dc, tt * 128, 128),
                                wv3[:, dc, :],
                                start=(wpart == 0 and xpart == 0 and dc == 0),
                                stop=(wpart == 1 and dc == 7))
                    vmk = sMM.m(mi)
                    vb_ = vhpb[:, tt * 260:(tt + 1) * 260]
                    SC.wait_ge(sMM.h, vmk)
                    SC.activation(out=vb_[:, 0:64], in_=P5[pb][:, 0:64],
                                  func=AF.Copy)
                    hm_ = sSq.m(SC.activation(out=vb_[:, 65:129],
                                              in_=P5[pb][:, 64:128],
                                              func=AF.Copy))
                    VE.wait_ge(sSq.h, hm_)
                    VE.tensor_tensor(out=vb_[:, 130:194], in0=P5[pb][:, 0:64],
                                     in1=vb_[:, 0:64], op=ALU.subtract)
                    sPE.m(VE.tensor_tensor(out=vb_[:, 195:259],
                                           in0=P5[pb][:, 64:128],
                                           in1=vb_[:, 65:129], op=ALU.subtract))
                    gset(f"P5:{pb}", sPE)
                gset("wsl:v", sMM)
                qkv_done = sPE.n
                if hp == 0:
                    ckpt()

                TEE.wait_ge(sPE.h, qkv_done)
                if hp == 0:
                    TEE.wait_ge(sGp.h, ones_ready)
                    gwait(SC, "PAB1")
                pairs = [(hl, q) for hl in range(2) for q in range(NQB)]

                def build_sw(i2):
                    hl, q = pairs[i2]
                    ab = (hl * NQB + q) % 2
                    sw = ARS_bf[:, 24640 + ab * 512: 25152 + ab * 512]
                    gwait(VE, f"sw7:{ab}")
                    VE.tensor_copy(sw[0:64, :],
                                   qs2[hl][64:128, q * QB:(q + 1) * QB])
                    return sPE.m(VE.tensor_copy(
                        sw[64:128, :], qs2[hl][0:64, q * QB:(q + 1) * QB])), sw

                m7n, swn = build_sw(0)
                for i2, (hloc, qb) in enumerate(pairs):
                    if True:
                        m7, sw7 = m7n, swn
                        nkb = 4 * qb + 4
                        SB3 = [0, 1, 3]
                        avb = (hloc * NQB + qb) % 2
                        AVP = PAB[0][:, avb * 512:(avb + 1) * 512]
                        qsl = slice(qb * QB, (qb + 1) * QB)

                        def emit_scores(kb):
                            sb = SB3[kb % 3]
                            gwait(TEE, f"P5:{sb}")
                            dofs = kb - 4 * qb
                            ksl = slice(kb * 128, (kb + 1) * 128)
                            TEE.matmul(P5[sb][:, :], ks2[hloc][:, ksl],
                                       qs2[hloc][:, qsl], start=True, stop=False)
                            mi = TEE.matmul(P5[sb][:, :], ks2[hloc][:, ksl],
                                            sw7[:, :], start=False,
                                            stop=(dofs < 0))
                            if dofs >= 0:
                                mi = TEE.matmul(P5[sb][:, :], identb[:, :],
                                                maskb[:, dofs * QB:(dofs + 1) * QB],
                                                start=False, stop=True)
                            return sMM.m(mi)

                        gwait(TEE, f"AVP:{avb}")
                        TEE.wait_ge(sPE.h, m7)
                        smarks = [emit_scores(0)]
                        if nkb > 1:
                            smarks.append(emit_scores(1))
                        for kb in range(nkb):
                            sb = SB3[kb % 3]
                            pp3 = kb % 3
                            ppf = kb % 2
                            SC.wait_ge(sMM.h, smarks[kb])
                            gwait(SC, f"ptl:{pp3}")
                            gwait(SC, f"pfp:{ppf}")
                            pm1 = sP.m(SC.activation(
                                out=ph3[pp3], in_=P5[sb][:, :], func=AF.Exp,
                                scale=0.125, bias=consts[:, 0:1]))
                            pm2 = sP.m(SC.activation(
                                out=PAB[1][:, ppf * 512:(ppf + 1) * 512],
                                in_=P5[sb][:, :], func=AF.Exp, scale=0.125,
                                bias=consts[:, 0:1]))
                            gset(f"P5:{sb}", sP)
                            if kb + 2 < nkb:
                                smarks.append(emit_scores(kb + 2))
                            VE.wait_ge(sP.h, pm2)
                            plm = sXn.m(VE.tensor_tensor(
                                out=pl3[pp3],
                                in0=PAB[1][:, ppf * 512:(ppf + 1) * 512],
                                in1=ph3[pp3], op=ALU.subtract))
                            gset(f"pfp:{ppf}", sXn)
                            kbase = kb * 260 + hloc * 65
                            vh_ = vhpb[:, kbase: kbase + 65]
                            vl_ = vhpb[:, kbase + 130: kbase + 195]
                            TEE.wait_ge(sP.h, pm1)
                            TEE.matmul(AVP[0:65, :], vh_, ph3[pp3],
                                       start=(kb == 0), stop=False)
                            TEE.matmul(AVP[0:65, :], vl_, ph3[pp3],
                                       start=False, stop=False)
                            TEE.wait_ge(sXn.h, plm)
                            sMM.m(TEE.matmul(AVP[0:65, :], vh_, pl3[pp3],
                                             start=False, stop=(kb == nkb - 1)))
                            gset(f"ptl:{pp3}", sMM)
                        av_done = sMM.n
                        gset(f"sw7:{avb}", sMM)
                        if i2 + 1 < len(pairs):
                            m7n, swn = build_sw(i2 + 1)
                        VE.wait_ge(sMM.h, av_done)
                        VE.tensor_copy(
                            attT[hloc * 64:hloc * 64 + 64,
                                 hp * T + qb * QB: hp * T + (qb + 1) * QB],
                            AVP[0:64, :])
                        gwait(VE, "srowb")
                        srm = sPE.m(VE.tensor_copy(srowb[64:65, 0:512],
                                                   AVP[64:65, :]))
                        gset(f"AVP:{avb}", sPE)
                        GP.wait_ge(sPE.h, srm)
                        if s > 0 and sall_ld_prev[0] is not None:
                            GP.wait_ge(sLd.h, sall_ld_prev[0])
                        sSp.d(GP.dma_start(
                            out=sdram[2 * hp + hloc: 2 * hp + hloc + 1,
                                      qb * QB:(qb + 1) * QB],
                            in_=srowb[64:65, 0:512]))
                        gset("srowb", sSp)
                if hp == 0:
                    ckpt()

            ckpt()
            SY.wait_ge(sMM.h, sMM.n)
            wo_ld = sLdW.d(SY.dma_start(
                out=wo_full.rearrange("p (c d) -> p c d", c=8),
                in_=wT_in["o"].rearrange("(c p) d -> p c d", p=128)))
            if s == 0:
                bo_ld = sLdW.d(SY.dma_start(out=brows[64:65, :], in_=b_in["o"]))
                TEE.wait_ge(sLdW.h, bo_ld)
            for nb2 in range(2):
                gwait(TEE, f"P5:{nb2}")
                bobm = sMM.m(TEE.matmul(
                    P5[nb2][:, :], onesP[64:65, 0:128],
                    brows[64:65, nb2 * 512:(nb2 + 1) * 512],
                    start=True, stop=True))
            VE.wait_ge(sMM.h, bobm)
            VE.tensor_copy(bob[:, 0:512], P5[0][:, :])
            sPE.m(VE.tensor_copy(bob[:, 512:1024], P5[1][:, :]))
            gset("P5:0", sPE)
            gset("P5:1", sPE)
            woh = ARA_bf[:, 16384:24576]
            wol = ARA_bf[:, 24576:32768]
            VE.wait_ge(sLdW.h, wo_ld)
            VE.tensor_copy(woh, wo_full[:, :])
            wom = sPE.m(VE.tensor_tensor(out=wol, in0=wo_full[:, :], in1=woh,
                                         op=ALU.subtract))
            SY.wait_ge(sSp.h, sSp.n)
            sall_ld = sLd.d(SY.dma_start(out=sall, in_=sdram[:, :]))
            sall_ld_prev[0] = sall_ld
            VE.wait_ge(sLd.h, sall_ld)
            VE.wait_ge(sMM.h, sMM.n)
            VE.reciprocal(srec, sall)
            srh = ARS_bf[0:16, 4096:6144]
            srl = ARS_bf[0:16, 6144:8192]
            VE.tensor_copy(srh, srec)
            sMs.m(VE.tensor_tensor(out=srl, in0=srec, in1=srh,
                                   op=ALU.subtract))
            gset("wstage", sMs)
            ohb = ARS_bf[0:16, 27968:28096]
            for hp in range(8):
                gwait(VE, "oh")
                ohm = sMs.m(VE.tensor_scalar(out=ohb[:, :], in0=ohJ[:, :],
                                             scalar1=float(2 * hp), scalar2=None,
                                             op0=ALU.is_equal))
                for qb in range(NQB):
                    TEE.wait_ge(sMs.h, ohm)
                    gwait(TEE, "P5:3")
                    TEE.matmul(P5[3][:, :], ohb[:, :],
                               srh[:, qb * QB:(qb + 1) * QB],
                               start=True, stop=False)
                    bm = sMM.m(TEE.matmul(P5[3][:, :], ohb[:, :],
                                          srl[:, qb * QB:(qb + 1) * QB],
                                          start=False, stop=True))
                    VE.wait_ge(sMM.h, bm)
                    tdiv = ARS[:, 10784:11296]
                    abase = 2 * hp * T + qb * 1024
                    VE.tensor_tensor(
                        out=tdiv,
                        in0=attT[:, hp * T + qb * QB: hp * T + (qb + 1) * QB],
                        in1=P5[3][:, :], op=ALU.mult)
                    VE.tensor_scalar(
                        out=tdiv, in0=tdiv,
                        scalar1=bqkT[:, 16 + hp:17 + hp], scalar2=None,
                        op0=ALU.add)
                    VE.tensor_copy(ARB_bf[:, abase:abase + 512], tdiv)
                    sPE.m(VE.tensor_tensor(
                        out=ARB_bf[:, abase + 512:abase + 1024], in0=tdiv,
                        in1=ARB_bf[:, abase:abase + 512], op=ALU.subtract))
                    gset("P5:3", sPE)
                gset("oh", sMM)
            attT_done = sPE.n
            ckpt()

            s1_prev = None
            for tt in range(NTT):
                pb = tt % 2
                gwait(SY, f"xst:{pb}")
                xm2 = sLd.d(SY.dma_start(
                    out=xst[:, pb * D:(pb + 1) * D],
                    in_=x_in[s, tt * 128:(tt + 1) * 128, :]))
                TEE.wait_ge(sPE.h, max(attT_done, wom))
                gwait(TEE, f"PAB{pb}")

                def atts(oc, part):
                    base = (2 * oc * T + (tt // 4) * 1024 + part * 512
                            + (tt % 4) * 128)
                    return ARB_bf[:, base: base + 128]

                for nb in range(2):
                    for wpart, apart in ((0, 0), (0, 1), (1, 0)):
                        wo_ = woh if wpart == 0 else wol
                        for oc in range(8):
                            mi = TEE.matmul(
                                PAB[pb][:, nb * 512:(nb + 1) * 512],
                                atts(oc, apart),
                                wo_[:, oc * 1024 + nb * 512: oc * 1024 + (nb + 1) * 512],
                                start=(wpart == 0 and apart == 0 and oc == 0),
                                stop=(wpart == 1 and oc == 7))
                opm = sMM.m(mi)

                idx = s * NTT + tt
                _md = os.environ.get("K_OPROJ_MODE", "full")
                VE.wait_ge(sMM.h, opm)
                VE.wait_ge(sLd.h, xm2)
                if s1_prev is not None:
                    VE.wait_ge(sSq.h, s1_prev)
                gwait(VE, f"xnst:{pb}")
                VE.scalar_tensor_tensor(
                    out=xnst[:, pb * D:(pb + 1) * D],
                    in0=xst[:, pb * D:(pb + 1) * D],
                    scalar=r_all[:, idx:idx + 1], in1=nwb[:, :],
                    op0=ALU.mult, op1=ALU.mult)
                if _md == "pab":
                    VE.tensor_scalar(out=yt[:, :], in0=PAB[pb][:, :], scalar1=1.0,
                                     scalar2=None, op0=ALU.mult)
                else:
                    VE.tensor_tensor(out=yt[:, :], in0=PAB[pb][:, :],
                                     in1=bob[:, :], op=ALU.add)
                    VE.tensor_tensor(out=yt[:, :], in0=yt[:, :],
                                     in1=xst[:, pb * D:(pb + 1) * D], op=ALU.add)
                pe_m = sPE.m(VE.tensor_tensor(out=yb[:, :], in0=yt[:, :],
                                              in1=xnst[:, pb * D:(pb + 1) * D],
                                              op=ALU.add) if _md != "pab" else
                             VE.tensor_copy(yb[:, :], yt[:, :]))
                gset(f"PAB{pb}", sPE)
                gset(f"xnst:{pb}", sPE)
                SC.wait_ge(sPE.h, pe_m)
                SC.activation(out=yt[:, :], in_=yb[:, :], func=AF.Copy,
                              accum_out=stat[:, 4:5])
                SC.activation(out=yt[:, :], in_=yb[:, :], func=AF.Square,
                              accum_out=stat[:, 5:6])
                s1 = sSq.m(SC.activation(out=stat[:, 16:17], in_=stat[:, 4:5],
                                         func=AF.Copy))
                s1_prev = s1
                VE.wait_ge(sSq.h, s1)
                VE.tensor_scalar(out=stat[:, 6:7], in0=stat[:, 4:5],
                                 scalar1=INV1024, scalar2=None, op0=ALU.mult)
                vm1 = sMs.m(VE.tensor_scalar(out=stat[:, 7:8], in0=stat[:, 5:6],
                                             scalar1=INV1024, scalar2=None,
                                             op0=ALU.mult))
                SC.wait_ge(sMs.h, vm1)
                am1 = sSq.m(SC.activation(out=stat[:, 8:9], in_=stat[:, 6:7],
                                          func=AF.Square))
                VE.wait_ge(sSq.h, am1)
                v1 = sMs.m(VE.scalar_tensor_tensor(
                    out=stat[:, 10:11], in0=stat[:, 7:8], scalar=1e-5,
                    in1=stat[:, 8:9], op0=ALU.add, op1=ALU.subtract))
                SC.wait_ge(sMs.h, v1)
                SC.activation(out=stat[:, 11:12], in_=stat[:, 10:11], func=AF.Ln)
                r1 = sR.m(SC.activation(out=stat[:, 12:13], in_=stat[:, 11:12],
                                        func=AF.Exp, scale=-0.5))
                VE.wait_ge(sR.h, r1)
                if dbg_n is not None:
                    for ci, sc_ in enumerate((6, 12, 4, 5, 7, 8, 9, 10, 11)):
                        VE.tensor_copy(vm_all[:, 32 * ci + idx:32 * ci + idx + 1],
                                       stat[:, sc_:sc_ + 1])
                if _md in ("pab", "noln"):
                    VE.tensor_copy(xst[:, pb * D:(pb + 1) * D], yb[:, :])
                else:
                    VE.tensor_scalar(out=xst[:, pb * D:(pb + 1) * D], in0=yb[:, :],
                                     scalar1=stat[:, 6:7], scalar2=stat[:, 12:13],
                                     op0=ALU.subtract, op1=ALU.mult)
                x1m = sMs.m(VE.tensor_reduce(
                    out=m1_all[:, idx:idx + 1], in_=xst[:, pb * D:(pb + 1) * D],
                    axis=AX.X, op=ALU.max, apply_absolute_value=True))
                GP.wait_ge(sMs.h, x1m)
                sSp.d(GP.dma_start(
                    out=x1dram[idx * 128:(idx + 1) * 128, :],
                    in_=xst[:, pb * D:(pb + 1) * D]))
                gset(f"xst:{pb}", sSp)

            ckpt()

        if stage == "ln1":
            GP.wait_ge(sSp.h, sSp.n)
            for i4 in range(4):
                sOut.d(GP.dma_start(
                    out=out[i4 // 2, (i4 % 2) * 1024:(i4 % 2 + 1) * 1024, :],
                    in_=x1dram[i4 * 1024:(i4 + 1) * 1024, :]))
            GP.wait_ge(sOut.h, sOut.n)
            return nc

        barrier()
        g1m = sMs.m(VE.tensor_reduce(out=stat[:, 3:4], in_=m1_all[:, 0:NRT],
                                     axis=AX.X, op=ALU.max))
        GP.wait_ge(sMs.h, g1m)
        sSp.d(GP.dma_start(out=m128d, in_=stat[:, 3:4]))
        SY.wait_ge(sSp.h, sSp.n)
        sLd.d(SY.dma_start(out=rowt[:, :], in_=m128d))
        VE.wait_ge(sLd.h, sLd.n)
        gm1 = sMs.m(VE.tensor_reduce(out=gsc[0:1, 0:1], in_=rowt[:, :],
                                     axis=AX.X, op=ALU.max))
        GP.wait_ge(sMs.h, gm1)
        GP.tensor_copy(ccst[0:1, 0:1], gsc[0:1, 0:1])
        sSp.d(GP.dma_start(out=cc_in[0].ap(), in_=ccst[:, :]))
        GP.wait_ge(sSp.h, sSp.n)
        GP.collective_compute(
            "AllReduce", ALU.max, ins=[cc_in[0].ap().opt()],
            outs=[cc_out[0].ap().opt()],
            replica_groups=[[0, 1, 2, 3, 4, 5, 6, 7]]).then_inc(sGp.h, 1)
        sGp.n += 1
        GP.wait_ge(sGp.h, sGp.n)
        sSp.d(GP.dma_start(out=rowt[0:1, 0:64], in_=cc_out[0].ap()))
        SC.wait_ge(sSp.h, sSp.n)
        SC.activation(out=gsc[0:1, 5:6], in_=rowt[0:1, 0:1], func=AF.Copy,
                      scale=sig2[0:1, 0:1])
        a1g = sSq.m(SC.activation(out=gsc[0:1, 3:4], in_=gsc[0:1, 5:6],
                                  func=AF.Copy, scale=INV127))
        VE.wait_ge(sSq.h, a1g)
        v1g = sMs.m(VE.reciprocal(gsc[0:1, 8:9], gsc[0:1, 3:4]))
        SC.wait_ge(sMs.h, v1g)
        scm = sSq.m(SC.activation(out=gsc[0:1, 4:5], in_=gsc[0:1, 8:9],
                                  func=AF.Copy))
        TEE.wait_ge(sSq.h, scm)
        bcm = sMM.m(TEE.matmul(P5[0][:, 0:2], onesP[0:1, 0:128], gsc[0:1, 3:5],
                               start=True, stop=True))
        VE.wait_ge(sMM.h, bcm)
        sPE.m(VE.tensor_copy(scbc[:, 0:2], P5[0][:, 0:2]))
        scx_ready = sPE.n
        ckpt()

        barrier()
        gates.pop("P5:0", None); gates.pop("P5:1", None)
        gates.pop("xst:0", None); gates.pop("xst:1", None)
        for t32 in range(NRT):
            pb = t32 % 2
            gwait(SY, f"xst:{pb}")
            xm = sLd.d(SY.dma_start(
                out=xst[:, pb * D:(pb + 1) * D],
                in_=x1dram[t32 * 128:(t32 + 1) * 128, :]))
            VE.wait_ge(sLd.h, xm)
            VE.wait_ge(sPE.h, scx_ready)
            gwait(VE, "wq")
            VE.tensor_scalar(out=wq_u[:, 0:1024], in0=xst[:, pb * D:(pb + 1) * D],
                             scalar1=scbc[:, 1:2], scalar2=0.5,
                             op0=ALU.mult, op1=ALU.add)
            gset(f"xst:{pb}", sMs, sMs.n + 1)
            VE.tensor_copy(wq_t1i[:, 0:1024], wq_u[:, 0:1024])
            VE.tensor_copy(wq_t2[:, 0:1024], wq_t1i[:, 0:1024])
            VE.tensor_tensor(out=wq_t1[:, 0:1024], in0=wq_t2[:, 0:1024],
                             in1=wq_u[:, 0:1024], op=ALU.is_gt)
            VE.tensor_tensor(out=wq_rnd[:, 0:1024], in0=wq_t2[:, 0:1024],
                             in1=wq_t1[:, 0:1024], op=ALU.subtract)
            qm = sMs.m(VE.tensor_scalar(out=wq_rnd[:, 0:1024], in0=wq_rnd[:, 0:1024],
                                        scalar1=-128.0, scalar2=127.0,
                                        op0=ALU.max, op1=ALU.min))
            for dc in range(8):
                j = t32 * 8 + dc
                TEE.wait_ge(sMs.h, qm)
                gwait(TEE, f"P5:{j % 2}")
                mtr = sTr.m(TEE.transpose(P5[j % 2][:, 0:128],
                                          wq_rnd[:, dc * 128:(dc + 1) * 128],
                                          identf[:, :]))
                VE.wait_ge(sTr.h, mtr)
                em = sEv.m(VE.tensor_copy(
                    xqT_bf[:, dc * R + t32 * 128: dc * R + (t32 + 1) * 128],
                    P5[j % 2][:, 0:128]))
                gset(f"P5:{j % 2}", sEv)
                if dc == 7:
                    gset("wq", sEv, em)
        xqT_ready = sEv.n
        ckpt()

        def prepB(j):
            bb = j % 2
            gwait(SY, "wstg")
            sLd.d(SY.dma_start(out=wstg[:, 0:1024],
                               in_=wg_in[j * 128:(j + 1) * 128, :]))
            l2 = sLd.d(SY.dma_start(out=wstg[:, 1024:2048],
                                    in_=wu_in[j * 128:(j + 1) * 128, :]))
            VE.wait_ge(sLd.h, l2)
            VE.tensor_reduce(out=stat[:, 17:18], in_=wstg[:, 0:1024],
                             axis=AX.X, op=ALU.max, apply_absolute_value=True)
            vr = sMs.m(VE.tensor_reduce(
                out=stat[:, 18:19], in_=wstg[:, 1024:2048],
                axis=AX.X, op=ALU.max, apply_absolute_value=True))
            SC.wait_ge(sMs.h, vr)
            ar = sSq.m(SC.activation(out=swg[:, 2 * bb:2 * bb + 2],
                                     in_=stat[:, 17:19], func=AF.Copy,
                                     scale=INV127))
            VE.wait_ge(sSq.h, ar)
            vr2 = sMs.m(VE.reciprocal(stat[:, 17:19], swg[:, 2 * bb:2 * bb + 2]))
            SC.wait_ge(sMs.h, vr2)
            ar2 = sSq.m(SC.activation(out=swu[:, 2 * bb:2 * bb + 2],
                                      in_=stat[:, 17:19], func=AF.Copy))
            VE.wait_ge(sSq.h, ar2)
            gwait(VE, "wq")
            qms = []
            for wi in range(2):
                sl = slice(wi * 1024, (wi + 1) * 1024)
                VE.tensor_scalar(out=wq_u[:, 0:1024], in0=wstg[:, sl],
                                 scalar1=swu[:, 2 * bb + wi:2 * bb + wi + 1],
                                 scalar2=0.5, op0=ALU.mult, op1=ALU.add)
                VE.tensor_copy(wq_t1i[:, 0:1024], wq_u[:, 0:1024])
                VE.tensor_copy(wq_t2[:, 0:1024], wq_t1i[:, 0:1024])
                VE.tensor_tensor(out=wq_t1[:, 0:1024], in0=wq_t2[:, 0:1024],
                                 in1=wq_u[:, 0:1024], op=ALU.is_gt)
                qms.append(sMs.m(VE.tensor_tensor(
                    out=wq_rnd[:, wi * 1024:(wi + 1) * 1024],
                    in0=wq_t2[:, 0:1024], in1=wq_t1[:, 0:1024], op=ALU.subtract)))
            gset("wstg", sMs)
            return ar, qms

        def prepB_T(j, qms):
            bb = j % 2
            for wi, dstT in ((0, wgT), (1, wuT)):
                for dc in range(8):
                    TEE.wait_ge(sMs.h, qms[wi])
                    gwait(TEE, f"P5:{dc % 2}")
                    mtr = sTr.m(TEE.transpose(
                        P5[dc % 2][:, 0:128],
                        wq_rnd[:, wi * 1024 + dc * 128: wi * 1024 + (dc + 1) * 128],
                        identf[:, :]))
                    VE.wait_ge(sTr.h, mtr)
                    gwait(VE, f"wgT:{bb}")
                    sEv.m(VE.tensor_copy(
                        dstT[bb][:, dc * 128:(dc + 1) * 128], P5[dc % 2][:, 0:128]))
                    gset(f"P5:{dc % 2}", sEv)
            gset("wq", sEv)
            return sEv.n

        ar_p, qms_p = prepB(0)
        wready_p = prepB_T(0, qms_p)
        for ib in range(NIC):
            bb = ib % 2
            armark, wready = ar_p, wready_p
            VE.wait_ge(sPE.h, scx_ready)
            VE.wait_ge(sSq.h, armark)
            VE.tensor_tensor(out=sgu[:, 2 * bb:2 * bb + 1],
                             in0=swg[:, 2 * bb:2 * bb + 1], in1=scbc[:, 0:1],
                             op=ALU.mult)
            sgm = sMs.m(VE.tensor_tensor(out=sgu[:, 2 * bb + 1:2 * bb + 2],
                                         in0=swg[:, 2 * bb + 1:2 * bb + 2],
                                         in1=scbc[:, 0:1], op=ALU.mult))
            if ib + 1 < NIC:
                ar_p, qms_p = prepB(ib + 1)
            for tb in range(8):
                pp = tb % 2
                TEE.wait_ge(sEv.h, max(wready, xqT_ready))
                gwait(TEE, f"PAB{pp}")
                for half, wT_ in ((0, wgT), (1, wuT)):
                    for dc in range(8):
                        mi = TEE.matmul(
                            PAB[pp][:, half * 512:(half + 1) * 512],
                            wT_[bb][:, dc * 128:(dc + 1) * 128],
                            xqT_bf[:, dc * R + tb * 512: dc * R + (tb + 1) * 512],
                            start=(dc == 0), stop=(dc == 7))
                gm_ = sMM.m(mi)
                gset(f"wgT:{bb}", sMM)
                SC.wait_ge(sMM.h, gm_)
                SC.wait_ge(sMs.h, sgm)
                gwait(SC, f"atl:{pp}")
                am = sSq.m(SC.activation(out=atl[pp], in_=PAB[pp][:, 0:512],
                                         func=AF.Silu,
                                         scale=sgu[:, 2 * bb:2 * bb + 1]))
                VE.wait_ge(sSq.h, am)
                gwait(VE, f"vtl:{pp}")
                VE.scalar_tensor_tensor(out=vtl[pp], in0=PAB[pp][:, 512:1024],
                                        scalar=sgu[:, 2 * bb + 1:2 * bb + 2],
                                        in1=atl[pp], op0=ALU.mult, op1=ALU.mult)
                gset(f"atl:{pp}", sMs, sMs.n + 1)
                vm = sMs.m(VE.tensor_reduce(
                    out=vm_all[:, ib * 8 + tb: ib * 8 + tb + 1], in_=vtl[pp],
                    axis=AX.X, op=ALU.max, apply_absolute_value=True))
                gates[f"PAB{pp}"] = (sMs.h, vm)
                GP.wait_ge(sMs.h, vm)
                sSp.d(GP.dma_start(
                    out=vdram[ib * 128:(ib + 1) * 128, tb * 512:(tb + 1) * 512],
                    in_=vtl[pp]))
                gset(f"vtl:{pp}", sSp)
            if ib + 1 < NIC:
                wready_p = prepB_T(ib + 1, qms_p)

        ckpt()
        if stage == "ffnv":
            GP.wait_ge(sSp.h, sSp.n)
            for i4 in range(4):
                sOut.d(GP.dma_start(
                    out=out[i4 // 2, (i4 % 2) * 1024:(i4 % 2 + 1) * 1024, :],
                    in_=x1dram[i4 * 1024:(i4 + 1) * 1024, :]))
            for ic in range(NIC):
                sOut.d(GP.dma_start(out=dbgv[ic * 128:(ic + 1) * 128, :],
                                    in_=vdram[ic * 128:(ic + 1) * 128, :]))
            GP.wait_ge(sOut.h, sOut.n)
            return nc

        barrier()
        g2m = sMs.m(VE.tensor_reduce(out=stat[:, 3:4], in_=vm_all[:, 0:NIC * 8],
                                     axis=AX.X, op=ALU.max))
        GP.wait_ge(sMs.h, g2m)
        sSp.d(GP.dma_start(out=m128d, in_=stat[:, 3:4]))
        SY.wait_ge(sSp.h, sSp.n)
        sLd.d(SY.dma_start(out=rowt[:, :], in_=m128d))
        VE.wait_ge(sLd.h, sLd.n)
        gm2 = sMs.m(VE.tensor_reduce(out=gsc[0:1, 8:9], in_=rowt[:, :],
                                     axis=AX.X, op=ALU.max))
        GP.wait_ge(sMs.h, gm2)
        GP.tensor_copy(ccst[0:1, 0:1], gsc[0:1, 8:9])
        sSp.d(GP.dma_start(out=cc_in[1].ap(), in_=ccst[:, :]))
        GP.wait_ge(sSp.h, sSp.n)
        GP.collective_compute(
            "AllReduce", ALU.max, ins=[cc_in[1].ap().opt()],
            outs=[cc_out[1].ap().opt()],
            replica_groups=[[0, 1, 2, 3, 4, 5, 6, 7]]).then_inc(sGp.h, 1)
        sGp.n += 1
        GP.wait_ge(sGp.h, sGp.n)
        sSp.d(GP.dma_start(out=rowt[0:1, 0:64], in_=cc_out[1].ap()))
        SC.wait_ge(sSp.h, sSp.n)
        SC.activation(out=gsc[0:1, 13:14], in_=rowt[0:1, 0:1], func=AF.Copy,
                      scale=sig2[0:1, 1:2])
        a2g = sSq.m(SC.activation(out=gsc[0:1, 11:12], in_=gsc[0:1, 13:14],
                                  func=AF.Copy, scale=INV127))
        VE.wait_ge(sSq.h, a2g)
        v2g = sMs.m(VE.reciprocal(gsc[0:1, 14:15], gsc[0:1, 11:12]))
        SC.wait_ge(sMs.h, v2g)
        scm2 = sSq.m(SC.activation(out=gsc[0:1, 12:13], in_=gsc[0:1, 14:15],
                                   func=AF.Copy))
        TEE.wait_ge(sSq.h, scm2)
        bcm2 = sMM.m(TEE.matmul(P5[0][:, 0:2], onesP[0:1, 0:128], gsc[0:1, 11:13],
                                start=True, stop=True))
        VE.wait_ge(sMM.h, bcm2)
        sPE.m(VE.tensor_copy(scbc[:, 2:4], P5[0][:, 0:2]))
        sch_ready = sPE.n
        ckpt()

        barrier()
        gates.pop("wstg", None); gates.pop("wq", None)
        gates.pop("P5:0", None); gates.pop("P5:1", None)
        gates.pop("xst:0", None); gates.pop("xst:1", None)
        for ds in range(8):
            gwait(SY, "wstg")
            l1 = sLd.d(SY.dma_start(out=wstg[:, :],
                                    in_=wd_in[ds * 128:(ds + 1) * 128, :]))
            VE.wait_ge(sLd.h, l1)
            gwait(VE, "wq")
            vr = sMs.m(VE.tensor_reduce(out=stat[:, 17:18], in_=wstg[:, :],
                                        axis=AX.X, op=ALU.max,
                                        apply_absolute_value=True))
            SC.wait_ge(sMs.h, vr)
            ar = sSq.m(SC.activation(out=sdall[:, ds:ds + 1], in_=stat[:, 17:18],
                                     func=AF.Copy, scale=INV127))
            VE.wait_ge(sSq.h, ar)
            vr2 = sMs.m(VE.reciprocal(stat[:, 18:19], sdall[:, ds:ds + 1]))
            SC.wait_ge(sMs.h, vr2)
            ar2 = sSq.m(SC.activation(out=stat[:, 13:14], in_=stat[:, 18:19],
                                      func=AF.Copy))
            VE.wait_ge(sSq.h, ar2)
            VE.tensor_scalar(out=wq_u[:, :], in0=wstg[:, :],
                             scalar1=stat[:, 13:14], scalar2=0.5,
                             op0=ALU.mult, op1=ALU.add)
            gset("wstg", sMs, sMs.n + 1)
            VE.tensor_copy(wq_t1i[:, :], wq_u[:, :])
            VE.tensor_copy(wq_t2[:, :], wq_t1i[:, :])
            VE.tensor_tensor(out=wq_t1[:, :], in0=wq_t2[:, :], in1=wq_u[:, :],
                             op=ALU.is_gt)
            qm = sMs.m(VE.tensor_tensor(out=wq_rnd[:, :], in0=wq_t2[:, :],
                                        in1=wq_t1[:, :], op=ALU.subtract))
            for ic in range(NIC):
                TEE.wait_ge(sMs.h, qm)
                gwait(TEE, f"P5:{ic % 2}")
                mtr = sTr.m(TEE.transpose(P5[ic % 2][:, 0:128],
                                          wq_rnd[:, ic * 128:(ic + 1) * 128],
                                          identf[:, :]))
                VE.wait_ge(sTr.h, mtr)
                sEv.m(VE.tensor_copy(
                    wdT_bf[:, ic * 1024 + ds * 128: ic * 1024 + (ds + 1) * 128],
                    P5[ic % 2][:, 0:128]))
                gset(f"P5:{ic % 2}", sEv)
            gset("wq", sEv)
        wdT_ready = sEv.n
        GP.wait_ge(sEv.h, wdT_ready)
        sSp.d(GP.dma_start(out=rsdd.rearrange("a (c p) -> (a p) c", p=128),
                           in_=sdall[:, 0:8], allow_slow_non_contiguous=True))
        SY.wait_ge(sSp.h, sSp.n)
        rl = sLd.d(SY.dma_start(out=nwrow[:, :], in_=rsdd))
        TEE.wait_ge(sLd.h, rl)
        gwait(TEE, "P5:0")
        TEE.matmul(P5[0][:, :], onesP[0:1, 0:128], nwrow[0:1, 0:512],
                   start=True, stop=True)
        gwait(TEE, "P5:1")
        sdm = sMM.m(TEE.matmul(P5[1][:, :], onesP[0:1, 0:128], nwrow[0:1, 512:1024],
                               start=True, stop=True))
        VE.wait_ge(sMM.h, sdm)
        VE.tensor_copy(sdb[:, 0:512], P5[0][:, :])
        sdbm = sPE.m(VE.tensor_copy(sdb[:, 512:1024], P5[1][:, :]))
        gset("P5:0", sPE)
        gset("P5:1", sPE)
        ckpt()

        vstgB = [vstg, ARS[:, 8448:11264]]
        htileB = [htile, ARA_bf[:, 27392:30208]]
        sLdVB = [sLdV, sLdV2]

        def ldV(i):
            pv = i % 2
            gwait(SY, f"vstg:{pv}")
            if i < 2:
                gwait(SY, "wq")
                gwait(SY, "wstg")
            return sLdVB[pv].d(SY.dma_start(
                out=vstgB[pv].rearrange("p (c t) -> p c t", c=NIC),
                in_=vdram[:, i * 128:(i + 1) * 128]
                .rearrange("(c p) t -> p c t", p=128)))

        def ldX(i):
            pv = i % 2
            gwait(SY, f"xst:{pv}")
            return sLd.d(SY.dma_start(
                out=xst[:, pv * D:(pv + 1) * D],
                in_=x1dram[i * 128:(i + 1) * 128, :]))

        def quantC(i, vm_):
            pv = i % 2
            VE.wait_ge(sLdVB[pv].h, vm_)
            VE.wait_ge(sPE.h, sch_ready)
            gwait(VE, f"htile:{pv}")
            VE.tensor_scalar(out=hq_u[:, :], in0=vstgB[pv], scalar1=scbc[:, 3:4],
                             scalar2=0.5, op0=ALU.mult, op1=ALU.add)
            gset(f"vstg:{pv}", sMs, sMs.n + 1)
            VE.tensor_copy(hq_t1i[:, :], hq_u[:, :])
            VE.tensor_copy(hq_t2[:, :], hq_t1i[:, :])
            VE.tensor_tensor(out=hq_t1[:, :], in0=hq_t2[:, :], in1=hq_u[:, :],
                             op=ALU.is_gt)
            VE.tensor_tensor(out=hq_u[:, :], in0=hq_t2[:, :], in1=hq_t1[:, :],
                             op=ALU.subtract)
            return sMs.m(VE.tensor_scalar(out=htileB[pv], in0=hq_u[:, :],
                                          scalar1=-128.0, scalar2=127.0,
                                          op0=ALU.max, op1=ALU.min))

        s2_prev = None
        vlds = {0: ldV(0), 1: ldV(1)}
        xms = {0: ldX(0), 1: ldX(1)}
        hm_p = quantC(0, vlds[0])
        for t32 in range(NRT):
            pb = t32 % 2
            s_, tt = t32 // NTT, t32 % NTT
            hm, xm_ = hm_p, xms[t32]
            if t32 + 1 < NRT:
                hm_p = quantC(t32 + 1, vlds[t32 + 1])
            TEE.wait_ge(sMs.h, hm)
            TEE.wait_ge(sEv.h, wdT_ready)
            TEE.wait_ge(sPE.h, sdbm)
            gwait(TEE, f"PAB{pb}")
            for nb in range(2):
                for ic in range(NIC):
                    mi = TEE.matmul(
                        PAB[pb][:, nb * 512:(nb + 1) * 512],
                        htileB[pb][:, ic * 128:(ic + 1) * 128],
                        wdT_bf[:, ic * 1024 + nb * 512: ic * 1024 + (nb + 1) * 512],
                        start=(ic == 0), stop=(ic == NIC - 1))
            dm = sMM.m(mi)
            gset(f"htile:{pb}", sMM)
            _mc = os.environ.get("K_C_MODE", "full")
            VE.wait_ge(sMM.h, dm)
            VE.wait_ge(sLd.h, xm_)
            if s2_prev is not None:
                VE.wait_ge(sSq.h, s2_prev)
            VE.scalar_tensor_tensor(out=yb[:, :], in0=PAB[pb][:, :],
                                    scalar=scbc[:, 2:3], in1=sdb[:, :],
                                    op0=ALU.mult, op1=ALU.mult)
            gset(f"PAB{pb}", sMs, sMs.n + 1)
            pe2 = sMs.m(VE.tensor_tensor(out=yt[:, :], in0=yb[:, :],
                                          in1=xst[:, pb * D:(pb + 1) * D],
                                          op=ALU.add) if _mc == "full" else
                        VE.tensor_copy(yt[:, :], yb[:, :]))
            SC.wait_ge(sMs.h, pe2)
            SC.activation(out=yb[:, :], in_=yt[:, :], func=AF.Copy,
                          accum_out=stat[:, 4:5])
            SC.activation(out=yb[:, :], in_=yt[:, :], func=AF.Square,
                          accum_out=stat[:, 5:6])
            s2 = sSq.m(SC.activation(out=stat[:, 16:17], in_=stat[:, 4:5],
                                     func=AF.Copy))
            s2_prev = s2
            VE.wait_ge(sSq.h, s2)
            VE.tensor_scalar(out=stat[:, 6:7], in0=stat[:, 4:5],
                             scalar1=INV1024, scalar2=None, op0=ALU.mult)
            vm1c = sMs.m(VE.tensor_scalar(out=stat[:, 7:8], in0=stat[:, 5:6],
                                          scalar1=INV1024, scalar2=None,
                                          op0=ALU.mult))
            SC.wait_ge(sMs.h, vm1c)
            am1c = sSq.m(SC.activation(out=stat[:, 8:9], in_=stat[:, 6:7],
                                       func=AF.Square))
            VE.wait_ge(sSq.h, am1c)
            v2 = sMs.m(VE.scalar_tensor_tensor(
                out=stat[:, 10:11], in0=stat[:, 7:8], scalar=1e-5,
                in1=stat[:, 8:9], op0=ALU.add, op1=ALU.subtract))
            SC.wait_ge(sMs.h, v2)
            SC.activation(out=stat[:, 11:12], in_=stat[:, 10:11], func=AF.Ln)
            r2 = sR.m(SC.activation(out=stat[:, 12:13], in_=stat[:, 11:12],
                                    func=AF.Exp, scale=-0.5))
            VE.wait_ge(sR.h, r2)
            om = sMs.m(VE.tensor_scalar(out=xst[:, pb * D:(pb + 1) * D],
                                        in0=yt[:, :],
                                        scalar1=stat[:, 6:7], scalar2=stat[:, 12:13],
                                        op0=ALU.subtract, op1=ALU.mult)
                       if _mc == "full" else
                       VE.tensor_copy(xst[:, pb * D:(pb + 1) * D], yt[:, :]))
            GP.wait_ge(sMs.h, om)
            sOut.d(GP.dma_start(out=out[s_, tt * 128:(tt + 1) * 128, :],
                                in_=xst[:, pb * D:(pb + 1) * D]))
            gset(f"xst:{pb}", sOut)
            if t32 + 2 < NRT:
                vlds[t32 + 2] = ldV(t32 + 2)
                xms[t32 + 2] = ldX(t32 + 2)
            if t32 == int(os.environ.get("K_T32", "0")):
                ckpt()

        if stage == "full2":
            for ic in range(NIC):
                sOut.d(GP.dma_start(out=dbgv[ic * 128:(ic + 1) * 128, :],
                                    in_=vdram[ic * 128:(ic + 1) * 128, :]))
            for i4 in range(4):
                sOut.d(GP.dma_start(out=dbgX2[i4 * 1024:(i4 + 1) * 1024, :],
                                    in_=x1dram[i4 * 1024:(i4 + 1) * 1024, :]))
        GP.wait_ge(sOut.h, sOut.n)
      except StopBuild:
        for s_ in ALLSEMS:
            GP.wait_ge(s_.h, s_.n)
        sOut.d(GP.dma_start(out=dbgA, in_=ARA[:, :]))
        sOut.d(GP.dma_start(out=dbgB, in_=ARB[:, :]))
        sOut.d(GP.dma_start(out=dbgS, in_=ARS[:, :]))
        sOut.d(GP.dma_start(out=dbgX, in_=x1dram))
        sOut.d(GP.dma_start(out=dbgSd, in_=sdram))
        sOut.d(GP.dma_start(out=dbgM, in_=vm_all[:, 0:320]))
        sOut.d(GP.dma_start(out=dbgW[:, 0:8], in_=scbc[:, :]))
        sOut.d(GP.dma_start(out=dbgW[:, 8:16], in_=sdall[:, :]))
        sOut.d(GP.dma_start(out=dbgW[:, 16:36], in_=stat[:, :]))
        sOut.d(GP.dma_start(out=dbgW[0:1, 36:48], in_=gsc[0:1, 0:12]))
        GP.wait_ge(sOut.h, sOut.n)

    return nc


def kernel(**inputs):
    x = np.ascontiguousarray(np.asarray(inputs["L0_attn_x"], dtype=np.float32))
    nw = np.asarray(inputs["L0_attn_norm_w"], dtype=np.float32).reshape(1, D)
    Ws = {w: np.ascontiguousarray(
        np.asarray(inputs[f"L0_attn_W{w}"], dtype=np.float32).T) for w in "qkvo"}
    bs = {w: np.asarray(inputs[f"L0_attn_b{w}"], dtype=np.float32).reshape(1, D)
          for w in "qkvo"}
    clips = np.array([[np.asarray(inputs["L0_ffn_clip_x"], np.float32).ravel()[0],
                       np.asarray(inputs["L0_ffn_clip_down"], np.float32).ravel()[0]]],
                     dtype=np.float32)
    wg = np.ascontiguousarray(np.asarray(inputs["L0_ffn_w_gate"], np.float32))
    wu = np.ascontiguousarray(np.asarray(inputs["L0_ffn_w_up"], np.float32))
    wd = np.ascontiguousarray(np.asarray(inputs["L0_ffn_w_down"], np.float32))

    nc = build_program("full")
    in_maps = []
    for c in range(8):
        m = {"x": np.ascontiguousarray(x[2 * c:2 * c + 2]),
             "norm_w": nw, "clips": clips,
             "w_gate": wg, "w_up": wu, "w_down": wd}
        for w in "qkvo":
            m[f"w{w}T"] = Ws[w]
            m[f"b{w}"] = bs[w]
        in_maps.append(m)
    res = run_bass_kernel_spmd(nc, in_maps, core_ids=list(range(8)))
    return np.concatenate([np.asarray(r["out"]) for r in res.results], axis=0)


# version 19
# speedup vs baseline: 1.0709x; 1.0201x over previous
import os
import numpy as np
import ml_dtypes
import concourse.bass as bass
import concourse.mybir as mybir
from concourse.bass_utils import run_bass_kernel_spmd

F32 = mybir.dt.float32
BF16 = mybir.dt.bfloat16
I32 = mybir.dt.int32
AF = mybir.ActivationFunctionType
ALU = mybir.AluOpType
AX = mybir.AxisListType

T = 2048
NSEQ = 2
D = 1024
II = 2816
NIC = II // 128
NTT = T // 128
R = NSEQ * T
NRT = R // 128
QB = 512
NQB = T // QB
INV127 = float(np.float32(1.0) / np.float32(127.0))
INV1024 = float(np.float32(1.0) / np.float32(1024.0))


class Sem:
    def __init__(self, nc, name):
        self.h = nc.alloc_semaphore(name)
        self.n = 0

    def d(self, instr):
        instr.then_inc(self.h, 16)
        self.n += 16
        return self.n

    def m(self, instr):
        instr.then_inc(self.h, 1)
        self.n += 1
        return self.n


class StopBuild(Exception):
    pass


def build_program(stage="full"):
    nc = bass.Bass("TRN2", target_bir_lowering=False,
                   detect_race_conditions=False)
    SY, TEE, VE, SC, GP = nc.sync, nc.tensor, nc.vector, nc.scalar, nc.gpsimd

    x_in = nc.dram_tensor("x", [NSEQ, T, D], F32, kind="ExternalInput").ap()
    nw_in = nc.dram_tensor("norm_w", [1, D], F32, kind="ExternalInput").ap()
    wT_in = {w: nc.dram_tensor(f"w{w}T", [D, D], F32, kind="ExternalInput").ap()
             for w in "qkvo"}
    b_in = {w: nc.dram_tensor(f"b{w}", [1, D], F32, kind="ExternalInput").ap()
            for w in "qkvo"}
    clip_in = nc.dram_tensor("clips", [1, 2], F32, kind="ExternalInput").ap()
    wg_in = nc.dram_tensor("w_gate", [II, D], F32, kind="ExternalInput").ap()
    wu_in = nc.dram_tensor("w_up", [II, D], F32, kind="ExternalInput").ap()
    wd_in = nc.dram_tensor("w_down", [D, II], F32, kind="ExternalInput").ap()
    out = nc.dram_tensor("out", [NSEQ, T, D], F32, kind="ExternalOutput").ap()
    if stage in ("ln1", "ffnv", "full2"):
        dbgv = nc.dram_tensor("dbgv", [II, R], F32, kind="ExternalOutput").ap()
    if stage == "full2":
        dbgX2 = nc.dram_tensor("dbgX2", [R, D], F32, kind="ExternalOutput").ap()
    dbg_n = int(stage[4:]) if isinstance(stage, str) and stage.startswith("dbg:") else None
    if dbg_n is not None:
        dbgA = nc.dram_tensor("dbgA", [128, 16384], F32, kind="ExternalOutput").ap()
        dbgB = nc.dram_tensor("dbgB", [128, 16384], F32, kind="ExternalOutput").ap()
        dbgS = nc.dram_tensor("dbgS", [128, 14336], F32, kind="ExternalOutput").ap()
        dbgX = nc.dram_tensor("dbgX", [R, D], F32, kind="ExternalOutput").ap()
        dbgSd = nc.dram_tensor("dbgSd", [16, T], F32, kind="ExternalOutput").ap()
        dbgM = nc.dram_tensor("dbgM", [128, 320], F32, kind="ExternalOutput").ap()
        dbgW = nc.dram_tensor("dbgW", [128, 48], F32, kind="ExternalOutput").ap()

    x1dram = nc.dram_tensor("x1dram", [R, D], F32).ap()
    vdram = nc.dram_tensor("vdram", [II, R], F32).ap()
    sdram = nc.dram_tensor("sdram", [16, T], F32).ap()
    m128d = nc.dram_tensor("m128d", [1, 128], F32).ap()
    rsdd = nc.dram_tensor("rsdd", [1, D], F32).ap()
    cc_in = [nc.dram_tensor(f"cc_in{i}", [1, 64], F32) for i in range(2)]
    cc_out = [nc.dram_tensor(f"cc_out{i}", [1, 64], F32) for i in range(2)]

    A = nc.alloc_sbuf_tensor
    ARA = A("ARA", [128, 16384], F32)
    ARB = A("ARB", [128, 16384], F32)
    ARS = A("ARS", [128, 14336], F32)
    ARA_bf = ARA.bitcast(BF16)
    ARA_i = ARA.bitcast(I32)
    ARB_bf = ARB.bitcast(BF16)
    ARS_bf = ARS.bitcast(BF16)
    ARS_i = ARS.bitcast(I32)

    xnT = ARA
    attT = ARB
    qT = ARS[:, 0:2048]
    kT = ARS[:, 2048:4096]
    qs2 = [ARS_bf[:, 0:2048], ARS_bf[:, 2048:4096]]
    ks2 = [ARS_bf[:, 4096:6144], ARS_bf[:, 6144:8192]]
    dh_s = ARS_bf[:, 19520:20032]
    dl_s = ARS_bf[:, 20032:20544]
    wstage = ARS[:, 11296:12320]
    a1scr = ARS[:, 2048:3072]
    xnT_bf = ARA_bf
    vhp = ARS[:, 4096:6176]
    wsq = ARS[:, 6176:7200]
    wsk = ARS[:, 7200:8224]
    wsv = ARS[:, 8224:9248]
    srowb = ARS[:, 9248:9760]
    ph3 = [ARS_bf[:, 19520:20032], ARS_bf[:, 20544:21056],
           ARS_bf[:, 21568:22080]]
    pl3 = [ARS_bf[:, 20032:20544], ARS_bf[:, 21056:21568],
           ARS_bf[:, 22080:22592]]
    vhpb = ARS_bf[:, 8192:12352]
    sall = ARS[0:16, 10784:12832]
    srec = ARS[0:16, 0:2048]
    wo_full = ARA[:, 0:8192]

    xqT_bf = ARA_bf
    wdT_bf = ARB_bf
    wstg = ARB[:, 11264:14080]
    vtl = [ARB[:, 14080 + b * 512:14592 + b * 512] for b in range(2)]
    wq_u = ARS[:, 0:2816]
    wq_t1 = ARS[:, 2816:5632]
    wq_t1i = ARS_i[:, 2816:5632]
    wq_t2 = ARS[:, 5632:8448]
    wq_rnd = ARS[:, 8448:11264]
    wgT = [ARS_bf[:, 22528 + b * 1024:23552 + b * 1024] for b in range(2)]
    wuT = [ARS_bf[:, 24576 + b * 1024:25600 + b * 1024] for b in range(2)]
    atl = [ARS[:, 13312 + b * 512:13824 + b * 512] for b in range(2)]

    yt = ARS[:, 2048:3072]
    yb = ARS[:, 3072:4096]
    xnst = ARS[:, 4096:6144]
    maskb = ARS_bf[:, 25664:27712]
    mask_i = ARS[:, 0:512]
    iden_i = ARS[:, 512:640]
    nwrow = ARS[0:1, 6176:7200]
    rowt = ARS[0:1, 13856:13984]
    oh = ARS[0:16, 13984:14112]
    ohJ = ARS[0:16, 14112:14240]

    vstg = ARA[:, 0:2816]
    hq_u = ARA[:, 2816:5632]
    hq_t1 = ARA[:, 5632:8448]
    hq_t1i = ARA_i[:, 5632:8448]
    hq_t2 = ARA[:, 8448:11264]
    htile = ARA_bf[:, 22528:25344]
    sdb = ARA[:, 12672:13696]

    xst = A("xst", [128, 2 * D], F32)
    nwb = A("nwb", [128, D], F32)
    brows = A("brows", [128, D], F32)
    identf = A("identf", [128, 128], F32)
    identb = A("identb", [128, 128], BF16)
    onesP = A("onesP", [128, 512], F32)
    r_all = A("r_all", [128, NRT], F32)
    m1_all = A("m1_all", [128, NRT], F32)
    vm_all = A("vm_all", [128, 320], F32)
    stat = A("stat", [128, 20], F32)
    consts = A("consts", [128, 4], F32)
    clipt = A("clipt", [1, 2], F32)
    sig2 = A("sig2", [1, 2], F32)
    gsc = A("gsc", [1, 16], F32)
    ccst = A("ccst", [1, 64], F32)
    scbc = A("scbc", [128, 8], F32)
    swg = A("swg", [128, 4], F32)
    swu = A("swu", [128, 4], F32)
    sgu = A("sgu", [128, 4], F32)
    sdall = A("sdall", [128, 8], F32)
    bqkT = A("bqkT", [128, 24], F32)
    bob = ARS[:, 9760:10784]

    P5 = [nc.alloc_psum_tensor(f"ps{i}", [128, 512], F32) for i in range(4)]
    PAB = [nc.alloc_psum_tensor("psA", [128, 1024], F32),
           nc.alloc_psum_tensor("psB", [128, 1024], F32)]

    sLd = Sem(nc, "sLd")
    sSp = Sem(nc, "sSp")
    sTr = Sem(nc, "sTr")
    sEv = Sem(nc, "sEv")
    sMM = Sem(nc, "sMM")
    sPE = Sem(nc, "sPE")
    sXn = Sem(nc, "sXn")
    sSq = Sem(nc, "sSq")
    sR = Sem(nc, "sR")
    sP = Sem(nc, "sP")
    sMs = Sem(nc, "sMs")
    sGp = Sem(nc, "sGp")
    sOut = Sem(nc, "sOut")
    sLdV = Sem(nc, "sLdV")
    sLdV2 = Sem(nc, "sLdV2")
    sLdW = Sem(nc, "sLdW")
    ALLSEMS = [sLd, sSp, sTr, sEv, sMM, sPE, sXn, sSq, sR, sP, sMs, sOut,
               sLdV, sLdV2, sLdW]

    gates = {}

    def gwait(eng, key):
        if key in gates:
            h, v = gates[key]
            eng.wait_ge(h, v)

    def gset(key, sem, val=None):
        gates[key] = (sem.h, sem.n if val is None else val)

    ck = [0]

    def ckpt():
        ck[0] += 1
        if dbg_n is not None and ck[0] == dbg_n:
            raise StopBuild

    def barrier():
        for s_ in ALLSEMS:
            GP.wait_ge(s_.h, s_.n)
        sGp.m(GP.tensor_copy(stat[0:1, 14:15], stat[0:1, 15:16]))
        for eng in (SY, TEE, VE, SC):
            eng.wait_ge(sGp.h, sGp.n)

    with nc.Block() as block:
      try:
        sLd.d(SY.dma_start(out=nwrow[:, :], in_=nw_in))
        for j, w in enumerate("qkv"):
            sLd.d(SY.dma_start(
                out=bqkT[:, 8 * j:8 * j + 8],
                in_=b_in[w].rearrange("a (c p) -> p (a c)", p=128),
                allow_slow_non_contiguous=True))
        sLd.d(SY.dma_start(out=clipt[:, :], in_=clip_in))
        ld0 = sLd.n

        GP.iota(iden_i[:, :], [[1, 128]], base=0, channel_multiplier=-1,
                allow_small_or_imprecise_dtypes=True)
        GP.iota(mask_i[:, :], [[1, QB]], base=0, channel_multiplier=-1,
                allow_small_or_imprecise_dtypes=True)
        GP.iota(ohJ[:, :], [[-1, 2], [0, 64]], base=0, channel_multiplier=1,
                allow_small_or_imprecise_dtypes=True)
        GP.memset(onesP[:, :], 1.0)
        GP.memset(consts[:, 0:1], -30.0)
        GP.memset(consts[:, 1:2], 0.5)
        GP.memset(srowb[:, :], 0.0)
        vv = vhpb.rearrange("p (t c) -> p t c", c=260)
        GP.memset(vv[:, :, 64:65], 1.0)
        GP.memset(vv[:, :, 129:130], 1.0)
        GP.memset(vv[:, :, 194:195], 0.0)
        GP.memset(vv[:, :, 259:260], 0.0)
        sGp.m(GP.memset(ccst[:, :], 0.0))
        gp0 = sGp.n

        VE.wait_ge(sGp.h, gp0)
        VE.wait_ge(sLd.h, ld0)
        VE.tensor_scalar(out=identf[:, :], in0=iden_i[:, :], scalar1=0.0,
                         scalar2=None, op0=ALU.is_equal)
        VE.tensor_scalar(out=identb[:, :], in0=iden_i[:, :], scalar1=0.0,
                         scalar2=None, op0=ALU.is_equal)
        for j, dd in enumerate((0.0, 128.0, 256.0, 384.0)):
            VE.tensor_scalar(out=maskb[:, j * QB:(j + 1) * QB], in0=mask_i[:, :],
                             scalar1=dd, scalar2=-1e9, op0=ALU.is_lt, op1=ALU.mult)
        init_ve = sMs.m(VE.tensor_copy(stat[:, 15:16], consts[:, 0:1]))

        SC.wait_ge(sLd.h, ld0)
        a_e = sSq.m(SC.activation(out=gsc[0:1, 6:8], in_=clipt[:, :], func=AF.Exp,
                                  scale=-1.0))
        VE.wait_ge(sSq.h, a_e)
        v_e = sMs.m(VE.tensor_scalar(out=gsc[0:1, 9:11], in0=gsc[0:1, 6:8],
                                     scalar1=1.0, scalar2=None, op0=ALU.add))
        SC.wait_ge(sMs.h, v_e)
        a_e2 = sSq.m(SC.activation(out=gsc[0:1, 1:3], in_=gsc[0:1, 9:11],
                                   func=AF.Copy))
        VE.wait_ge(sSq.h, a_e2)
        sSq.m(VE.reciprocal(sig2[:, :], gsc[0:1, 1:3])) if False else None
        sMs.m(VE.reciprocal(sig2[0:1, :], gsc[0:1, 1:3]))

        TEE.wait_ge(sMs.h, init_ve)
        TEE.matmul(PAB[0][:, 0:512], onesP[0:1, 0:128], nwrow[0:1, 0:512],
                   start=True, stop=True)
        sMM.m(TEE.matmul(PAB[0][:, 512:1024], onesP[0:1, 0:128], nwrow[0:1, 512:1024],
                         start=True, stop=True))
        VE.wait_ge(sMM.h, sMM.n)
        sPE.m(VE.tensor_copy(nwb[:, :], PAB[0][:, :]))
        nwb_ready = sPE.n
        gset("PAB0", sPE)

        sall_ld_prev = [None]

        for s in range(NSEQ):
            VE.wait_ge(sMM.h, sMM.n)
            SC.wait_ge(sMs.h, sMs.n)

            bXn, bTr, bEv, bLd, bSq, bMs, bR = (
                sXn.n, sTr.n, sEv.n, sLd.n, sSq.n, sMs.n, sR.n)
            for tt in range(NTT):
                if tt >= 2:
                    SY.wait_ge(sXn.h, bXn + tt - 1)
                gwait(SY, f"xst:{tt % 2}")
                sLd.d(SY.dma_start(
                    out=xst[:, (tt % 2) * D:(tt % 2 + 1) * D],
                    in_=x_in[s, tt * 128:(tt + 1) * 128, :]))
                SC.wait_ge(sLd.h, bLd + 16 * (tt + 1))
                if tt >= 2:
                    SC.wait_ge(sTr.h, bTr + 8 * (tt - 1))
                SC.activation(
                    out=xnst[:, (tt % 2) * D:(tt % 2 + 1) * D],
                    in_=xst[:, (tt % 2) * D:(tt % 2 + 1) * D],
                    func=AF.Square, accum_out=stat[:, 0:1])
                sSq.m(SC.activation(out=stat[:, 16:17], in_=stat[:, 0:1],
                                    func=AF.Copy))
                VE.wait_ge(sSq.h, bSq + tt + 1)
                sMs.m(VE.tensor_scalar(out=stat[:, 1:2], in0=stat[:, 0:1],
                                       scalar1=INV1024, scalar2=1e-6,
                                       op0=ALU.mult, op1=ALU.add))
                SC.wait_ge(sMs.h, bMs + tt + 1)
                SC.activation(out=stat[:, 2:3], in_=stat[:, 1:2], func=AF.Ln)
                sR.m(SC.activation(out=r_all[:, s * NTT + tt: s * NTT + tt + 1],
                                   in_=stat[:, 2:3], func=AF.Exp, scale=-0.5))
                idx = s * NTT + tt
                VE.wait_ge(sPE.h, nwb_ready)
                VE.wait_ge(sR.h, bR + tt + 1)
                if tt >= 2:
                    VE.wait_ge(sTr.h, bTr + 8 * (tt - 1))
                sXn.m(VE.scalar_tensor_tensor(
                    out=xnst[:, (tt % 2) * D:(tt % 2 + 1) * D],
                    in0=xst[:, (tt % 2) * D:(tt % 2 + 1) * D],
                    scalar=r_all[:, idx:idx + 1], in1=nwb[:, :],
                    op0=ALU.mult, op1=ALU.mult))
                gset(f"xst:{tt % 2}", sXn)
                for dc in range(8):
                    j = tt * 8 + dc
                    TEE.wait_ge(sXn.h, bXn + tt + 1)
                    gwait(TEE, f"P5:{j % 2}")
                    mtr = sTr.m(TEE.transpose(
                        P5[j % 2][:, 0:128],
                        xnst[:, (tt % 2) * D + dc * 128:(tt % 2) * D + (dc + 1) * 128],
                        identf[:, :]))
                    VE.wait_ge(sTr.h, mtr)
                    sEv.m(VE.tensor_copy(a1scr[:, dc * 128:(dc + 1) * 128],
                                         P5[j % 2][:, 0:128]))
                    gset(f"P5:{j % 2}", sEv)
                hi3 = xnT_bf.rearrange("p (c z t) -> p c z t", c=8, z=2)[
                    :, :, 0, tt * 128:(tt + 1) * 128]
                lo3 = xnT_bf.rearrange("p (c z t) -> p c z t", c=8, z=2)[
                    :, :, 1, tt * 128:(tt + 1) * 128]
                VE.tensor_copy(hi3, a1scr.rearrange("p (c t) -> p c t", c=8))
                sEv.m(VE.tensor_tensor(
                    out=lo3, in0=a1scr.rearrange("p (c t) -> p c t", c=8),
                    in1=hi3, op=ALU.subtract))
            xnT_ready = sEv.n
            ckpt()
            GP.wait_ge(sTr.h, sTr.n)
            vv2 = vhpb.rearrange("p (t c) -> p t c", c=260)
            GP.memset(vv2[:, :, 64:65], 1.0)
            GP.memset(vv2[:, :, 129:130], 1.0)
            GP.memset(vv2[:, :, 194:195], 0.0)
            sGp.m(GP.memset(vv2[:, :, 259:260], 0.0))
            ones_ready = sGp.n

            for hp in range(8):
                wsplit_done = []
                for wname, dst in (("q", wsq), ("k", wsk), ("v", wsv)):
                    gwait(SY, "wstage")
                    gwait(SY, f"wsl:{wname}")
                    srcw = wT_in[wname][:, hp * 128:(hp + 1) * 128]
                    wld = sLd.d(SY.dma_start(
                        out=wstage.rearrange("p (c o) -> p c o", c=8),
                        in_=srcw.rearrange("(c p) o -> p c o", p=128)))
                    dbf = dst.bitcast(BF16)
                    VE.wait_ge(sLd.h, wld)
                    VE.tensor_copy(dbf[:, 0:1024], wstage[:, :])
                    wm = sMs.m(VE.tensor_tensor(out=dbf[:, 1024:2048],
                                                in0=wstage[:, :],
                                                in1=dbf[:, 0:1024],
                                                op=ALU.subtract))
                    gset("wstage", sMs)
                    wsplit_done.append(wm)
                w_ld = sLd.n

                TEE.wait_ge(sLd.h, w_ld)
                TEE.wait_ge(sMs.h, wsplit_done[2])
                TEE.wait_ge(sEv.h, xnT_ready)

                def xnslice(part, dc, c0, w):
                    off = dc * 4096 + (0 if part == 0 else 2048)
                    return xnT_bf[:, off + c0: off + c0 + w]

                for wi, wap in ((0, wsq), (1, wsk)):
                    wbf = wap.bitcast(BF16)
                    w3h = wbf[:, 0:1024].rearrange("p (c o) -> p c o", c=8)
                    w3l = wbf[:, 1024:2048].rearrange("p (c o) -> p c o", c=8)
                    bcol = bqkT[:, 8 * wi + hp: 8 * wi + hp + 1]
                    for tb in range(4):
                        gwait(TEE, f"P5:{tb % 2}")
                        for wpart, xpart in ((0, 0), (0, 1), (1, 0)):
                            w3 = w3h if wpart == 0 else w3l
                            for dc in range(8):
                                mi = TEE.matmul(
                                    P5[tb % 2][:, :], w3[:, dc, :],
                                    xnslice(xpart, dc, tb * QB, QB),
                                    start=(wpart == 0 and xpart == 0 and dc == 0),
                                    stop=(wpart == 1 and dc == 7))
                        qkm = sMM.m(mi)
                        VE.wait_ge(sMM.h, qkm)
                        VE.tensor_scalar(out=dh_s, in0=P5[tb % 2][:, :],
                                         scalar1=bcol, scalar2=None, op0=ALU.add)
                        pm_ = sPE.m(VE.scalar_tensor_tensor(
                            out=dl_s, in0=P5[tb % 2][:, :], scalar=bcol,
                            in1=dh_s, op0=ALU.add, op1=ALU.subtract))
                        gset(f"P5:{tb % 2}", sPE)
                        qsl2 = slice(tb * QB, (tb + 1) * QB)
                        ds2 = ks2 if wi == 1 else qs2
                        pieces = [(ds2[0], 0, dh_s, 0), (ds2[0], 64, dl_s, 0),
                                  (ds2[1], 0, dh_s, 64), (ds2[1], 64, dl_s, 64)]
                        for dst_, dp, src_, sp in pieces:
                            mcp = VE.tensor_copy(dst_[dp:dp + 64, qsl2],
                                                 src_[sp:sp + 64, :])
                        sPE.m(mcp)
                gset("wsl:q", sMM)
                gset("wsl:k", sMM)
                wvbf = wsv.bitcast(BF16)
                wv3h = wvbf[:, 0:1024].rearrange("p (c o) -> p c o", c=8)
                wv3l = wvbf[:, 1024:2048].rearrange("p (c o) -> p c o", c=8)
                for tt in range(NTT):
                    pb = 2 + tt % 2
                    sl = P5[pb][:, 0:128]
                    gwait(TEE, f"P5:{pb}")
                    for wpart, xpart in ((0, 0), (0, 1), (1, 0)):
                        wv3 = wv3h if wpart == 0 else wv3l
                        for dc in range(8):
                            mi = TEE.matmul(
                                sl, xnslice(xpart, dc, tt * 128, 128),
                                wv3[:, dc, :],
                                start=(wpart == 0 and xpart == 0 and dc == 0),
                                stop=(wpart == 1 and dc == 7))
                    vmk = sMM.m(mi)
                    vb_ = vhpb[:, tt * 260:(tt + 1) * 260]
                    SC.wait_ge(sMM.h, vmk)
                    SC.activation(out=vb_[:, 0:64], in_=P5[pb][:, 0:64],
                                  func=AF.Copy)
                    hm_ = sSq.m(SC.activation(out=vb_[:, 65:129],
                                              in_=P5[pb][:, 64:128],
                                              func=AF.Copy))
                    VE.wait_ge(sSq.h, hm_)
                    VE.tensor_tensor(out=vb_[:, 130:194], in0=P5[pb][:, 0:64],
                                     in1=vb_[:, 0:64], op=ALU.subtract)
                    sPE.m(VE.tensor_tensor(out=vb_[:, 195:259],
                                           in0=P5[pb][:, 64:128],
                                           in1=vb_[:, 65:129], op=ALU.subtract))
                    gset(f"P5:{pb}", sPE)
                gset("wsl:v", sMM)
                qkv_done = sPE.n
                if hp == 0:
                    ckpt()

                TEE.wait_ge(sPE.h, qkv_done)
                if hp == 0:
                    TEE.wait_ge(sGp.h, ones_ready)
                    gwait(SC, "PAB1")
                pairs = [(hl, q) for hl in range(2) for q in range(NQB)]

                def build_sw(i2):
                    hl, q = pairs[i2]
                    ab = (hl * NQB + q) % 2
                    sw = ARS_bf[:, 24640 + ab * 512: 25152 + ab * 512]
                    gwait(VE, f"sw7:{ab}")
                    VE.tensor_copy(sw[0:64, :],
                                   qs2[hl][64:128, q * QB:(q + 1) * QB])
                    return sPE.m(VE.tensor_copy(
                        sw[64:128, :], qs2[hl][0:64, q * QB:(q + 1) * QB])), sw

                m7n, swn = build_sw(0)
                for i2, (hloc, qb) in enumerate(pairs):
                    if True:
                        m7, sw7 = m7n, swn
                        nkb = 4 * qb + 4
                        SB3 = [0, 1, 3]
                        avb = (hloc * NQB + qb) % 2
                        AVP = PAB[0][:, avb * 512:(avb + 1) * 512]
                        qsl = slice(qb * QB, (qb + 1) * QB)

                        def emit_scores(kb):
                            sb = SB3[kb % 3]
                            gwait(TEE, f"P5:{sb}")
                            dofs = kb - 4 * qb
                            ksl = slice(kb * 128, (kb + 1) * 128)
                            TEE.matmul(P5[sb][:, :], ks2[hloc][:, ksl],
                                       qs2[hloc][:, qsl], start=True, stop=False)
                            mi = TEE.matmul(P5[sb][:, :], ks2[hloc][:, ksl],
                                            sw7[:, :], start=False,
                                            stop=(dofs < 0))
                            if dofs >= 0:
                                mi = TEE.matmul(P5[sb][:, :], identb[:, :],
                                                maskb[:, dofs * QB:(dofs + 1) * QB],
                                                start=False, stop=True)
                            return sMM.m(mi)

                        gwait(TEE, f"AVP:{avb}")
                        TEE.wait_ge(sPE.h, m7)
                        smarks = [emit_scores(0)]
                        if nkb > 1:
                            smarks.append(emit_scores(1))
                        for kb in range(nkb):
                            sb = SB3[kb % 3]
                            pp3 = kb % 3
                            ppf = kb % 2
                            SC.wait_ge(sMM.h, smarks[kb])
                            gwait(SC, f"ptl:{pp3}")
                            gwait(SC, f"pfp:{ppf}")
                            pm1 = sP.m(SC.activation(
                                out=ph3[pp3], in_=P5[sb][:, :], func=AF.Exp,
                                scale=0.125, bias=consts[:, 0:1]))
                            pm2 = sP.m(SC.activation(
                                out=PAB[1][:, ppf * 512:(ppf + 1) * 512],
                                in_=P5[sb][:, :], func=AF.Exp, scale=0.125,
                                bias=consts[:, 0:1]))
                            gset(f"P5:{sb}", sP)
                            if kb + 2 < nkb:
                                smarks.append(emit_scores(kb + 2))
                            VE.wait_ge(sP.h, pm2)
                            plm = sXn.m(VE.tensor_tensor(
                                out=pl3[pp3],
                                in0=PAB[1][:, ppf * 512:(ppf + 1) * 512],
                                in1=ph3[pp3], op=ALU.subtract))
                            gset(f"pfp:{ppf}", sXn)
                            kbase = kb * 260 + hloc * 65
                            vh_ = vhpb[:, kbase: kbase + 65]
                            vl_ = vhpb[:, kbase + 130: kbase + 195]
                            TEE.wait_ge(sP.h, pm1)
                            TEE.matmul(AVP[0:65, :], vh_, ph3[pp3],
                                       start=(kb == 0), stop=False)
                            TEE.matmul(AVP[0:65, :], vl_, ph3[pp3],
                                       start=False, stop=False)
                            TEE.wait_ge(sXn.h, plm)
                            sMM.m(TEE.matmul(AVP[0:65, :], vh_, pl3[pp3],
                                             start=False, stop=(kb == nkb - 1)))
                            gset(f"ptl:{pp3}", sMM)
                        av_done = sMM.n
                        gset(f"sw7:{avb}", sMM)
                        if i2 + 1 < len(pairs):
                            m7n, swn = build_sw(i2 + 1)
                        VE.wait_ge(sMM.h, av_done)
                        VE.tensor_copy(
                            attT[hloc * 64:hloc * 64 + 64,
                                 hp * T + qb * QB: hp * T + (qb + 1) * QB],
                            AVP[0:64, :])
                        gwait(VE, "srowb")
                        srm = sPE.m(VE.tensor_copy(srowb[64:65, 0:512],
                                                   AVP[64:65, :]))
                        gset(f"AVP:{avb}", sPE)
                        GP.wait_ge(sPE.h, srm)
                        if s > 0 and sall_ld_prev[0] is not None:
                            GP.wait_ge(sLd.h, sall_ld_prev[0])
                        sSp.d(GP.dma_start(
                            out=sdram[2 * hp + hloc: 2 * hp + hloc + 1,
                                      qb * QB:(qb + 1) * QB],
                            in_=srowb[64:65, 0:512]))
                        gset("srowb", sSp)
                if hp == 0:
                    ckpt()

            ckpt()
            SY.wait_ge(sMM.h, sMM.n)
            wo_ld = sLdW.d(SY.dma_start(
                out=wo_full.rearrange("p (c d) -> p c d", c=8),
                in_=wT_in["o"].rearrange("(c p) d -> p c d", p=128)))
            if s == 0:
                bo_ld = sLdW.d(SY.dma_start(out=brows[64:65, :], in_=b_in["o"]))
                TEE.wait_ge(sLdW.h, bo_ld)
            for nb2 in range(2):
                gwait(TEE, f"P5:{nb2}")
                bobm = sMM.m(TEE.matmul(
                    P5[nb2][:, :], onesP[64:65, 0:128],
                    brows[64:65, nb2 * 512:(nb2 + 1) * 512],
                    start=True, stop=True))
            VE.wait_ge(sMM.h, bobm)
            VE.tensor_copy(bob[:, 0:512], P5[0][:, :])
            sPE.m(VE.tensor_copy(bob[:, 512:1024], P5[1][:, :]))
            gset("P5:0", sPE)
            gset("P5:1", sPE)
            woh = ARA_bf[:, 16384:24576]
            wol = ARA_bf[:, 24576:32768]
            VE.wait_ge(sLdW.h, wo_ld)
            VE.tensor_copy(woh, wo_full[:, :])
            wom = sPE.m(VE.tensor_tensor(out=wol, in0=wo_full[:, :], in1=woh,
                                         op=ALU.subtract))
            SY.wait_ge(sSp.h, sSp.n)
            sall_ld = sLd.d(SY.dma_start(out=sall, in_=sdram[:, :]))
            sall_ld_prev[0] = sall_ld
            VE.wait_ge(sLd.h, sall_ld)
            VE.wait_ge(sMM.h, sMM.n)
            VE.reciprocal(srec, sall)
            srh = ARS_bf[0:16, 4096:6144]
            srl = ARS_bf[0:16, 6144:8192]
            VE.tensor_copy(srh, srec)
            sMs.m(VE.tensor_tensor(out=srl, in0=srec, in1=srh,
                                   op=ALU.subtract))
            gset("wstage", sMs)
            ohb = ARS_bf[0:16, 27968:28096]
            for hp in range(8):
                gwait(VE, "oh")
                ohm = sMs.m(VE.tensor_scalar(out=ohb[:, :], in0=ohJ[:, :],
                                             scalar1=float(2 * hp), scalar2=None,
                                             op0=ALU.is_equal))
                for qb in range(NQB):
                    TEE.wait_ge(sMs.h, ohm)
                    gwait(TEE, "P5:3")
                    TEE.matmul(P5[3][:, :], ohb[:, :],
                               srh[:, qb * QB:(qb + 1) * QB],
                               start=True, stop=False)
                    bm = sMM.m(TEE.matmul(P5[3][:, :], ohb[:, :],
                                          srl[:, qb * QB:(qb + 1) * QB],
                                          start=False, stop=True))
                    VE.wait_ge(sMM.h, bm)
                    tdiv = ARS[:, 10784:11296]
                    abase = 2 * hp * T + qb * 1024
                    VE.tensor_tensor(
                        out=tdiv,
                        in0=attT[:, hp * T + qb * QB: hp * T + (qb + 1) * QB],
                        in1=P5[3][:, :], op=ALU.mult)
                    VE.tensor_scalar(
                        out=tdiv, in0=tdiv,
                        scalar1=bqkT[:, 16 + hp:17 + hp], scalar2=None,
                        op0=ALU.add)
                    VE.tensor_copy(ARB_bf[:, abase:abase + 512], tdiv)
                    sPE.m(VE.tensor_tensor(
                        out=ARB_bf[:, abase + 512:abase + 1024], in0=tdiv,
                        in1=ARB_bf[:, abase:abase + 512], op=ALU.subtract))
                    gset("P5:3", sPE)
                gset("oh", sMM)
            attT_done = sPE.n
            ckpt()

            s1_prev = None
            for tt in range(NTT):
                pb = tt % 2
                gwait(SY, f"xst:{pb}")
                xm2 = sLd.d(SY.dma_start(
                    out=xst[:, pb * D:(pb + 1) * D],
                    in_=x_in[s, tt * 128:(tt + 1) * 128, :]))
                TEE.wait_ge(sPE.h, max(attT_done, wom))
                gwait(TEE, f"PAB{pb}")

                def atts(oc, part):
                    base = (2 * oc * T + (tt // 4) * 1024 + part * 512
                            + (tt % 4) * 128)
                    return ARB_bf[:, base: base + 128]

                for nb in range(2):
                    for wpart, apart in ((0, 0), (0, 1), (1, 0)):
                        wo_ = woh if wpart == 0 else wol
                        for oc in range(8):
                            mi = TEE.matmul(
                                PAB[pb][:, nb * 512:(nb + 1) * 512],
                                atts(oc, apart),
                                wo_[:, oc * 1024 + nb * 512: oc * 1024 + (nb + 1) * 512],
                                start=(wpart == 0 and apart == 0 and oc == 0),
                                stop=(wpart == 1 and oc == 7))
                opm = sMM.m(mi)

                idx = s * NTT + tt
                _md = os.environ.get("K_OPROJ_MODE", "full")
                VE.wait_ge(sMM.h, opm)
                VE.wait_ge(sLd.h, xm2)
                if s1_prev is not None:
                    VE.wait_ge(sSq.h, s1_prev)
                gwait(VE, f"xnst:{pb}")
                VE.scalar_tensor_tensor(
                    out=xnst[:, pb * D:(pb + 1) * D],
                    in0=xst[:, pb * D:(pb + 1) * D],
                    scalar=r_all[:, idx:idx + 1], in1=nwb[:, :],
                    op0=ALU.mult, op1=ALU.mult)
                if _md == "pab":
                    VE.tensor_scalar(out=yt[:, :], in0=PAB[pb][:, :], scalar1=1.0,
                                     scalar2=None, op0=ALU.mult)
                else:
                    VE.tensor_tensor(out=yt[:, :], in0=PAB[pb][:, :],
                                     in1=bob[:, :], op=ALU.add)
                    VE.tensor_tensor(out=yt[:, :], in0=yt[:, :],
                                     in1=xst[:, pb * D:(pb + 1) * D], op=ALU.add)
                pe_m = sPE.m(VE.tensor_tensor(out=yb[:, :], in0=yt[:, :],
                                              in1=xnst[:, pb * D:(pb + 1) * D],
                                              op=ALU.add) if _md != "pab" else
                             VE.tensor_copy(yb[:, :], yt[:, :]))
                gset(f"PAB{pb}", sPE)
                gset(f"xnst:{pb}", sPE)
                SC.wait_ge(sPE.h, pe_m)
                SC.activation(out=yt[:, :], in_=yb[:, :], func=AF.Copy,
                              accum_out=stat[:, 4:5])
                SC.activation(out=yt[:, :], in_=yb[:, :], func=AF.Square,
                              accum_out=stat[:, 5:6])
                s1 = sSq.m(SC.activation(out=stat[:, 16:17], in_=stat[:, 4:5],
                                         func=AF.Copy))
                s1_prev = s1
                VE.wait_ge(sSq.h, s1)
                VE.tensor_scalar(out=stat[:, 6:7], in0=stat[:, 4:5],
                                 scalar1=INV1024, scalar2=None, op0=ALU.mult)
                vm1 = sMs.m(VE.tensor_scalar(out=stat[:, 7:8], in0=stat[:, 5:6],
                                             scalar1=INV1024, scalar2=None,
                                             op0=ALU.mult))
                SC.wait_ge(sMs.h, vm1)
                am1 = sSq.m(SC.activation(out=stat[:, 8:9], in_=stat[:, 6:7],
                                          func=AF.Square))
                VE.wait_ge(sSq.h, am1)
                v1 = sMs.m(VE.scalar_tensor_tensor(
                    out=stat[:, 10:11], in0=stat[:, 7:8], scalar=1e-5,
                    in1=stat[:, 8:9], op0=ALU.add, op1=ALU.subtract))
                SC.wait_ge(sMs.h, v1)
                SC.activation(out=stat[:, 11:12], in_=stat[:, 10:11], func=AF.Ln)
                r1 = sR.m(SC.activation(out=stat[:, 12:13], in_=stat[:, 11:12],
                                        func=AF.Exp, scale=-0.5))
                VE.wait_ge(sR.h, r1)
                if dbg_n is not None:
                    for ci, sc_ in enumerate((6, 12, 4, 5, 7, 8, 9, 10, 11)):
                        VE.tensor_copy(vm_all[:, 32 * ci + idx:32 * ci + idx + 1],
                                       stat[:, sc_:sc_ + 1])
                if _md in ("pab", "noln"):
                    VE.tensor_copy(xst[:, pb * D:(pb + 1) * D], yb[:, :])
                else:
                    VE.tensor_scalar(out=xst[:, pb * D:(pb + 1) * D], in0=yb[:, :],
                                     scalar1=stat[:, 6:7], scalar2=stat[:, 12:13],
                                     op0=ALU.subtract, op1=ALU.mult)
                x1m = sMs.m(VE.tensor_reduce(
                    out=m1_all[:, idx:idx + 1], in_=xst[:, pb * D:(pb + 1) * D],
                    axis=AX.X, op=ALU.max, apply_absolute_value=True))
                GP.wait_ge(sMs.h, x1m)
                sSp.d(GP.dma_start(
                    out=x1dram[idx * 128:(idx + 1) * 128, :],
                    in_=xst[:, pb * D:(pb + 1) * D]))
                gset(f"xst:{pb}", sSp)

            ckpt()

        if stage == "ln1":
            GP.wait_ge(sSp.h, sSp.n)
            for i4 in range(4):
                sOut.d(GP.dma_start(
                    out=out[i4 // 2, (i4 % 2) * 1024:(i4 % 2 + 1) * 1024, :],
                    in_=x1dram[i4 * 1024:(i4 + 1) * 1024, :]))
            GP.wait_ge(sOut.h, sOut.n)
            return nc

        barrier()
        g1m = sMs.m(VE.tensor_reduce(out=stat[:, 3:4], in_=m1_all[:, 0:NRT],
                                     axis=AX.X, op=ALU.max))
        GP.wait_ge(sMs.h, g1m)
        sSp.d(GP.dma_start(out=m128d, in_=stat[:, 3:4]))
        SY.wait_ge(sSp.h, sSp.n)
        sLd.d(SY.dma_start(out=rowt[:, :], in_=m128d))
        VE.wait_ge(sLd.h, sLd.n)
        gm1 = sMs.m(VE.tensor_reduce(out=gsc[0:1, 0:1], in_=rowt[:, :],
                                     axis=AX.X, op=ALU.max))
        GP.wait_ge(sMs.h, gm1)
        GP.tensor_copy(ccst[0:1, 0:1], gsc[0:1, 0:1])
        sSp.d(GP.dma_start(out=cc_in[0].ap(), in_=ccst[:, :]))
        GP.wait_ge(sSp.h, sSp.n)
        GP.collective_compute(
            "AllReduce", ALU.max, ins=[cc_in[0].ap().opt()],
            outs=[cc_out[0].ap().opt()],
            replica_groups=[[0, 1, 2, 3, 4, 5, 6, 7]]).then_inc(sGp.h, 1)
        sGp.n += 1
        GP.wait_ge(sGp.h, sGp.n)
        sSp.d(GP.dma_start(out=rowt[0:1, 0:64], in_=cc_out[0].ap()))
        SC.wait_ge(sSp.h, sSp.n)
        SC.activation(out=gsc[0:1, 5:6], in_=rowt[0:1, 0:1], func=AF.Copy,
                      scale=sig2[0:1, 0:1])
        a1g = sSq.m(SC.activation(out=gsc[0:1, 3:4], in_=gsc[0:1, 5:6],
                                  func=AF.Copy, scale=INV127))
        VE.wait_ge(sSq.h, a1g)
        v1g = sMs.m(VE.reciprocal(gsc[0:1, 8:9], gsc[0:1, 3:4]))
        SC.wait_ge(sMs.h, v1g)
        scm = sSq.m(SC.activation(out=gsc[0:1, 4:5], in_=gsc[0:1, 8:9],
                                  func=AF.Copy))
        TEE.wait_ge(sSq.h, scm)
        bcm = sMM.m(TEE.matmul(P5[0][:, 0:2], onesP[0:1, 0:128], gsc[0:1, 3:5],
                               start=True, stop=True))
        VE.wait_ge(sMM.h, bcm)
        sPE.m(VE.tensor_copy(scbc[:, 0:2], P5[0][:, 0:2]))
        scx_ready = sPE.n
        ckpt()

        barrier()
        gates.pop("P5:0", None); gates.pop("P5:1", None)
        gates.pop("xst:0", None); gates.pop("xst:1", None)
        xq3 = xqT_bf.rearrange("p (c r) -> p c r", r=R)
        for t32 in range(NRT):
            pb = t32 % 2
            gwait(SY, f"xst:{pb}")
            xm = sLd.d(SY.dma_start(
                out=xst[:, pb * D:(pb + 1) * D],
                in_=x1dram[t32 * 128:(t32 + 1) * 128, :]))
            TEE.wait_ge(sLd.h, xm)
            hmarks = []
            for half in range(2):
                bk = (2 * t32 + half) % 4
                gwait(TEE, f"P5:{bk}")
                for k4 in range(4):
                    dc = half * 4 + k4
                    mtr = sTr.m(TEE.transpose(
                        P5[bk][:, k4 * 128:(k4 + 1) * 128],
                        xst[:, pb * D + dc * 128: pb * D + (dc + 1) * 128],
                        identf[:, :]))
                hmarks.append(mtr)
            gset(f"xst:{pb}", sTr)
            for half in range(2):
                bk = (2 * t32 + half) % 4
                VE.wait_ge(sTr.h, hmarks[half])
                VE.wait_ge(sPE.h, scx_ready)
                gwait(VE, "wq")
                VE.tensor_scalar(out=wq_u[:, 0:512], in0=P5[bk][:, :],
                                 scalar1=scbc[:, 1:2], scalar2=0.5,
                                 op0=ALU.mult, op1=ALU.add)
                VE.tensor_copy(wq_t1i[:, 0:512], wq_u[:, 0:512])
                VE.tensor_copy(wq_t2[:, 0:512], wq_t1i[:, 0:512])
                VE.tensor_tensor(out=wq_t1[:, 0:512], in0=wq_t2[:, 0:512],
                                 in1=wq_u[:, 0:512], op=ALU.is_gt)
                VE.tensor_tensor(out=wq_rnd[:, 0:512], in0=wq_t2[:, 0:512],
                                 in1=wq_t1[:, 0:512], op=ALU.subtract)
                qm2 = sMs.m(VE.tensor_scalar(
                    out=xq3[:, half * 4:(half + 1) * 4,
                            t32 * 128:(t32 + 1) * 128],
                    in0=wq_rnd[:, 0:512].rearrange("p (c r) -> p c r", r=128),
                    scalar1=-128.0, scalar2=127.0, op0=ALU.max, op1=ALU.min))
                gset(f"P5:{bk}", sMs, qm2)
                gset("wq", sMs, qm2)
        xqT_ready = sMs.n
        ckpt()

        def prepB(j):
            bb = j % 2
            gwait(SY, "wstg")
            sLd.d(SY.dma_start(out=wstg[:, 0:1024],
                               in_=wg_in[j * 128:(j + 1) * 128, :]))
            l2 = sLd.d(SY.dma_start(out=wstg[:, 1024:2048],
                                    in_=wu_in[j * 128:(j + 1) * 128, :]))
            VE.wait_ge(sLd.h, l2)
            VE.tensor_reduce(out=stat[:, 17:18], in_=wstg[:, 0:1024],
                             axis=AX.X, op=ALU.max, apply_absolute_value=True)
            vr = sMs.m(VE.tensor_reduce(
                out=stat[:, 18:19], in_=wstg[:, 1024:2048],
                axis=AX.X, op=ALU.max, apply_absolute_value=True))
            SC.wait_ge(sMs.h, vr)
            ar = sSq.m(SC.activation(out=swg[:, 2 * bb:2 * bb + 2],
                                     in_=stat[:, 17:19], func=AF.Copy,
                                     scale=INV127))
            VE.wait_ge(sSq.h, ar)
            vr2 = sMs.m(VE.reciprocal(stat[:, 17:19], swg[:, 2 * bb:2 * bb + 2]))
            SC.wait_ge(sMs.h, vr2)
            ar2 = sSq.m(SC.activation(out=swu[:, 2 * bb:2 * bb + 2],
                                      in_=stat[:, 17:19], func=AF.Copy))
            VE.wait_ge(sSq.h, ar2)
            gwait(VE, "wq")
            qms = []
            for wi in range(2):
                sl = slice(wi * 1024, (wi + 1) * 1024)
                VE.tensor_scalar(out=wq_u[:, 0:1024], in0=wstg[:, sl],
                                 scalar1=swu[:, 2 * bb + wi:2 * bb + wi + 1],
                                 scalar2=0.5, op0=ALU.mult, op1=ALU.add)
                VE.tensor_copy(wq_t1i[:, 0:1024], wq_u[:, 0:1024])
                VE.tensor_copy(wq_t2[:, 0:1024], wq_t1i[:, 0:1024])
                VE.tensor_tensor(out=wq_t1[:, 0:1024], in0=wq_t2[:, 0:1024],
                                 in1=wq_u[:, 0:1024], op=ALU.is_gt)
                qms.append(sMs.m(VE.tensor_tensor(
                    out=wq_rnd[:, wi * 1024:(wi + 1) * 1024],
                    in0=wq_t2[:, 0:1024], in1=wq_t1[:, 0:1024], op=ALU.subtract)))
            gset("wstg", sMs)
            return ar, qms

        def prepB_T(j, qms):
            bb = j % 2
            for wi, dstT in ((0, wgT), (1, wuT)):
                for dc in range(8):
                    TEE.wait_ge(sMs.h, qms[wi])
                    gwait(TEE, f"P5:{dc % 2}")
                    mtr = sTr.m(TEE.transpose(
                        P5[dc % 2][:, 0:128],
                        wq_rnd[:, wi * 1024 + dc * 128: wi * 1024 + (dc + 1) * 128],
                        identf[:, :]))
                    VE.wait_ge(sTr.h, mtr)
                    gwait(VE, f"wgT:{bb}")
                    sEv.m(VE.tensor_copy(
                        dstT[bb][:, dc * 128:(dc + 1) * 128], P5[dc % 2][:, 0:128]))
                    gset(f"P5:{dc % 2}", sEv)
            gset("wq", sEv)
            return sEv.n

        ar_p, qms_p = prepB(0)
        wready_p = prepB_T(0, qms_p)
        for ib in range(NIC):
            bb = ib % 2
            armark, wready = ar_p, wready_p
            VE.wait_ge(sPE.h, scx_ready)
            VE.wait_ge(sSq.h, armark)
            VE.tensor_tensor(out=sgu[:, 2 * bb:2 * bb + 1],
                             in0=swg[:, 2 * bb:2 * bb + 1], in1=scbc[:, 0:1],
                             op=ALU.mult)
            sgm = sMs.m(VE.tensor_tensor(out=sgu[:, 2 * bb + 1:2 * bb + 2],
                                         in0=swg[:, 2 * bb + 1:2 * bb + 2],
                                         in1=scbc[:, 0:1], op=ALU.mult))
            if ib + 1 < NIC:
                ar_p, qms_p = prepB(ib + 1)
            for tb in range(8):
                pp = tb % 2
                TEE.wait_ge(sEv.h, wready)
                TEE.wait_ge(sMs.h, xqT_ready)
                gwait(TEE, f"PAB{pp}")
                for half, wT_ in ((0, wgT), (1, wuT)):
                    for dc in range(8):
                        mi = TEE.matmul(
                            PAB[pp][:, half * 512:(half + 1) * 512],
                            wT_[bb][:, dc * 128:(dc + 1) * 128],
                            xqT_bf[:, dc * R + tb * 512: dc * R + (tb + 1) * 512],
                            start=(dc == 0), stop=(dc == 7))
                gm_ = sMM.m(mi)
                gset(f"wgT:{bb}", sMM)
                SC.wait_ge(sMM.h, gm_)
                SC.wait_ge(sMs.h, sgm)
                gwait(SC, f"atl:{pp}")
                am = sSq.m(SC.activation(out=atl[pp], in_=PAB[pp][:, 0:512],
                                         func=AF.Silu,
                                         scale=sgu[:, 2 * bb:2 * bb + 1]))
                VE.wait_ge(sSq.h, am)
                gwait(VE, f"vtl:{pp}")
                VE.scalar_tensor_tensor(out=vtl[pp], in0=PAB[pp][:, 512:1024],
                                        scalar=sgu[:, 2 * bb + 1:2 * bb + 2],
                                        in1=atl[pp], op0=ALU.mult, op1=ALU.mult)
                gset(f"atl:{pp}", sMs, sMs.n + 1)
                vm = sMs.m(VE.tensor_reduce(
                    out=vm_all[:, ib * 8 + tb: ib * 8 + tb + 1], in_=vtl[pp],
                    axis=AX.X, op=ALU.max, apply_absolute_value=True))
                gates[f"PAB{pp}"] = (sMs.h, vm)
                GP.wait_ge(sMs.h, vm)
                sSp.d(GP.dma_start(
                    out=vdram[ib * 128:(ib + 1) * 128, tb * 512:(tb + 1) * 512],
                    in_=vtl[pp]))
                gset(f"vtl:{pp}", sSp)
            if ib + 1 < NIC:
                wready_p = prepB_T(ib + 1, qms_p)

        ckpt()
        if stage == "ffnv":
            GP.wait_ge(sSp.h, sSp.n)
            for i4 in range(4):
                sOut.d(GP.dma_start(
                    out=out[i4 // 2, (i4 % 2) * 1024:(i4 % 2 + 1) * 1024, :],
                    in_=x1dram[i4 * 1024:(i4 + 1) * 1024, :]))
            for ic in range(NIC):
                sOut.d(GP.dma_start(out=dbgv[ic * 128:(ic + 1) * 128, :],
                                    in_=vdram[ic * 128:(ic + 1) * 128, :]))
            GP.wait_ge(sOut.h, sOut.n)
            return nc

        barrier()
        g2m = sMs.m(VE.tensor_reduce(out=stat[:, 3:4], in_=vm_all[:, 0:NIC * 8],
                                     axis=AX.X, op=ALU.max))
        GP.wait_ge(sMs.h, g2m)
        sSp.d(GP.dma_start(out=m128d, in_=stat[:, 3:4]))
        SY.wait_ge(sSp.h, sSp.n)
        sLd.d(SY.dma_start(out=rowt[:, :], in_=m128d))
        VE.wait_ge(sLd.h, sLd.n)
        gm2 = sMs.m(VE.tensor_reduce(out=gsc[0:1, 8:9], in_=rowt[:, :],
                                     axis=AX.X, op=ALU.max))
        GP.wait_ge(sMs.h, gm2)
        GP.tensor_copy(ccst[0:1, 0:1], gsc[0:1, 8:9])
        sSp.d(GP.dma_start(out=cc_in[1].ap(), in_=ccst[:, :]))
        GP.wait_ge(sSp.h, sSp.n)
        GP.collective_compute(
            "AllReduce", ALU.max, ins=[cc_in[1].ap().opt()],
            outs=[cc_out[1].ap().opt()],
            replica_groups=[[0, 1, 2, 3, 4, 5, 6, 7]]).then_inc(sGp.h, 1)
        sGp.n += 1
        GP.wait_ge(sGp.h, sGp.n)
        sSp.d(GP.dma_start(out=rowt[0:1, 0:64], in_=cc_out[1].ap()))
        SC.wait_ge(sSp.h, sSp.n)
        SC.activation(out=gsc[0:1, 13:14], in_=rowt[0:1, 0:1], func=AF.Copy,
                      scale=sig2[0:1, 1:2])
        a2g = sSq.m(SC.activation(out=gsc[0:1, 11:12], in_=gsc[0:1, 13:14],
                                  func=AF.Copy, scale=INV127))
        VE.wait_ge(sSq.h, a2g)
        v2g = sMs.m(VE.reciprocal(gsc[0:1, 14:15], gsc[0:1, 11:12]))
        SC.wait_ge(sMs.h, v2g)
        scm2 = sSq.m(SC.activation(out=gsc[0:1, 12:13], in_=gsc[0:1, 14:15],
                                   func=AF.Copy))
        TEE.wait_ge(sSq.h, scm2)
        bcm2 = sMM.m(TEE.matmul(P5[0][:, 0:2], onesP[0:1, 0:128], gsc[0:1, 11:13],
                                start=True, stop=True))
        VE.wait_ge(sMM.h, bcm2)
        sPE.m(VE.tensor_copy(scbc[:, 2:4], P5[0][:, 0:2]))
        sch_ready = sPE.n
        ckpt()

        barrier()
        gates.pop("wstg", None); gates.pop("wq", None)
        gates.pop("P5:0", None); gates.pop("P5:1", None)
        gates.pop("xst:0", None); gates.pop("xst:1", None)
        for ds in range(8):
            gwait(SY, "wstg")
            l1 = sLd.d(SY.dma_start(out=wstg[:, :],
                                    in_=wd_in[ds * 128:(ds + 1) * 128, :]))
            VE.wait_ge(sLd.h, l1)
            gwait(VE, "wq")
            vr = sMs.m(VE.tensor_reduce(out=stat[:, 17:18], in_=wstg[:, :],
                                        axis=AX.X, op=ALU.max,
                                        apply_absolute_value=True))
            SC.wait_ge(sMs.h, vr)
            ar = sSq.m(SC.activation(out=sdall[:, ds:ds + 1], in_=stat[:, 17:18],
                                     func=AF.Copy, scale=INV127))
            VE.wait_ge(sSq.h, ar)
            vr2 = sMs.m(VE.reciprocal(stat[:, 18:19], sdall[:, ds:ds + 1]))
            SC.wait_ge(sMs.h, vr2)
            ar2 = sSq.m(SC.activation(out=stat[:, 13:14], in_=stat[:, 18:19],
                                      func=AF.Copy))
            VE.wait_ge(sSq.h, ar2)
            VE.tensor_scalar(out=wq_u[:, :], in0=wstg[:, :],
                             scalar1=stat[:, 13:14], scalar2=0.5,
                             op0=ALU.mult, op1=ALU.add)
            gset("wstg", sMs, sMs.n + 1)
            VE.tensor_copy(wq_t1i[:, :], wq_u[:, :])
            VE.tensor_copy(wq_t2[:, :], wq_t1i[:, :])
            VE.tensor_tensor(out=wq_t1[:, :], in0=wq_t2[:, :], in1=wq_u[:, :],
                             op=ALU.is_gt)
            qm = sMs.m(VE.tensor_tensor(out=wq_rnd[:, :], in0=wq_t2[:, :],
                                        in1=wq_t1[:, :], op=ALU.subtract))
            for ic in range(NIC):
                TEE.wait_ge(sMs.h, qm)
                gwait(TEE, f"P5:{ic % 2}")
                mtr = sTr.m(TEE.transpose(P5[ic % 2][:, 0:128],
                                          wq_rnd[:, ic * 128:(ic + 1) * 128],
                                          identf[:, :]))
                VE.wait_ge(sTr.h, mtr)
                sEv.m(VE.tensor_copy(
                    wdT_bf[:, ic * 1024 + ds * 128: ic * 1024 + (ds + 1) * 128],
                    P5[ic % 2][:, 0:128]))
                gset(f"P5:{ic % 2}", sEv)
            gset("wq", sEv)
        wdT_ready = sEv.n
        GP.wait_ge(sEv.h, wdT_ready)
        sSp.d(GP.dma_start(out=rsdd.rearrange("a (c p) -> (a p) c", p=128),
                           in_=sdall[:, 0:8], allow_slow_non_contiguous=True))
        SY.wait_ge(sSp.h, sSp.n)
        rl = sLd.d(SY.dma_start(out=nwrow[:, :], in_=rsdd))
        TEE.wait_ge(sLd.h, rl)
        gwait(TEE, "P5:0")
        TEE.matmul(P5[0][:, :], onesP[0:1, 0:128], nwrow[0:1, 0:512],
                   start=True, stop=True)
        gwait(TEE, "P5:1")
        sdm = sMM.m(TEE.matmul(P5[1][:, :], onesP[0:1, 0:128], nwrow[0:1, 512:1024],
                               start=True, stop=True))
        VE.wait_ge(sMM.h, sdm)
        VE.tensor_copy(sdb[:, 0:512], P5[0][:, :])
        sdbm = sPE.m(VE.tensor_copy(sdb[:, 512:1024], P5[1][:, :]))
        gset("P5:0", sPE)
        gset("P5:1", sPE)
        ckpt()

        vstgB = [vstg, ARS[:, 8448:11264]]
        htileB = [htile, ARA_bf[:, 27392:30208]]
        sLdVB = [sLdV, sLdV2]

        def ldV(i):
            pv = i % 2
            gwait(SY, f"vstg:{pv}")
            if i < 2:
                gwait(SY, "wq")
                gwait(SY, "wstg")
            return sLdVB[pv].d(SY.dma_start(
                out=vstgB[pv].rearrange("p (c t) -> p c t", c=NIC),
                in_=vdram[:, i * 128:(i + 1) * 128]
                .rearrange("(c p) t -> p c t", p=128)))

        def ldX(i):
            pv = i % 2
            gwait(SY, f"xst:{pv}")
            return sLd.d(SY.dma_start(
                out=xst[:, pv * D:(pv + 1) * D],
                in_=x1dram[i * 128:(i + 1) * 128, :]))

        def quantC(i, vm_):
            pv = i % 2
            VE.wait_ge(sLdVB[pv].h, vm_)
            VE.wait_ge(sPE.h, sch_ready)
            gwait(VE, f"htile:{pv}")
            VE.tensor_scalar(out=hq_u[:, :], in0=vstgB[pv], scalar1=scbc[:, 3:4],
                             scalar2=0.5, op0=ALU.mult, op1=ALU.add)
            gset(f"vstg:{pv}", sMs, sMs.n + 1)
            VE.tensor_copy(hq_t1i[:, :], hq_u[:, :])
            VE.tensor_copy(hq_t2[:, :], hq_t1i[:, :])
            VE.tensor_tensor(out=hq_t1[:, :], in0=hq_t2[:, :], in1=hq_u[:, :],
                             op=ALU.is_gt)
            VE.tensor_tensor(out=hq_u[:, :], in0=hq_t2[:, :], in1=hq_t1[:, :],
                             op=ALU.subtract)
            return sMs.m(VE.tensor_scalar(out=htileB[pv], in0=hq_u[:, :],
                                          scalar1=-128.0, scalar2=127.0,
                                          op0=ALU.max, op1=ALU.min))

        s2_prev = None
        vlds = {0: ldV(0), 1: ldV(1)}
        xms = {0: ldX(0), 1: ldX(1)}
        hm_p = quantC(0, vlds[0])
        for t32 in range(NRT):
            pb = t32 % 2
            s_, tt = t32 // NTT, t32 % NTT
            hm, xm_ = hm_p, xms[t32]
            if t32 + 1 < NRT:
                hm_p = quantC(t32 + 1, vlds[t32 + 1])
            TEE.wait_ge(sMs.h, hm)
            TEE.wait_ge(sEv.h, wdT_ready)
            TEE.wait_ge(sPE.h, sdbm)
            gwait(TEE, f"PAB{pb}")
            for nb in range(2):
                for ic in range(NIC):
                    mi = TEE.matmul(
                        PAB[pb][:, nb * 512:(nb + 1) * 512],
                        htileB[pb][:, ic * 128:(ic + 1) * 128],
                        wdT_bf[:, ic * 1024 + nb * 512: ic * 1024 + (nb + 1) * 512],
                        start=(ic == 0), stop=(ic == NIC - 1))
            dm = sMM.m(mi)
            gset(f"htile:{pb}", sMM)
            _mc = os.environ.get("K_C_MODE", "full")
            VE.wait_ge(sMM.h, dm)
            VE.wait_ge(sLd.h, xm_)
            if s2_prev is not None:
                VE.wait_ge(sSq.h, s2_prev)
            VE.scalar_tensor_tensor(out=yb[:, :], in0=PAB[pb][:, :],
                                    scalar=scbc[:, 2:3], in1=sdb[:, :],
                                    op0=ALU.mult, op1=ALU.mult)
            gset(f"PAB{pb}", sMs, sMs.n + 1)
            pe2 = sMs.m(VE.tensor_tensor(out=yt[:, :], in0=yb[:, :],
                                          in1=xst[:, pb * D:(pb + 1) * D],
                                          op=ALU.add) if _mc == "full" else
                        VE.tensor_copy(yt[:, :], yb[:, :]))
            SC.wait_ge(sMs.h, pe2)
            SC.activation(out=yb[:, :], in_=yt[:, :], func=AF.Copy,
                          accum_out=stat[:, 4:5])
            SC.activation(out=yb[:, :], in_=yt[:, :], func=AF.Square,
                          accum_out=stat[:, 5:6])
            s2 = sSq.m(SC.activation(out=stat[:, 16:17], in_=stat[:, 4:5],
                                     func=AF.Copy))
            s2_prev = s2
            VE.wait_ge(sSq.h, s2)
            VE.tensor_scalar(out=stat[:, 6:7], in0=stat[:, 4:5],
                             scalar1=INV1024, scalar2=None, op0=ALU.mult)
            vm1c = sMs.m(VE.tensor_scalar(out=stat[:, 7:8], in0=stat[:, 5:6],
                                          scalar1=INV1024, scalar2=None,
                                          op0=ALU.mult))
            SC.wait_ge(sMs.h, vm1c)
            am1c = sSq.m(SC.activation(out=stat[:, 8:9], in_=stat[:, 6:7],
                                       func=AF.Square))
            VE.wait_ge(sSq.h, am1c)
            v2 = sMs.m(VE.scalar_tensor_tensor(
                out=stat[:, 10:11], in0=stat[:, 7:8], scalar=1e-5,
                in1=stat[:, 8:9], op0=ALU.add, op1=ALU.subtract))
            SC.wait_ge(sMs.h, v2)
            SC.activation(out=stat[:, 11:12], in_=stat[:, 10:11], func=AF.Ln)
            r2 = sR.m(SC.activation(out=stat[:, 12:13], in_=stat[:, 11:12],
                                    func=AF.Exp, scale=-0.5))
            VE.wait_ge(sR.h, r2)
            om = sMs.m(VE.tensor_scalar(out=xst[:, pb * D:(pb + 1) * D],
                                        in0=yt[:, :],
                                        scalar1=stat[:, 6:7], scalar2=stat[:, 12:13],
                                        op0=ALU.subtract, op1=ALU.mult)
                       if _mc == "full" else
                       VE.tensor_copy(xst[:, pb * D:(pb + 1) * D], yt[:, :]))
            GP.wait_ge(sMs.h, om)
            sOut.d(GP.dma_start(out=out[s_, tt * 128:(tt + 1) * 128, :],
                                in_=xst[:, pb * D:(pb + 1) * D]))
            gset(f"xst:{pb}", sOut)
            if t32 + 2 < NRT:
                vlds[t32 + 2] = ldV(t32 + 2)
                xms[t32 + 2] = ldX(t32 + 2)
            if t32 == int(os.environ.get("K_T32", "0")):
                ckpt()

        if stage == "full2":
            for ic in range(NIC):
                sOut.d(GP.dma_start(out=dbgv[ic * 128:(ic + 1) * 128, :],
                                    in_=vdram[ic * 128:(ic + 1) * 128, :]))
            for i4 in range(4):
                sOut.d(GP.dma_start(out=dbgX2[i4 * 1024:(i4 + 1) * 1024, :],
                                    in_=x1dram[i4 * 1024:(i4 + 1) * 1024, :]))
        GP.wait_ge(sOut.h, sOut.n)
      except StopBuild:
        for s_ in ALLSEMS:
            GP.wait_ge(s_.h, s_.n)
        sOut.d(GP.dma_start(out=dbgA, in_=ARA[:, :]))
        sOut.d(GP.dma_start(out=dbgB, in_=ARB[:, :]))
        sOut.d(GP.dma_start(out=dbgS, in_=ARS[:, :]))
        sOut.d(GP.dma_start(out=dbgX, in_=x1dram))
        sOut.d(GP.dma_start(out=dbgSd, in_=sdram))
        sOut.d(GP.dma_start(out=dbgM, in_=vm_all[:, 0:320]))
        sOut.d(GP.dma_start(out=dbgW[:, 0:8], in_=scbc[:, :]))
        sOut.d(GP.dma_start(out=dbgW[:, 8:16], in_=sdall[:, :]))
        sOut.d(GP.dma_start(out=dbgW[:, 16:36], in_=stat[:, :]))
        sOut.d(GP.dma_start(out=dbgW[0:1, 36:48], in_=gsc[0:1, 0:12]))
        GP.wait_ge(sOut.h, sOut.n)

    return nc


def kernel(**inputs):
    x = np.ascontiguousarray(np.asarray(inputs["L0_attn_x"], dtype=np.float32))
    nw = np.asarray(inputs["L0_attn_norm_w"], dtype=np.float32).reshape(1, D)
    Ws = {w: np.ascontiguousarray(
        np.asarray(inputs[f"L0_attn_W{w}"], dtype=np.float32).T) for w in "qkvo"}
    bs = {w: np.asarray(inputs[f"L0_attn_b{w}"], dtype=np.float32).reshape(1, D)
          for w in "qkvo"}
    clips = np.array([[np.asarray(inputs["L0_ffn_clip_x"], np.float32).ravel()[0],
                       np.asarray(inputs["L0_ffn_clip_down"], np.float32).ravel()[0]]],
                     dtype=np.float32)
    wg = np.ascontiguousarray(np.asarray(inputs["L0_ffn_w_gate"], np.float32))
    wu = np.ascontiguousarray(np.asarray(inputs["L0_ffn_w_up"], np.float32))
    wd = np.ascontiguousarray(np.asarray(inputs["L0_ffn_w_down"], np.float32))

    nc = build_program("full")
    in_maps = []
    for c in range(8):
        m = {"x": np.ascontiguousarray(x[2 * c:2 * c + 2]),
             "norm_w": nw, "clips": clips,
             "w_gate": wg, "w_up": wu, "w_down": wd}
        for w in "qkvo":
            m[f"w{w}T"] = Ws[w]
            m[f"b{w}"] = bs[w]
        in_maps.append(m)
    res = run_bass_kernel_spmd(nc, in_maps, core_ids=list(range(8)))
    return np.concatenate([np.asarray(r["out"]) for r in res.results], axis=0)


# version 20 (final)
# speedup vs baseline: 1.0721x; 1.0011x over previous
import os
import numpy as np
import ml_dtypes
import concourse.bass as bass
import concourse.mybir as mybir
from concourse.bass_utils import run_bass_kernel_spmd

F32 = mybir.dt.float32
BF16 = mybir.dt.bfloat16
I32 = mybir.dt.int32
AF = mybir.ActivationFunctionType
ALU = mybir.AluOpType
AX = mybir.AxisListType

T = 2048
NSEQ = 2
D = 1024
II = 2816
NIC = II // 128
NTT = T // 128
R = NSEQ * T
NRT = R // 128
QB = 512
NQB = T // QB
INV127 = float(np.float32(1.0) / np.float32(127.0))
INV1024 = float(np.float32(1.0) / np.float32(1024.0))


class Sem:
    def __init__(self, nc, name):
        self.h = nc.alloc_semaphore(name)
        self.n = 0

    def d(self, instr):
        instr.then_inc(self.h, 16)
        self.n += 16
        return self.n

    def m(self, instr):
        instr.then_inc(self.h, 1)
        self.n += 1
        return self.n


class StopBuild(Exception):
    pass


def build_program(stage="full"):
    nc = bass.Bass("TRN2", target_bir_lowering=False,
                   detect_race_conditions=False)
    SY, TEE, VE, SC, GP = nc.sync, nc.tensor, nc.vector, nc.scalar, nc.gpsimd

    x_in = nc.dram_tensor("x", [NSEQ, T, D], F32, kind="ExternalInput").ap()
    nw_in = nc.dram_tensor("norm_w", [1, D], F32, kind="ExternalInput").ap()
    wT_in = {w: nc.dram_tensor(f"w{w}T", [D, D], F32, kind="ExternalInput").ap()
             for w in "qkvo"}
    b_in = {w: nc.dram_tensor(f"b{w}", [1, D], F32, kind="ExternalInput").ap()
            for w in "qkvo"}
    clip_in = nc.dram_tensor("clips", [1, 2], F32, kind="ExternalInput").ap()
    wg_in = nc.dram_tensor("w_gate", [II, D], F32, kind="ExternalInput").ap()
    wu_in = nc.dram_tensor("w_up", [II, D], F32, kind="ExternalInput").ap()
    wd_in = nc.dram_tensor("w_down", [D, II], F32, kind="ExternalInput").ap()
    out = nc.dram_tensor("out", [NSEQ, T, D], F32, kind="ExternalOutput").ap()
    if stage in ("ln1", "ffnv", "full2"):
        dbgv = nc.dram_tensor("dbgv", [II, R], F32, kind="ExternalOutput").ap()
    if stage == "full2":
        dbgX2 = nc.dram_tensor("dbgX2", [R, D], F32, kind="ExternalOutput").ap()
    dbg_n = int(stage[4:]) if isinstance(stage, str) and stage.startswith("dbg:") else None
    if dbg_n is not None:
        dbgA = nc.dram_tensor("dbgA", [128, 16384], F32, kind="ExternalOutput").ap()
        dbgB = nc.dram_tensor("dbgB", [128, 16384], F32, kind="ExternalOutput").ap()
        dbgS = nc.dram_tensor("dbgS", [128, 14336], F32, kind="ExternalOutput").ap()
        dbgX = nc.dram_tensor("dbgX", [R, D], F32, kind="ExternalOutput").ap()
        dbgSd = nc.dram_tensor("dbgSd", [16, T], F32, kind="ExternalOutput").ap()
        dbgM = nc.dram_tensor("dbgM", [128, 320], F32, kind="ExternalOutput").ap()
        dbgW = nc.dram_tensor("dbgW", [128, 48], F32, kind="ExternalOutput").ap()

    x1dram = nc.dram_tensor("x1dram", [R, D], F32).ap()
    vdram = nc.dram_tensor("vdram", [II, R], F32).ap()
    sdram = nc.dram_tensor("sdram", [16, T], F32).ap()
    m128d = nc.dram_tensor("m128d", [1, 128], F32).ap()
    rsdd = nc.dram_tensor("rsdd", [1, D], F32).ap()
    cc_in = [nc.dram_tensor(f"cc_in{i}", [1, 64], F32) for i in range(2)]
    cc_out = [nc.dram_tensor(f"cc_out{i}", [1, 64], F32) for i in range(2)]

    A = nc.alloc_sbuf_tensor
    ARA = A("ARA", [128, 16384], F32)
    ARB = A("ARB", [128, 16384], F32)
    ARS = A("ARS", [128, 14336], F32)
    ARA_bf = ARA.bitcast(BF16)
    ARA_i = ARA.bitcast(I32)
    ARB_bf = ARB.bitcast(BF16)
    ARS_bf = ARS.bitcast(BF16)
    ARS_i = ARS.bitcast(I32)

    xnT = ARA
    attT = ARB
    qT = ARS[:, 0:2048]
    kT = ARS[:, 2048:4096]
    qs2 = [ARS_bf[:, 0:2048], ARS_bf[:, 2048:4096]]
    ks2 = [ARS_bf[:, 4096:6144], ARS_bf[:, 6144:8192]]
    dh_s = ARS_bf[:, 19520:20032]
    dl_s = ARS_bf[:, 20032:20544]
    wstage = ARS[:, 11296:12320]
    a1scr = ARS[:, 2048:3072]
    xnT_bf = ARA_bf
    vhp = ARS[:, 4096:6176]
    wsq = ARS[:, 6176:7200]
    wsk = ARS[:, 7200:8224]
    wsv = ARS[:, 8224:9248]
    srowb = ARS[:, 9248:9760]
    ph3 = [ARS_bf[:, 19520:20032], ARS_bf[:, 20544:21056],
           ARS_bf[:, 21568:22080]]
    pl3 = [ARS_bf[:, 20032:20544], ARS_bf[:, 21056:21568],
           ARS_bf[:, 22080:22592]]
    vhpb = ARS_bf[:, 8192:12352]
    sall = ARS[0:16, 10784:12832]
    srec = ARS[0:16, 0:2048]
    wo_full = ARA[:, 0:8192]

    xqT_bf = ARA_bf
    wdT_bf = ARB_bf
    wstg = ARB[:, 11264:14080]
    vtl = [ARB[:, 14080 + b * 512:14592 + b * 512] for b in range(2)]
    wq_u = ARS[:, 0:2816]
    wq_t1 = ARS[:, 2816:5632]
    wq_t1i = ARS_i[:, 2816:5632]
    wq_t2 = ARS[:, 5632:8448]
    wq_rnd = ARS[:, 8448:11264]
    wgT = [ARS_bf[:, 22528 + b * 1024:23552 + b * 1024] for b in range(2)]
    wuT = [ARS_bf[:, 24576 + b * 1024:25600 + b * 1024] for b in range(2)]
    atl = [ARS[:, 13312 + b * 512:13824 + b * 512] for b in range(2)]

    yt = ARS[:, 2048:3072]
    yb = ARS[:, 3072:4096]
    xnst = ARS[:, 4096:6144]
    maskb = ARS_bf[:, 25664:27712]
    mask_i = ARS[:, 0:512]
    iden_i = ARS[:, 512:640]
    nwrow = ARS[0:1, 6176:7200]
    rowt = ARS[0:1, 13856:13984]
    oh = ARS[0:16, 13984:14112]
    ohJ = ARS[0:16, 14112:14240]

    vstg = ARA[:, 0:2816]
    hq_u = ARA[:, 2816:5632]
    hq_t1 = ARA[:, 5632:8448]
    hq_t1i = ARA_i[:, 5632:8448]
    hq_t2 = ARA[:, 8448:11264]
    htile = ARA_bf[:, 22528:25344]
    sdb = ARA[:, 12672:13696]

    xst = A("xst", [128, 2 * D], F32)
    nwb = A("nwb", [128, D], F32)
    brows = A("brows", [128, D], F32)
    identf = A("identf", [128, 128], F32)
    identb = A("identb", [128, 128], BF16)
    onesP = A("onesP", [128, 512], F32)
    r_all = A("r_all", [128, NRT], F32)
    m1_all = A("m1_all", [128, NRT], F32)
    vm_all = A("vm_all", [128, 320], F32)
    stat = A("stat", [128, 20], F32)
    consts = A("consts", [128, 4], F32)
    clipt = A("clipt", [1, 2], F32)
    sig2 = A("sig2", [1, 2], F32)
    gsc = A("gsc", [1, 16], F32)
    ccst = A("ccst", [1, 64], F32)
    scbc = A("scbc", [128, 8], F32)
    swg = A("swg", [128, 4], F32)
    swu = A("swu", [128, 4], F32)
    sgu = A("sgu", [128, 4], F32)
    sdall = A("sdall", [128, 8], F32)
    bqkT = A("bqkT", [128, 24], F32)
    bob = ARS[:, 9760:10784]

    P5 = [nc.alloc_psum_tensor(f"ps{i}", [128, 512], F32) for i in range(4)]
    PAB = [nc.alloc_psum_tensor("psA", [128, 1024], F32),
           nc.alloc_psum_tensor("psB", [128, 1024], F32)]

    sLd = Sem(nc, "sLd")
    sSp = Sem(nc, "sSp")
    sTr = Sem(nc, "sTr")
    sEv = Sem(nc, "sEv")
    sMM = Sem(nc, "sMM")
    sPE = Sem(nc, "sPE")
    sXn = Sem(nc, "sXn")
    sSq = Sem(nc, "sSq")
    sR = Sem(nc, "sR")
    sP = Sem(nc, "sP")
    sMs = Sem(nc, "sMs")
    sGp = Sem(nc, "sGp")
    sOut = Sem(nc, "sOut")
    sLdV = Sem(nc, "sLdV")
    sLdV2 = Sem(nc, "sLdV2")
    sLdW = Sem(nc, "sLdW")
    ALLSEMS = [sLd, sSp, sTr, sEv, sMM, sPE, sXn, sSq, sR, sP, sMs, sOut,
               sLdV, sLdV2, sLdW]

    gates = {}

    def gwait(eng, key):
        if key in gates:
            h, v = gates[key]
            eng.wait_ge(h, v)

    def gset(key, sem, val=None):
        gates[key] = (sem.h, sem.n if val is None else val)

    ck = [0]

    def ckpt():
        ck[0] += 1
        if dbg_n is not None and ck[0] == dbg_n:
            raise StopBuild

    def barrier():
        for s_ in ALLSEMS:
            GP.wait_ge(s_.h, s_.n)
        sGp.m(GP.tensor_copy(stat[0:1, 14:15], stat[0:1, 15:16]))
        for eng in (SY, TEE, VE, SC):
            eng.wait_ge(sGp.h, sGp.n)

    with nc.Block() as block:
      try:
        sLd.d(SY.dma_start(out=nwrow[:, :], in_=nw_in))
        for j, w in enumerate("qkv"):
            sLd.d(SY.dma_start(
                out=bqkT[:, 8 * j:8 * j + 8],
                in_=b_in[w].rearrange("a (c p) -> p (a c)", p=128),
                allow_slow_non_contiguous=True))
        sLd.d(SY.dma_start(out=clipt[:, :], in_=clip_in))
        ld0 = sLd.n

        GP.iota(iden_i[:, :], [[1, 128]], base=0, channel_multiplier=-1,
                allow_small_or_imprecise_dtypes=True)
        GP.iota(mask_i[:, :], [[1, QB]], base=0, channel_multiplier=-1,
                allow_small_or_imprecise_dtypes=True)
        GP.iota(ohJ[:, :], [[-1, 2], [0, 64]], base=0, channel_multiplier=1,
                allow_small_or_imprecise_dtypes=True)
        GP.memset(onesP[:, :], 1.0)
        GP.memset(consts[:, 0:1], -30.0)
        GP.memset(consts[:, 1:2], 0.5)
        GP.memset(srowb[:, :], 0.0)
        vv = vhpb.rearrange("p (t c) -> p t c", c=260)
        GP.memset(vv[:, :, 64:65], 1.0)
        GP.memset(vv[:, :, 129:130], 1.0)
        GP.memset(vv[:, :, 194:195], 0.0)
        GP.memset(vv[:, :, 259:260], 0.0)
        sGp.m(GP.memset(ccst[:, :], 0.0))
        gp0 = sGp.n

        VE.wait_ge(sGp.h, gp0)
        VE.wait_ge(sLd.h, ld0)
        VE.tensor_scalar(out=identf[:, :], in0=iden_i[:, :], scalar1=0.0,
                         scalar2=None, op0=ALU.is_equal)
        VE.tensor_scalar(out=identb[:, :], in0=iden_i[:, :], scalar1=0.0,
                         scalar2=None, op0=ALU.is_equal)
        for j, dd in enumerate((0.0, 128.0, 256.0, 384.0)):
            VE.tensor_scalar(out=maskb[:, j * QB:(j + 1) * QB], in0=mask_i[:, :],
                             scalar1=dd, scalar2=-1e9, op0=ALU.is_lt, op1=ALU.mult)
        init_ve = sMs.m(VE.tensor_copy(stat[:, 15:16], consts[:, 0:1]))

        SC.wait_ge(sLd.h, ld0)
        a_e = sSq.m(SC.activation(out=gsc[0:1, 6:8], in_=clipt[:, :], func=AF.Exp,
                                  scale=-1.0))
        VE.wait_ge(sSq.h, a_e)
        v_e = sMs.m(VE.tensor_scalar(out=gsc[0:1, 9:11], in0=gsc[0:1, 6:8],
                                     scalar1=1.0, scalar2=None, op0=ALU.add))
        SC.wait_ge(sMs.h, v_e)
        a_e2 = sSq.m(SC.activation(out=gsc[0:1, 1:3], in_=gsc[0:1, 9:11],
                                   func=AF.Copy))
        VE.wait_ge(sSq.h, a_e2)
        sSq.m(VE.reciprocal(sig2[:, :], gsc[0:1, 1:3])) if False else None
        sMs.m(VE.reciprocal(sig2[0:1, :], gsc[0:1, 1:3]))

        TEE.wait_ge(sMs.h, init_ve)
        TEE.matmul(PAB[0][:, 0:512], onesP[0:1, 0:128], nwrow[0:1, 0:512],
                   start=True, stop=True)
        sMM.m(TEE.matmul(PAB[0][:, 512:1024], onesP[0:1, 0:128], nwrow[0:1, 512:1024],
                         start=True, stop=True))
        VE.wait_ge(sMM.h, sMM.n)
        sPE.m(VE.tensor_copy(nwb[:, :], PAB[0][:, :]))
        nwb_ready = sPE.n
        gset("PAB0", sPE)

        sall_ld_prev = [None]

        for s in range(NSEQ):
            VE.wait_ge(sMM.h, sMM.n)
            SC.wait_ge(sMs.h, sMs.n)

            bXn, bTr, bEv, bLd, bSq, bMs, bR = (
                sXn.n, sTr.n, sEv.n, sLd.n, sSq.n, sMs.n, sR.n)
            for tt in range(NTT):
                if tt >= 2:
                    SY.wait_ge(sXn.h, bXn + tt - 1)
                gwait(SY, f"xst:{tt % 2}")
                sLd.d(SY.dma_start(
                    out=xst[:, (tt % 2) * D:(tt % 2 + 1) * D],
                    in_=x_in[s, tt * 128:(tt + 1) * 128, :]))
                SC.wait_ge(sLd.h, bLd + 16 * (tt + 1))
                if tt >= 2:
                    SC.wait_ge(sTr.h, bTr + 8 * (tt - 1))
                SC.activation(
                    out=xnst[:, (tt % 2) * D:(tt % 2 + 1) * D],
                    in_=xst[:, (tt % 2) * D:(tt % 2 + 1) * D],
                    func=AF.Square, accum_out=stat[:, 0:1])
                sSq.m(SC.activation(out=stat[:, 16:17], in_=stat[:, 0:1],
                                    func=AF.Copy))
                VE.wait_ge(sSq.h, bSq + tt + 1)
                sMs.m(VE.tensor_scalar(out=stat[:, 1:2], in0=stat[:, 0:1],
                                       scalar1=INV1024, scalar2=1e-6,
                                       op0=ALU.mult, op1=ALU.add))
                SC.wait_ge(sMs.h, bMs + tt + 1)
                SC.activation(out=stat[:, 2:3], in_=stat[:, 1:2], func=AF.Ln)
                sR.m(SC.activation(out=r_all[:, s * NTT + tt: s * NTT + tt + 1],
                                   in_=stat[:, 2:3], func=AF.Exp, scale=-0.5))
                idx = s * NTT + tt
                VE.wait_ge(sPE.h, nwb_ready)
                VE.wait_ge(sR.h, bR + tt + 1)
                if tt >= 2:
                    VE.wait_ge(sTr.h, bTr + 8 * (tt - 1))
                sXn.m(VE.scalar_tensor_tensor(
                    out=xnst[:, (tt % 2) * D:(tt % 2 + 1) * D],
                    in0=xst[:, (tt % 2) * D:(tt % 2 + 1) * D],
                    scalar=r_all[:, idx:idx + 1], in1=nwb[:, :],
                    op0=ALU.mult, op1=ALU.mult))
                gset(f"xst:{tt % 2}", sXn)
                for dc in range(8):
                    j = tt * 8 + dc
                    TEE.wait_ge(sXn.h, bXn + tt + 1)
                    gwait(TEE, f"P5:{j % 2}")
                    mtr = sTr.m(TEE.transpose(
                        P5[j % 2][:, 0:128],
                        xnst[:, (tt % 2) * D + dc * 128:(tt % 2) * D + (dc + 1) * 128],
                        identf[:, :]))
                    VE.wait_ge(sTr.h, mtr)
                    sEv.m(VE.tensor_copy(a1scr[:, dc * 128:(dc + 1) * 128],
                                         P5[j % 2][:, 0:128]))
                    gset(f"P5:{j % 2}", sEv)
                hi3 = xnT_bf.rearrange("p (c z t) -> p c z t", c=8, z=2)[
                    :, :, 0, tt * 128:(tt + 1) * 128]
                lo3 = xnT_bf.rearrange("p (c z t) -> p c z t", c=8, z=2)[
                    :, :, 1, tt * 128:(tt + 1) * 128]
                VE.tensor_copy(hi3, a1scr.rearrange("p (c t) -> p c t", c=8))
                sEv.m(VE.tensor_tensor(
                    out=lo3, in0=a1scr.rearrange("p (c t) -> p c t", c=8),
                    in1=hi3, op=ALU.subtract))
            xnT_ready = sEv.n
            ckpt()
            GP.wait_ge(sTr.h, sTr.n)
            vv2 = vhpb.rearrange("p (t c) -> p t c", c=260)
            GP.memset(vv2[:, :, 64:65], 1.0)
            GP.memset(vv2[:, :, 129:130], 1.0)
            GP.memset(vv2[:, :, 194:195], 0.0)
            sGp.m(GP.memset(vv2[:, :, 259:260], 0.0))
            ones_ready = sGp.n

            for hp in range(8):
                wsplit_done = []
                for wname, dst in (("q", wsq), ("k", wsk), ("v", wsv)):
                    gwait(SY, "wstage")
                    gwait(SY, f"wsl:{wname}")
                    srcw = wT_in[wname][:, hp * 128:(hp + 1) * 128]
                    wld = sLd.d(SY.dma_start(
                        out=wstage.rearrange("p (c o) -> p c o", c=8),
                        in_=srcw.rearrange("(c p) o -> p c o", p=128)))
                    dbf = dst.bitcast(BF16)
                    VE.wait_ge(sLd.h, wld)
                    VE.tensor_copy(dbf[:, 0:1024], wstage[:, :])
                    wm = sMs.m(VE.tensor_tensor(out=dbf[:, 1024:2048],
                                                in0=wstage[:, :],
                                                in1=dbf[:, 0:1024],
                                                op=ALU.subtract))
                    gset("wstage", sMs)
                    wsplit_done.append(wm)
                w_ld = sLd.n

                TEE.wait_ge(sLd.h, w_ld)
                TEE.wait_ge(sMs.h, wsplit_done[2])
                TEE.wait_ge(sEv.h, xnT_ready)

                def xnslice(part, dc, c0, w):
                    off = dc * 4096 + (0 if part == 0 else 2048)
                    return xnT_bf[:, off + c0: off + c0 + w]

                for wi, wap in ((0, wsq), (1, wsk)):
                    wbf = wap.bitcast(BF16)
                    w3h = wbf[:, 0:1024].rearrange("p (c o) -> p c o", c=8)
                    w3l = wbf[:, 1024:2048].rearrange("p (c o) -> p c o", c=8)
                    bcol = bqkT[:, 8 * wi + hp: 8 * wi + hp + 1]
                    for tb in range(4):
                        gwait(TEE, f"P5:{tb % 2}")
                        for wpart, xpart in ((0, 0), (0, 1), (1, 0)):
                            w3 = w3h if wpart == 0 else w3l
                            for dc in range(8):
                                mi = TEE.matmul(
                                    P5[tb % 2][:, :], w3[:, dc, :],
                                    xnslice(xpart, dc, tb * QB, QB),
                                    start=(wpart == 0 and xpart == 0 and dc == 0),
                                    stop=(wpart == 1 and dc == 7))
                        qkm = sMM.m(mi)
                        VE.wait_ge(sMM.h, qkm)
                        VE.tensor_scalar(out=dh_s, in0=P5[tb % 2][:, :],
                                         scalar1=bcol, scalar2=None, op0=ALU.add)
                        pm_ = sPE.m(VE.scalar_tensor_tensor(
                            out=dl_s, in0=P5[tb % 2][:, :], scalar=bcol,
                            in1=dh_s, op0=ALU.add, op1=ALU.subtract))
                        gset(f"P5:{tb % 2}", sPE)
                        qsl2 = slice(tb * QB, (tb + 1) * QB)
                        ds2 = ks2 if wi == 1 else qs2
                        pieces = [(ds2[0], 0, dh_s, 0), (ds2[0], 64, dl_s, 0),
                                  (ds2[1], 0, dh_s, 64), (ds2[1], 64, dl_s, 64)]
                        for dst_, dp, src_, sp in pieces:
                            mcp = VE.tensor_copy(dst_[dp:dp + 64, qsl2],
                                                 src_[sp:sp + 64, :])
                        sPE.m(mcp)
                gset("wsl:q", sMM)
                gset("wsl:k", sMM)
                wvbf = wsv.bitcast(BF16)
                wv3h = wvbf[:, 0:1024].rearrange("p (c o) -> p c o", c=8)
                wv3l = wvbf[:, 1024:2048].rearrange("p (c o) -> p c o", c=8)
                for tt in range(NTT):
                    pb = 2 + tt % 2
                    sl = P5[pb][:, 0:128]
                    gwait(TEE, f"P5:{pb}")
                    for wpart, xpart in ((0, 0), (0, 1), (1, 0)):
                        wv3 = wv3h if wpart == 0 else wv3l
                        for dc in range(8):
                            mi = TEE.matmul(
                                sl, xnslice(xpart, dc, tt * 128, 128),
                                wv3[:, dc, :],
                                start=(wpart == 0 and xpart == 0 and dc == 0),
                                stop=(wpart == 1 and dc == 7))
                    vmk = sMM.m(mi)
                    vb_ = vhpb[:, tt * 260:(tt + 1) * 260]
                    SC.wait_ge(sMM.h, vmk)
                    SC.activation(out=vb_[:, 0:64], in_=P5[pb][:, 0:64],
                                  func=AF.Copy)
                    hm_ = sSq.m(SC.activation(out=vb_[:, 65:129],
                                              in_=P5[pb][:, 64:128],
                                              func=AF.Copy))
                    VE.wait_ge(sSq.h, hm_)
                    VE.tensor_tensor(out=vb_[:, 130:194], in0=P5[pb][:, 0:64],
                                     in1=vb_[:, 0:64], op=ALU.subtract)
                    sPE.m(VE.tensor_tensor(out=vb_[:, 195:259],
                                           in0=P5[pb][:, 64:128],
                                           in1=vb_[:, 65:129], op=ALU.subtract))
                    gset(f"P5:{pb}", sPE)
                gset("wsl:v", sMM)
                qkv_done = sPE.n
                if hp == 0:
                    ckpt()

                TEE.wait_ge(sPE.h, qkv_done)
                if hp == 0:
                    TEE.wait_ge(sGp.h, ones_ready)
                    gwait(SC, "PAB1")
                pairs = [(hl, q) for hl in range(2) for q in range(NQB)]

                def build_sw(i2):
                    hl, q = pairs[i2]
                    ab = (hl * NQB + q) % 2
                    sw = ARS_bf[:, 24640 + ab * 512: 25152 + ab * 512]
                    gwait(VE, f"sw7:{ab}")
                    VE.tensor_copy(sw[0:64, :],
                                   qs2[hl][64:128, q * QB:(q + 1) * QB])
                    return sPE.m(VE.tensor_copy(
                        sw[64:128, :], qs2[hl][0:64, q * QB:(q + 1) * QB])), sw

                m7n, swn = build_sw(0)
                for i2, (hloc, qb) in enumerate(pairs):
                    if True:
                        m7, sw7 = m7n, swn
                        nkb = 4 * qb + 4
                        SB3 = [0, 1, 3]
                        avb = (hloc * NQB + qb) % 2
                        AVP = PAB[0][:, avb * 512:(avb + 1) * 512]
                        qsl = slice(qb * QB, (qb + 1) * QB)

                        def emit_scores(kb):
                            sb = SB3[kb % 3]
                            gwait(TEE, f"P5:{sb}")
                            dofs = kb - 4 * qb
                            ksl = slice(kb * 128, (kb + 1) * 128)
                            TEE.matmul(P5[sb][:, :], ks2[hloc][:, ksl],
                                       qs2[hloc][:, qsl], start=True, stop=False)
                            mi = TEE.matmul(P5[sb][:, :], ks2[hloc][:, ksl],
                                            sw7[:, :], start=False,
                                            stop=(dofs < 0))
                            if dofs >= 0:
                                mi = TEE.matmul(P5[sb][:, :], identb[:, :],
                                                maskb[:, dofs * QB:(dofs + 1) * QB],
                                                start=False, stop=True)
                            return sMM.m(mi)

                        gwait(TEE, f"AVP:{avb}")
                        TEE.wait_ge(sPE.h, m7)
                        smarks = [emit_scores(0)]
                        if nkb > 1:
                            smarks.append(emit_scores(1))
                        for kb in range(nkb):
                            sb = SB3[kb % 3]
                            pp3 = kb % 3
                            ppf = kb % 2
                            SC.wait_ge(sMM.h, smarks[kb])
                            gwait(SC, f"ptl:{pp3}")
                            gwait(SC, f"pfp:{ppf}")
                            pm1 = sP.m(SC.activation(
                                out=ph3[pp3], in_=P5[sb][:, :], func=AF.Exp,
                                scale=0.125, bias=consts[:, 0:1]))
                            pm2 = sP.m(SC.activation(
                                out=PAB[1][:, ppf * 512:(ppf + 1) * 512],
                                in_=P5[sb][:, :], func=AF.Exp, scale=0.125,
                                bias=consts[:, 0:1]))
                            gset(f"P5:{sb}", sP)
                            if kb + 2 < nkb:
                                smarks.append(emit_scores(kb + 2))
                            VE.wait_ge(sP.h, pm2)
                            plm = sXn.m(VE.tensor_tensor(
                                out=pl3[pp3],
                                in0=PAB[1][:, ppf * 512:(ppf + 1) * 512],
                                in1=ph3[pp3], op=ALU.subtract))
                            gset(f"pfp:{ppf}", sXn)
                            kbase = kb * 260 + hloc * 65
                            vh_ = vhpb[:, kbase: kbase + 65]
                            vl_ = vhpb[:, kbase + 130: kbase + 195]
                            TEE.wait_ge(sP.h, pm1)
                            TEE.matmul(AVP[0:65, :], vh_, ph3[pp3],
                                       start=(kb == 0), stop=False)
                            TEE.matmul(AVP[0:65, :], vl_, ph3[pp3],
                                       start=False, stop=False)
                            TEE.wait_ge(sXn.h, plm)
                            sMM.m(TEE.matmul(AVP[0:65, :], vh_, pl3[pp3],
                                             start=False, stop=(kb == nkb - 1)))
                            gset(f"ptl:{pp3}", sMM)
                        av_done = sMM.n
                        gset(f"sw7:{avb}", sMM)
                        if i2 + 1 < len(pairs):
                            m7n, swn = build_sw(i2 + 1)
                        VE.wait_ge(sMM.h, av_done)
                        VE.tensor_copy(
                            attT[hloc * 64:hloc * 64 + 64,
                                 hp * T + qb * QB: hp * T + (qb + 1) * QB],
                            AVP[0:64, :])
                        gwait(VE, "srowb")
                        srm = sPE.m(VE.tensor_copy(srowb[64:65, 0:512],
                                                   AVP[64:65, :]))
                        gset(f"AVP:{avb}", sPE)
                        GP.wait_ge(sPE.h, srm)
                        if s > 0 and sall_ld_prev[0] is not None:
                            GP.wait_ge(sLd.h, sall_ld_prev[0])
                        sSp.d(GP.dma_start(
                            out=sdram[2 * hp + hloc: 2 * hp + hloc + 1,
                                      qb * QB:(qb + 1) * QB],
                            in_=srowb[64:65, 0:512]))
                        gset("srowb", sSp)
                if hp == 0:
                    ckpt()

            ckpt()
            SY.wait_ge(sMM.h, sMM.n)
            wo_ld = sLdW.d(SY.dma_start(
                out=wo_full.rearrange("p (c d) -> p c d", c=8),
                in_=wT_in["o"].rearrange("(c p) d -> p c d", p=128)))
            if s == 0:
                bo_ld = sLdW.d(SY.dma_start(out=brows[64:65, :], in_=b_in["o"]))
                TEE.wait_ge(sLdW.h, bo_ld)
            for nb2 in range(2):
                gwait(TEE, f"P5:{nb2}")
                bobm = sMM.m(TEE.matmul(
                    P5[nb2][:, :], onesP[64:65, 0:128],
                    brows[64:65, nb2 * 512:(nb2 + 1) * 512],
                    start=True, stop=True))
            VE.wait_ge(sMM.h, bobm)
            VE.tensor_copy(bob[:, 0:512], P5[0][:, :])
            sPE.m(VE.tensor_copy(bob[:, 512:1024], P5[1][:, :]))
            gset("P5:0", sPE)
            gset("P5:1", sPE)
            woh = ARA_bf[:, 16384:24576]
            wol = ARA_bf[:, 24576:32768]
            VE.wait_ge(sLdW.h, wo_ld)
            VE.tensor_copy(woh, wo_full[:, :])
            wom = sPE.m(VE.tensor_tensor(out=wol, in0=wo_full[:, :], in1=woh,
                                         op=ALU.subtract))
            SY.wait_ge(sSp.h, sSp.n)
            sall_ld = sLd.d(SY.dma_start(out=sall, in_=sdram[:, :]))
            sall_ld_prev[0] = sall_ld
            VE.wait_ge(sLd.h, sall_ld)
            VE.wait_ge(sMM.h, sMM.n)
            VE.reciprocal(srec, sall)
            srh = ARS_bf[0:16, 4096:6144]
            srl = ARS_bf[0:16, 6144:8192]
            VE.tensor_copy(srh, srec)
            sMs.m(VE.tensor_tensor(out=srl, in0=srec, in1=srh,
                                   op=ALU.subtract))
            gset("wstage", sMs)
            ohb = ARS_bf[0:16, 27968:28096]
            for hp in range(8):
                gwait(VE, "oh")
                ohm = sMs.m(VE.tensor_scalar(out=ohb[:, :], in0=ohJ[:, :],
                                             scalar1=float(2 * hp), scalar2=None,
                                             op0=ALU.is_equal))
                for qb in range(NQB):
                    TEE.wait_ge(sMs.h, ohm)
                    gwait(TEE, "P5:3")
                    TEE.matmul(P5[3][:, :], ohb[:, :],
                               srh[:, qb * QB:(qb + 1) * QB],
                               start=True, stop=False)
                    bm = sMM.m(TEE.matmul(P5[3][:, :], ohb[:, :],
                                          srl[:, qb * QB:(qb + 1) * QB],
                                          start=False, stop=True))
                    VE.wait_ge(sMM.h, bm)
                    tdiv = ARS[:, 10784:11296]
                    abase = 2 * hp * T + qb * 1024
                    VE.tensor_tensor(
                        out=tdiv,
                        in0=attT[:, hp * T + qb * QB: hp * T + (qb + 1) * QB],
                        in1=P5[3][:, :], op=ALU.mult)
                    VE.tensor_scalar(
                        out=tdiv, in0=tdiv,
                        scalar1=bqkT[:, 16 + hp:17 + hp], scalar2=None,
                        op0=ALU.add)
                    VE.tensor_copy(ARB_bf[:, abase:abase + 512], tdiv)
                    sPE.m(VE.tensor_tensor(
                        out=ARB_bf[:, abase + 512:abase + 1024], in0=tdiv,
                        in1=ARB_bf[:, abase:abase + 512], op=ALU.subtract))
                    gset("P5:3", sPE)
                gset("oh", sMM)
            attT_done = sPE.n
            ckpt()

            s1_prev = None
            for tt in range(NTT):
                pb = tt % 2
                gwait(SY, f"xst:{pb}")
                xm2 = sLd.d(SY.dma_start(
                    out=xst[:, pb * D:(pb + 1) * D],
                    in_=x_in[s, tt * 128:(tt + 1) * 128, :]))
                TEE.wait_ge(sPE.h, max(attT_done, wom))
                gwait(TEE, f"PAB{pb}")

                def atts(oc, part):
                    base = (2 * oc * T + (tt // 4) * 1024 + part * 512
                            + (tt % 4) * 128)
                    return ARB_bf[:, base: base + 128]

                for nb in range(2):
                    for wpart, apart in ((0, 0), (0, 1), (1, 0)):
                        wo_ = woh if wpart == 0 else wol
                        for oc in range(8):
                            mi = TEE.matmul(
                                PAB[pb][:, nb * 512:(nb + 1) * 512],
                                atts(oc, apart),
                                wo_[:, oc * 1024 + nb * 512: oc * 1024 + (nb + 1) * 512],
                                start=(wpart == 0 and apart == 0 and oc == 0),
                                stop=(wpart == 1 and oc == 7))
                opm = sMM.m(mi)

                idx = s * NTT + tt
                _md = os.environ.get("K_OPROJ_MODE", "full")
                VE.wait_ge(sMM.h, opm)
                VE.wait_ge(sLd.h, xm2)
                if s1_prev is not None:
                    VE.wait_ge(sSq.h, s1_prev)
                gwait(VE, f"xnst:{pb}")
                VE.scalar_tensor_tensor(
                    out=xnst[:, pb * D:(pb + 1) * D],
                    in0=xst[:, pb * D:(pb + 1) * D],
                    scalar=r_all[:, idx:idx + 1], in1=nwb[:, :],
                    op0=ALU.mult, op1=ALU.mult)
                if _md == "pab":
                    VE.tensor_scalar(out=yt[:, :], in0=PAB[pb][:, :], scalar1=1.0,
                                     scalar2=None, op0=ALU.mult)
                else:
                    VE.tensor_tensor(out=yt[:, :], in0=PAB[pb][:, :],
                                     in1=bob[:, :], op=ALU.add)
                    VE.tensor_tensor(out=yt[:, :], in0=yt[:, :],
                                     in1=xst[:, pb * D:(pb + 1) * D], op=ALU.add)
                pe_m = sPE.m(VE.tensor_tensor(out=yb[:, :], in0=yt[:, :],
                                              in1=xnst[:, pb * D:(pb + 1) * D],
                                              op=ALU.add) if _md != "pab" else
                             VE.tensor_copy(yb[:, :], yt[:, :]))
                gset(f"PAB{pb}", sPE)
                gset(f"xnst:{pb}", sPE)
                SC.wait_ge(sPE.h, pe_m)
                SC.activation(out=yt[:, :], in_=yb[:, :], func=AF.Copy,
                              accum_out=stat[:, 4:5])
                SC.activation(out=yt[:, :], in_=yb[:, :], func=AF.Square,
                              accum_out=stat[:, 5:6])
                s1 = sSq.m(SC.activation(out=stat[:, 16:17], in_=stat[:, 4:5],
                                         func=AF.Copy))
                s1_prev = s1
                VE.wait_ge(sSq.h, s1)
                VE.tensor_scalar(out=stat[:, 6:7], in0=stat[:, 4:5],
                                 scalar1=INV1024, scalar2=None, op0=ALU.mult)
                vm1 = sMs.m(VE.tensor_scalar(out=stat[:, 7:8], in0=stat[:, 5:6],
                                             scalar1=INV1024, scalar2=None,
                                             op0=ALU.mult))
                SC.wait_ge(sMs.h, vm1)
                am1 = sSq.m(SC.activation(out=stat[:, 8:9], in_=stat[:, 6:7],
                                          func=AF.Square))
                VE.wait_ge(sSq.h, am1)
                v1 = sMs.m(VE.scalar_tensor_tensor(
                    out=stat[:, 10:11], in0=stat[:, 7:8], scalar=1e-5,
                    in1=stat[:, 8:9], op0=ALU.add, op1=ALU.subtract))
                SC.wait_ge(sMs.h, v1)
                SC.activation(out=stat[:, 11:12], in_=stat[:, 10:11], func=AF.Ln)
                r1 = sR.m(SC.activation(out=stat[:, 12:13], in_=stat[:, 11:12],
                                        func=AF.Exp, scale=-0.5))
                VE.wait_ge(sR.h, r1)
                if dbg_n is not None:
                    for ci, sc_ in enumerate((6, 12, 4, 5, 7, 8, 9, 10, 11)):
                        VE.tensor_copy(vm_all[:, 32 * ci + idx:32 * ci + idx + 1],
                                       stat[:, sc_:sc_ + 1])
                if _md in ("pab", "noln"):
                    VE.tensor_copy(xst[:, pb * D:(pb + 1) * D], yb[:, :])
                else:
                    VE.tensor_scalar(out=xst[:, pb * D:(pb + 1) * D], in0=yb[:, :],
                                     scalar1=stat[:, 6:7], scalar2=stat[:, 12:13],
                                     op0=ALU.subtract, op1=ALU.mult)
                x1m = sMs.m(VE.tensor_reduce(
                    out=m1_all[:, idx:idx + 1], in_=xst[:, pb * D:(pb + 1) * D],
                    axis=AX.X, op=ALU.max, apply_absolute_value=True))
                GP.wait_ge(sMs.h, x1m)
                sSp.d(GP.dma_start(
                    out=x1dram[idx * 128:(idx + 1) * 128, :],
                    in_=xst[:, pb * D:(pb + 1) * D]))
                gset(f"xst:{pb}", sSp)

            ckpt()

        if stage == "ln1":
            GP.wait_ge(sSp.h, sSp.n)
            for i4 in range(4):
                sOut.d(GP.dma_start(
                    out=out[i4 // 2, (i4 % 2) * 1024:(i4 % 2 + 1) * 1024, :],
                    in_=x1dram[i4 * 1024:(i4 + 1) * 1024, :]))
            GP.wait_ge(sOut.h, sOut.n)
            return nc

        barrier()
        g1m = sMs.m(VE.tensor_reduce(out=stat[:, 3:4], in_=m1_all[:, 0:NRT],
                                     axis=AX.X, op=ALU.max))
        GP.wait_ge(sMs.h, g1m)
        sSp.d(GP.dma_start(out=m128d, in_=stat[:, 3:4]))
        SY.wait_ge(sSp.h, sSp.n)
        sLd.d(SY.dma_start(out=rowt[:, :], in_=m128d))
        VE.wait_ge(sLd.h, sLd.n)
        gm1 = sMs.m(VE.tensor_reduce(out=gsc[0:1, 0:1], in_=rowt[:, :],
                                     axis=AX.X, op=ALU.max))
        GP.wait_ge(sMs.h, gm1)
        GP.tensor_copy(ccst[0:1, 0:1], gsc[0:1, 0:1])
        sSp.d(GP.dma_start(out=cc_in[0].ap(), in_=ccst[:, :]))
        GP.wait_ge(sSp.h, sSp.n)
        GP.collective_compute(
            "AllReduce", ALU.max, ins=[cc_in[0].ap().opt()],
            outs=[cc_out[0].ap().opt()],
            replica_groups=[[0, 1, 2, 3, 4, 5, 6, 7]]).then_inc(sGp.h, 1)
        sGp.n += 1
        GP.wait_ge(sGp.h, sGp.n)
        sSp.d(GP.dma_start(out=rowt[0:1, 0:64], in_=cc_out[0].ap()))
        SC.wait_ge(sSp.h, sSp.n)
        SC.activation(out=gsc[0:1, 5:6], in_=rowt[0:1, 0:1], func=AF.Copy,
                      scale=sig2[0:1, 0:1])
        a1g = sSq.m(SC.activation(out=gsc[0:1, 3:4], in_=gsc[0:1, 5:6],
                                  func=AF.Copy, scale=INV127))
        VE.wait_ge(sSq.h, a1g)
        v1g = sMs.m(VE.reciprocal(gsc[0:1, 8:9], gsc[0:1, 3:4]))
        SC.wait_ge(sMs.h, v1g)
        scm = sSq.m(SC.activation(out=gsc[0:1, 4:5], in_=gsc[0:1, 8:9],
                                  func=AF.Copy))
        TEE.wait_ge(sSq.h, scm)
        bcm = sMM.m(TEE.matmul(P5[0][:, 0:2], onesP[0:1, 0:128], gsc[0:1, 3:5],
                               start=True, stop=True))
        VE.wait_ge(sMM.h, bcm)
        sPE.m(VE.tensor_copy(scbc[:, 0:2], P5[0][:, 0:2]))
        scx_ready = sPE.n
        ckpt()

        barrier()
        gates.pop("P5:0", None); gates.pop("P5:1", None)
        gates.pop("xst:0", None); gates.pop("xst:1", None)
        xq3 = xqT_bf.rearrange("p (c r) -> p c r", r=R)
        for t32 in range(NRT):
            pb = t32 % 2
            gwait(SY, f"xst:{pb}")
            xm = sLd.d(SY.dma_start(
                out=xst[:, pb * D:(pb + 1) * D],
                in_=x1dram[t32 * 128:(t32 + 1) * 128, :]))
            TEE.wait_ge(sLd.h, xm)
            hmarks = []
            for half in range(2):
                bk = (2 * t32 + half) % 4
                gwait(TEE, f"P5:{bk}")
                for k4 in range(4):
                    dc = half * 4 + k4
                    mtr = sTr.m(TEE.transpose(
                        P5[bk][:, k4 * 128:(k4 + 1) * 128],
                        xst[:, pb * D + dc * 128: pb * D + (dc + 1) * 128],
                        identf[:, :]))
                hmarks.append(mtr)
            gset(f"xst:{pb}", sTr)
            for half in range(2):
                bk = (2 * t32 + half) % 4
                VE.wait_ge(sTr.h, hmarks[half])
                VE.wait_ge(sPE.h, scx_ready)
                gwait(VE, "wq")
                VE.tensor_scalar(out=wq_u[:, 0:512], in0=P5[bk][:, :],
                                 scalar1=scbc[:, 1:2], scalar2=0.5,
                                 op0=ALU.mult, op1=ALU.add)
                VE.tensor_copy(wq_t1i[:, 0:512], wq_u[:, 0:512])
                VE.tensor_copy(wq_t2[:, 0:512], wq_t1i[:, 0:512])
                VE.tensor_tensor(out=wq_t1[:, 0:512], in0=wq_t2[:, 0:512],
                                 in1=wq_u[:, 0:512], op=ALU.is_gt)
                VE.tensor_tensor(out=wq_rnd[:, 0:512], in0=wq_t2[:, 0:512],
                                 in1=wq_t1[:, 0:512], op=ALU.subtract)
                qm2 = sMs.m(VE.tensor_scalar(
                    out=xq3[:, half * 4:(half + 1) * 4,
                            t32 * 128:(t32 + 1) * 128],
                    in0=wq_rnd[:, 0:512].rearrange("p (c r) -> p c r", r=128),
                    scalar1=-128.0, scalar2=127.0, op0=ALU.max, op1=ALU.min))
                gset(f"P5:{bk}", sMs, qm2)
                gset("wq", sMs, qm2)
        xqT_ready = sMs.n
        ckpt()

        def prepB(j):
            bb = j % 2
            gwait(SY, "wstg")
            sLd.d(SY.dma_start(out=wstg[:, 0:1024],
                               in_=wg_in[j * 128:(j + 1) * 128, :]))
            l2 = sLd.d(SY.dma_start(out=wstg[:, 1024:2048],
                                    in_=wu_in[j * 128:(j + 1) * 128, :]))
            VE.wait_ge(sLd.h, l2)
            VE.tensor_reduce(out=stat[:, 17:18], in_=wstg[:, 0:1024],
                             axis=AX.X, op=ALU.max, apply_absolute_value=True)
            vr = sMs.m(VE.tensor_reduce(
                out=stat[:, 18:19], in_=wstg[:, 1024:2048],
                axis=AX.X, op=ALU.max, apply_absolute_value=True))
            SC.wait_ge(sMs.h, vr)
            ar = sSq.m(SC.activation(out=swg[:, 2 * bb:2 * bb + 2],
                                     in_=stat[:, 17:19], func=AF.Copy,
                                     scale=INV127))
            VE.wait_ge(sSq.h, ar)
            vr2 = sMs.m(VE.reciprocal(stat[:, 17:19], swg[:, 2 * bb:2 * bb + 2]))
            SC.wait_ge(sMs.h, vr2)
            ar2 = sSq.m(SC.activation(out=swu[:, 2 * bb:2 * bb + 2],
                                      in_=stat[:, 17:19], func=AF.Copy))
            VE.wait_ge(sSq.h, ar2)
            gwait(VE, "wq")
            qms = []
            for wi in range(2):
                sl = slice(wi * 1024, (wi + 1) * 1024)
                VE.tensor_scalar(out=wq_u[:, 0:1024], in0=wstg[:, sl],
                                 scalar1=swu[:, 2 * bb + wi:2 * bb + wi + 1],
                                 scalar2=0.5, op0=ALU.mult, op1=ALU.add)
                VE.tensor_copy(wq_t1i[:, 0:1024], wq_u[:, 0:1024])
                VE.tensor_copy(wq_t2[:, 0:1024], wq_t1i[:, 0:1024])
                VE.tensor_tensor(out=wq_t1[:, 0:1024], in0=wq_t2[:, 0:1024],
                                 in1=wq_u[:, 0:1024], op=ALU.is_gt)
                qms.append(sMs.m(VE.tensor_tensor(
                    out=wq_rnd[:, wi * 1024:(wi + 1) * 1024],
                    in0=wq_t2[:, 0:1024], in1=wq_t1[:, 0:1024], op=ALU.subtract)))
            gset("wstg", sMs)
            return ar, qms

        def prepB_T(j, qms):
            bb = j % 2
            for wi, dstT in ((0, wgT), (1, wuT)):
                for dc in range(8):
                    TEE.wait_ge(sMs.h, qms[wi])
                    gwait(TEE, f"P5:{dc % 2}")
                    mtr = sTr.m(TEE.transpose(
                        P5[dc % 2][:, 0:128],
                        wq_rnd[:, wi * 1024 + dc * 128: wi * 1024 + (dc + 1) * 128],
                        identf[:, :]))
                    VE.wait_ge(sTr.h, mtr)
                    gwait(VE, f"wgT:{bb}")
                    sEv.m(VE.tensor_copy(
                        dstT[bb][:, dc * 128:(dc + 1) * 128], P5[dc % 2][:, 0:128]))
                    gset(f"P5:{dc % 2}", sEv)
            gset("wq", sEv)
            return sEv.n

        ar_p, qms_p = prepB(0)
        wready_p = prepB_T(0, qms_p)
        for ib in range(NIC):
            bb = ib % 2
            armark, wready = ar_p, wready_p
            VE.wait_ge(sPE.h, scx_ready)
            VE.wait_ge(sSq.h, armark)
            VE.tensor_tensor(out=sgu[:, 2 * bb:2 * bb + 1],
                             in0=swg[:, 2 * bb:2 * bb + 1], in1=scbc[:, 0:1],
                             op=ALU.mult)
            sgm = sMs.m(VE.tensor_tensor(out=sgu[:, 2 * bb + 1:2 * bb + 2],
                                         in0=swg[:, 2 * bb + 1:2 * bb + 2],
                                         in1=scbc[:, 0:1], op=ALU.mult))
            if ib + 1 < NIC:
                ar_p, qms_p = prepB(ib + 1)
            for tb in range(8):
                pp = tb % 2
                TEE.wait_ge(sEv.h, wready)
                TEE.wait_ge(sMs.h, xqT_ready)
                gwait(TEE, f"PAB{pp}")
                for half, wT_ in ((0, wgT), (1, wuT)):
                    for dc in range(8):
                        mi = TEE.matmul(
                            PAB[pp][:, half * 512:(half + 1) * 512],
                            wT_[bb][:, dc * 128:(dc + 1) * 128],
                            xqT_bf[:, dc * R + tb * 512: dc * R + (tb + 1) * 512],
                            start=(dc == 0), stop=(dc == 7))
                gm_ = sMM.m(mi)
                gset(f"wgT:{bb}", sMM)
                SC.wait_ge(sMM.h, gm_)
                SC.wait_ge(sMs.h, sgm)
                gwait(SC, f"atl:{pp}")
                am = sSq.m(SC.activation(out=atl[pp], in_=PAB[pp][:, 0:512],
                                         func=AF.Silu,
                                         scale=sgu[:, 2 * bb:2 * bb + 1]))
                VE.wait_ge(sSq.h, am)
                gwait(VE, f"vtl:{pp}")
                VE.scalar_tensor_tensor(out=vtl[pp], in0=PAB[pp][:, 512:1024],
                                        scalar=sgu[:, 2 * bb + 1:2 * bb + 2],
                                        in1=atl[pp], op0=ALU.mult, op1=ALU.mult)
                gset(f"atl:{pp}", sMs, sMs.n + 1)
                vm = sMs.m(VE.tensor_reduce(
                    out=vm_all[:, ib * 8 + tb: ib * 8 + tb + 1], in_=vtl[pp],
                    axis=AX.X, op=ALU.max, apply_absolute_value=True))
                gates[f"PAB{pp}"] = (sMs.h, vm)
                GP.wait_ge(sMs.h, vm)
                sSp.d(GP.dma_start(
                    out=vdram[ib * 128:(ib + 1) * 128, tb * 512:(tb + 1) * 512],
                    in_=vtl[pp]))
                gset(f"vtl:{pp}", sSp)
            if ib + 1 < NIC:
                wready_p = prepB_T(ib + 1, qms_p)

        ckpt()
        if stage == "ffnv":
            GP.wait_ge(sSp.h, sSp.n)
            for i4 in range(4):
                sOut.d(GP.dma_start(
                    out=out[i4 // 2, (i4 % 2) * 1024:(i4 % 2 + 1) * 1024, :],
                    in_=x1dram[i4 * 1024:(i4 + 1) * 1024, :]))
            for ic in range(NIC):
                sOut.d(GP.dma_start(out=dbgv[ic * 128:(ic + 1) * 128, :],
                                    in_=vdram[ic * 128:(ic + 1) * 128, :]))
            GP.wait_ge(sOut.h, sOut.n)
            return nc

        for ds in range(8):
            gwait(SY, "wstg")
            l1 = sLd.d(SY.dma_start(out=wstg[:, :],
                                    in_=wd_in[ds * 128:(ds + 1) * 128, :]))
            VE.wait_ge(sLd.h, l1)
            gwait(VE, "wq")
            vr = sMs.m(VE.tensor_reduce(out=stat[:, 17:18], in_=wstg[:, :],
                                        axis=AX.X, op=ALU.max,
                                        apply_absolute_value=True))
            SC.wait_ge(sMs.h, vr)
            ar = sSq.m(SC.activation(out=sdall[:, ds:ds + 1], in_=stat[:, 17:18],
                                     func=AF.Copy, scale=INV127))
            VE.wait_ge(sSq.h, ar)
            vr2 = sMs.m(VE.reciprocal(stat[:, 18:19], sdall[:, ds:ds + 1]))
            SC.wait_ge(sMs.h, vr2)
            ar2 = sSq.m(SC.activation(out=stat[:, 13:14], in_=stat[:, 18:19],
                                      func=AF.Copy))
            VE.wait_ge(sSq.h, ar2)
            VE.tensor_scalar(out=wq_u[:, :], in0=wstg[:, :],
                             scalar1=stat[:, 13:14], scalar2=0.5,
                             op0=ALU.mult, op1=ALU.add)
            gset("wstg", sMs, sMs.n + 1)
            VE.tensor_copy(wq_t1i[:, :], wq_u[:, :])
            VE.tensor_copy(wq_t2[:, :], wq_t1i[:, :])
            VE.tensor_tensor(out=wq_t1[:, :], in0=wq_t2[:, :], in1=wq_u[:, :],
                             op=ALU.is_gt)
            qm = sMs.m(VE.tensor_tensor(out=wq_rnd[:, :], in0=wq_t2[:, :],
                                        in1=wq_t1[:, :], op=ALU.subtract))
            for ic in range(NIC):
                TEE.wait_ge(sMs.h, qm)
                gwait(TEE, f"P5:{ic % 2}")
                mtr = sTr.m(TEE.transpose(P5[ic % 2][:, 0:128],
                                          wq_rnd[:, ic * 128:(ic + 1) * 128],
                                          identf[:, :]))
                VE.wait_ge(sTr.h, mtr)
                sEv.m(VE.tensor_copy(
                    wdT_bf[:, ic * 1024 + ds * 128: ic * 1024 + (ds + 1) * 128],
                    P5[ic % 2][:, 0:128]))
                gset(f"P5:{ic % 2}", sEv)
            gset("wq", sEv)
        wdT_ready = sEv.n

        g2m = sMs.m(VE.tensor_reduce(out=stat[:, 3:4], in_=vm_all[:, 0:NIC * 8],
                                     axis=AX.X, op=ALU.max))
        GP.wait_ge(sMs.h, g2m)
        sSp.d(GP.dma_start(out=m128d, in_=stat[:, 3:4]))
        SY.wait_ge(sSp.h, sSp.n)
        sLd.d(SY.dma_start(out=rowt[:, :], in_=m128d))
        VE.wait_ge(sLd.h, sLd.n)
        gm2 = sMs.m(VE.tensor_reduce(out=gsc[0:1, 8:9], in_=rowt[:, :],
                                     axis=AX.X, op=ALU.max))
        GP.wait_ge(sMs.h, gm2)
        GP.tensor_copy(ccst[0:1, 0:1], gsc[0:1, 8:9])
        sSp.d(GP.dma_start(out=cc_in[1].ap(), in_=ccst[:, :]))
        GP.wait_ge(sSp.h, sSp.n)
        GP.collective_compute(
            "AllReduce", ALU.max, ins=[cc_in[1].ap().opt()],
            outs=[cc_out[1].ap().opt()],
            replica_groups=[[0, 1, 2, 3, 4, 5, 6, 7]]).then_inc(sGp.h, 1)
        sGp.n += 1
        GP.wait_ge(sGp.h, sGp.n)
        sSp.d(GP.dma_start(out=rowt[0:1, 0:64], in_=cc_out[1].ap()))
        SC.wait_ge(sSp.h, sSp.n)
        SC.activation(out=gsc[0:1, 13:14], in_=rowt[0:1, 0:1], func=AF.Copy,
                      scale=sig2[0:1, 1:2])
        a2g = sSq.m(SC.activation(out=gsc[0:1, 11:12], in_=gsc[0:1, 13:14],
                                  func=AF.Copy, scale=INV127))
        VE.wait_ge(sSq.h, a2g)
        v2g = sMs.m(VE.reciprocal(gsc[0:1, 14:15], gsc[0:1, 11:12]))
        SC.wait_ge(sMs.h, v2g)
        scm2 = sSq.m(SC.activation(out=gsc[0:1, 12:13], in_=gsc[0:1, 14:15],
                                   func=AF.Copy))
        TEE.wait_ge(sSq.h, scm2)
        bcm2 = sMM.m(TEE.matmul(P5[0][:, 0:2], onesP[0:1, 0:128], gsc[0:1, 11:13],
                                start=True, stop=True))
        VE.wait_ge(sMM.h, bcm2)
        sPE.m(VE.tensor_copy(scbc[:, 2:4], P5[0][:, 0:2]))
        sch_ready = sPE.n
        ckpt()

        barrier()
        gates.pop("wstg", None); gates.pop("wq", None)
        gates.pop("P5:0", None); gates.pop("P5:1", None)
        gates.pop("xst:0", None); gates.pop("xst:1", None)
        GP.wait_ge(sEv.h, wdT_ready)
        sSp.d(GP.dma_start(out=rsdd.rearrange("a (c p) -> (a p) c", p=128),
                           in_=sdall[:, 0:8], allow_slow_non_contiguous=True))
        SY.wait_ge(sSp.h, sSp.n)
        rl = sLd.d(SY.dma_start(out=nwrow[:, :], in_=rsdd))
        TEE.wait_ge(sLd.h, rl)
        gwait(TEE, "P5:0")
        TEE.matmul(P5[0][:, :], onesP[0:1, 0:128], nwrow[0:1, 0:512],
                   start=True, stop=True)
        gwait(TEE, "P5:1")
        sdm = sMM.m(TEE.matmul(P5[1][:, :], onesP[0:1, 0:128], nwrow[0:1, 512:1024],
                               start=True, stop=True))
        VE.wait_ge(sMM.h, sdm)
        VE.tensor_copy(sdb[:, 0:512], P5[0][:, :])
        sdbm = sPE.m(VE.tensor_copy(sdb[:, 512:1024], P5[1][:, :]))
        gset("P5:0", sPE)
        gset("P5:1", sPE)
        ckpt()

        vstgB = [vstg, ARS[:, 8448:11264]]
        htileB = [htile, ARA_bf[:, 27392:30208]]
        sLdVB = [sLdV, sLdV2]

        def ldV(i):
            pv = i % 2
            gwait(SY, f"vstg:{pv}")
            if i < 2:
                gwait(SY, "wq")
                gwait(SY, "wstg")
            return sLdVB[pv].d(SY.dma_start(
                out=vstgB[pv].rearrange("p (c t) -> p c t", c=NIC),
                in_=vdram[:, i * 128:(i + 1) * 128]
                .rearrange("(c p) t -> p c t", p=128)))

        def ldX(i):
            pv = i % 2
            gwait(SY, f"xst:{pv}")
            return sLd.d(SY.dma_start(
                out=xst[:, pv * D:(pv + 1) * D],
                in_=x1dram[i * 128:(i + 1) * 128, :]))

        def quantC(i, vm_):
            pv = i % 2
            VE.wait_ge(sLdVB[pv].h, vm_)
            VE.wait_ge(sPE.h, sch_ready)
            gwait(VE, f"htile:{pv}")
            VE.tensor_scalar(out=hq_u[:, :], in0=vstgB[pv], scalar1=scbc[:, 3:4],
                             scalar2=0.5, op0=ALU.mult, op1=ALU.add)
            gset(f"vstg:{pv}", sMs, sMs.n + 1)
            VE.tensor_copy(hq_t1i[:, :], hq_u[:, :])
            VE.tensor_copy(hq_t2[:, :], hq_t1i[:, :])
            VE.tensor_tensor(out=hq_t1[:, :], in0=hq_t2[:, :], in1=hq_u[:, :],
                             op=ALU.is_gt)
            VE.tensor_tensor(out=hq_u[:, :], in0=hq_t2[:, :], in1=hq_t1[:, :],
                             op=ALU.subtract)
            return sMs.m(VE.tensor_scalar(out=htileB[pv], in0=hq_u[:, :],
                                          scalar1=-128.0, scalar2=127.0,
                                          op0=ALU.max, op1=ALU.min))

        s2_prev = None
        vlds = {0: ldV(0), 1: ldV(1)}
        xms = {0: ldX(0), 1: ldX(1)}
        hm_p = quantC(0, vlds[0])
        for t32 in range(NRT):
            pb = t32 % 2
            s_, tt = t32 // NTT, t32 % NTT
            hm, xm_ = hm_p, xms[t32]
            if t32 + 1 < NRT:
                hm_p = quantC(t32 + 1, vlds[t32 + 1])
            TEE.wait_ge(sMs.h, hm)
            TEE.wait_ge(sEv.h, wdT_ready)
            TEE.wait_ge(sPE.h, sdbm)
            gwait(TEE, f"PAB{pb}")
            for nb in range(2):
                for ic in range(NIC):
                    mi = TEE.matmul(
                        PAB[pb][:, nb * 512:(nb + 1) * 512],
                        htileB[pb][:, ic * 128:(ic + 1) * 128],
                        wdT_bf[:, ic * 1024 + nb * 512: ic * 1024 + (nb + 1) * 512],
                        start=(ic == 0), stop=(ic == NIC - 1))
            dm = sMM.m(mi)
            gset(f"htile:{pb}", sMM)
            _mc = os.environ.get("K_C_MODE", "full")
            VE.wait_ge(sMM.h, dm)
            VE.wait_ge(sLd.h, xm_)
            if s2_prev is not None:
                VE.wait_ge(sSq.h, s2_prev)
            VE.scalar_tensor_tensor(out=yb[:, :], in0=PAB[pb][:, :],
                                    scalar=scbc[:, 2:3], in1=sdb[:, :],
                                    op0=ALU.mult, op1=ALU.mult)
            gset(f"PAB{pb}", sMs, sMs.n + 1)
            pe2 = sMs.m(VE.tensor_tensor(out=yt[:, :], in0=yb[:, :],
                                          in1=xst[:, pb * D:(pb + 1) * D],
                                          op=ALU.add) if _mc == "full" else
                        VE.tensor_copy(yt[:, :], yb[:, :]))
            SC.wait_ge(sMs.h, pe2)
            SC.activation(out=yb[:, :], in_=yt[:, :], func=AF.Copy,
                          accum_out=stat[:, 4:5])
            SC.activation(out=yb[:, :], in_=yt[:, :], func=AF.Square,
                          accum_out=stat[:, 5:6])
            s2 = sSq.m(SC.activation(out=stat[:, 16:17], in_=stat[:, 4:5],
                                     func=AF.Copy))
            s2_prev = s2
            VE.wait_ge(sSq.h, s2)
            VE.tensor_scalar(out=stat[:, 6:7], in0=stat[:, 4:5],
                             scalar1=INV1024, scalar2=None, op0=ALU.mult)
            vm1c = sMs.m(VE.tensor_scalar(out=stat[:, 7:8], in0=stat[:, 5:6],
                                          scalar1=INV1024, scalar2=None,
                                          op0=ALU.mult))
            SC.wait_ge(sMs.h, vm1c)
            am1c = sSq.m(SC.activation(out=stat[:, 8:9], in_=stat[:, 6:7],
                                       func=AF.Square))
            VE.wait_ge(sSq.h, am1c)
            v2 = sMs.m(VE.scalar_tensor_tensor(
                out=stat[:, 10:11], in0=stat[:, 7:8], scalar=1e-5,
                in1=stat[:, 8:9], op0=ALU.add, op1=ALU.subtract))
            SC.wait_ge(sMs.h, v2)
            SC.activation(out=stat[:, 11:12], in_=stat[:, 10:11], func=AF.Ln)
            r2 = sR.m(SC.activation(out=stat[:, 12:13], in_=stat[:, 11:12],
                                    func=AF.Exp, scale=-0.5))
            VE.wait_ge(sR.h, r2)
            om = sMs.m(VE.tensor_scalar(out=xst[:, pb * D:(pb + 1) * D],
                                        in0=yt[:, :],
                                        scalar1=stat[:, 6:7], scalar2=stat[:, 12:13],
                                        op0=ALU.subtract, op1=ALU.mult)
                       if _mc == "full" else
                       VE.tensor_copy(xst[:, pb * D:(pb + 1) * D], yt[:, :]))
            GP.wait_ge(sMs.h, om)
            sOut.d(GP.dma_start(out=out[s_, tt * 128:(tt + 1) * 128, :],
                                in_=xst[:, pb * D:(pb + 1) * D]))
            gset(f"xst:{pb}", sOut)
            if t32 + 2 < NRT:
                vlds[t32 + 2] = ldV(t32 + 2)
                xms[t32 + 2] = ldX(t32 + 2)
            if t32 == int(os.environ.get("K_T32", "0")):
                ckpt()

        if stage == "full2":
            for ic in range(NIC):
                sOut.d(GP.dma_start(out=dbgv[ic * 128:(ic + 1) * 128, :],
                                    in_=vdram[ic * 128:(ic + 1) * 128, :]))
            for i4 in range(4):
                sOut.d(GP.dma_start(out=dbgX2[i4 * 1024:(i4 + 1) * 1024, :],
                                    in_=x1dram[i4 * 1024:(i4 + 1) * 1024, :]))
        GP.wait_ge(sOut.h, sOut.n)
      except StopBuild:
        for s_ in ALLSEMS:
            GP.wait_ge(s_.h, s_.n)
        sOut.d(GP.dma_start(out=dbgA, in_=ARA[:, :]))
        sOut.d(GP.dma_start(out=dbgB, in_=ARB[:, :]))
        sOut.d(GP.dma_start(out=dbgS, in_=ARS[:, :]))
        sOut.d(GP.dma_start(out=dbgX, in_=x1dram))
        sOut.d(GP.dma_start(out=dbgSd, in_=sdram))
        sOut.d(GP.dma_start(out=dbgM, in_=vm_all[:, 0:320]))
        sOut.d(GP.dma_start(out=dbgW[:, 0:8], in_=scbc[:, :]))
        sOut.d(GP.dma_start(out=dbgW[:, 8:16], in_=sdall[:, :]))
        sOut.d(GP.dma_start(out=dbgW[:, 16:36], in_=stat[:, :]))
        sOut.d(GP.dma_start(out=dbgW[0:1, 36:48], in_=gsc[0:1, 0:12]))
        GP.wait_ge(sOut.h, sOut.n)

    return nc


def kernel(**inputs):
    x = np.ascontiguousarray(np.asarray(inputs["L0_attn_x"], dtype=np.float32))
    nw = np.asarray(inputs["L0_attn_norm_w"], dtype=np.float32).reshape(1, D)
    Ws = {w: np.ascontiguousarray(
        np.asarray(inputs[f"L0_attn_W{w}"], dtype=np.float32).T) for w in "qkvo"}
    bs = {w: np.asarray(inputs[f"L0_attn_b{w}"], dtype=np.float32).reshape(1, D)
          for w in "qkvo"}
    clips = np.array([[np.asarray(inputs["L0_ffn_clip_x"], np.float32).ravel()[0],
                       np.asarray(inputs["L0_ffn_clip_down"], np.float32).ravel()[0]]],
                     dtype=np.float32)
    wg = np.ascontiguousarray(np.asarray(inputs["L0_ffn_w_gate"], np.float32))
    wu = np.ascontiguousarray(np.asarray(inputs["L0_ffn_w_up"], np.float32))
    wd = np.ascontiguousarray(np.asarray(inputs["L0_ffn_w_down"], np.float32))

    nc = build_program("full")
    in_maps = []
    for c in range(8):
        m = {"x": np.ascontiguousarray(x[2 * c:2 * c + 2]),
             "norm_w": nw, "clips": clips,
             "w_gate": wg, "w_up": wu, "w_down": wd}
        for w in "qkvo":
            m[f"w{w}T"] = Ws[w]
            m[f"b{w}"] = bs[w]
        in_maps.append(m)
    res = run_bass_kernel_spmd(nc, in_maps, core_ids=list(range(8)))
    return np.concatenate([np.asarray(r["out"]) for r in res.results], axis=0)
